# version 08
# baseline (speedup 1.0000x reference)
import numpy as np
import ml_dtypes
from contextlib import ExitStack

import concourse.bass as bass
import concourse.mybir as mybir
from concourse.bass_utils import run_bass_kernel_spmd

F32 = mybir.dt.float32
BF16 = mybir.dt.bfloat16
AF = mybir.ActivationFunctionType
ALU = mybir.AluOpType
AX = mybir.AxisListType
BFNP = ml_dtypes.bfloat16

NCORES = 8
D = 1024
TOK = 4096
TT = 256
NT = TOK // TT
NST = 2 * NT
NH = 16
DH = 64
EPS_ATT = 1e-3
CAPF = 4096.0
BATCH_TILES = 8


class Ctr:
    def __init__(self, es, nc, name):
        self.h = es.enter_context(nc.semaphore(name))
        self.n = 0

    def inc(self, ins, k=1):
        ins.then_inc(self.h, k)
        self.n += k
        return self.n


class Bank:

    def __init__(self):
        self.pend = []

    def use(self, eng):
        for h, v in self.pend:
            eng.wait_ge(h, v)
        self.pend = []

    def consumed(self, ctr):
        self.pend.append((ctr.h, ctr.n))


def build_program(nt=NT, lvl=99, dump=None):
    nc = bass.Bass("TRN2", target_bir_lowering=False)

    x_in = nc.dram_tensor("x", [TOK, D], F32, kind="ExternalInput").ap()
    wq_in = nc.dram_tensor("wq", [D, D], F32, kind="ExternalInput").ap()
    wk_in = nc.dram_tensor("wk", [D, D], F32, kind="ExternalInput").ap()
    wv_in = nc.dram_tensor("wv", [D, D], F32, kind="ExternalInput").ap()
    wo_in = nc.dram_tensor("wo", [D, D], F32, kind="ExternalInput").ap()
    we_in = nc.dram_tensor("we", [D, D], F32, kind="ExternalInput").ap()
    wg_in = nc.dram_tensor("wg", [D, 8], F32, kind="ExternalInput").ap()
    bo_in = nc.dram_tensor("bo", [D], F32, kind="ExternalInput").ap()
    be_in = nc.dram_tensor("be", [D], F32, kind="ExternalInput").ap()
    idt_in = nc.dram_tensor("idt", [128, 128], BF16, kind="ExternalInput").ap()
    tri0_in = nc.dram_tensor("triu04", [128, 128], BF16, kind="ExternalInput").ap()
    tri1_in = nc.dram_tensor("triu1f", [128, 128], BF16, kind="ExternalInput").ap()
    ocb_in = nc.dram_tensor("ocb", [128, 1], BF16, kind="ExternalInput").ap()
    orb_in = nc.dram_tensor("orb", [65, 128], BF16, kind="ExternalInput").ap()
    orf_in = nc.dram_tensor("orf", [1, 128], F32, kind="ExternalInput").ap()
    prev_in = nc.dram_tensor("prevc", [8, 1], F32, kind="ExternalInput").ap()
    cvv_in = nc.dram_tensor("cvv", [128, 4], F32, kind="ExternalInput").ap()
    y_out = nc.dram_tensor("y", [TOK, D], F32, kind="ExternalOutput").ap()

    z2d = nc.dram_tensor("z2d", [NT, 128, 2048], BF16)
    cc_in = nc.dram_tensor("cc_in", [1, 8], F32)
    cc_out = nc.dram_tensor("cc_out", [8, 8], F32, addr_space="Shared")

    es = ExitStack()

    def sb(name, shape, dtype):
        return es.enter_context(nc.sbuf_tensor(name, shape, dtype))

    def ps(name, shape, dtype):
        return es.enter_context(nc.psum_tensor(name, shape, dtype))

    XT = sb("XT", [128, 2048], F32)
    OUTS = sb("OUTS", [128, 1024], F32)
    NRM_TM = sb("NRM_TM", [128, 2048], BF16)
    NRM_FM = sb("NRM_FM", [128, 2048], BF16)
    QF = sb("QF", [128, 2048], BF16)
    KF = sb("KF", [128, 2048], BF16)
    KT = sb("KT", [128, 2048], BF16)
    VT = sb("VT", [128, 2080], BF16)
    SM = sb("SM", [128, 2048], BF16)
    AT = sb("AT", [128, 2048], BF16)
    AFULL = sb("AFULL", [128, 2048], BF16)
    Z2B = sb("Z2B", [128, 2048], BF16)
    YB = sb("YB", [128, NST * 1024], BF16)
    WQb = sb("WQb", [128, 8192], BF16)
    WKb = sb("WKb", [128, 8192], BF16)
    WVb = sb("WVb", [128, 8192], BF16)
    WOb = sb("WOb", [128, 8192], BF16)
    WEb = sb("WEb", [128, 8192], BF16)
    WGb = sb("WGb", [128, 64], BF16)
    BOBE = sb("BOBE", [65, 1024], BF16)
    IDT = sb("IDT", [128, 128], BF16)
    TRI0 = sb("TRI0", [128, 128], BF16)
    TRI1F = sb("TRI1F", [128, 128], BF16)
    OCB = sb("OCB", [128, 1], BF16)
    ORB = sb("ORB", [65, 128], BF16)
    ORF = sb("ORF", [1, 128], F32)
    PREVC = sb("PREVC", [8, 1], F32)
    CVV = sb("CVV", [128, 4], F32)
    SG = sb("SGt", [8, 8], F32)
    SOFF = sb("SOFF", [1, 8], F32)
    OFFB = sb("OFFB", [128, 8], F32)
    STE = sb("STE", [128, 8 * 65], F32)
    STO = sb("STO", [128, 8 * 65], F32)
    STEB = sb("STEB", [128, 8 * 65], BF16)
    STOB = sb("STOB", [128, 8 * 65], BF16)
    CNT = sb("CNT", [1, 8], F32)
    MHOT = sb("MHOT", [128, NST * 8], BF16)
    GATE = sb("GATE", [128, NST], F32)
    RANK = sb("RANK", [128, NST], F32)
    M2C = sb("M2C", [128, NST], F32)
    R2C = sb("R2C", [128, NST], F32)
    DINV = sb("DINV", [128, 16], F32)
    SCR8 = sb("SCR8", [128, 8], F32)
    SEXPC = sb("SEXPC", [128, 2], F32)
    NEGM = sb("NEGM", [128, 2], F32)
    SUMT = sb("SUMT", [128, 2], F32)
    SQT = sb("SQT", [128, 2], F32)
    MT = sb("MT", [128, 2], F32)
    T1C = sb("T1C", [128, 2], F32)
    T2C = sb("T2C", [128, 2], F32)
    RSTD = sb("RSTD", [128, 2], F32)
    KGC = sb("KGC", [128, 2], F32)
    OFFC = sb("OFFC", [128, 1], F32)

    dumpd = dict(XT=XT, NRM_TM=NRM_TM, NRM_FM=NRM_FM, QF=QF, KF=KF, KT=KT,
                 VT=VT, SM=SM, AT=AT, AFULL=AFULL, YB=YB, MHOT=MHOT,
                 GATE=GATE, RANK=RANK, M2C=M2C, R2C=R2C, STE=STE, STO=STO,
                 CNT=CNT, OFFB=OFFB, DINV=DINV, SGt=SG, KGC=KGC, WQb=WQb,
                 SUMT=SUMT, SQT=SQT, MT=MT, RSTD=RSTD, T2C=T2C, Z2B=Z2B, OUTS=OUTS)

    PB0 = ps("PB0", [128, 1024], BF16)
    PB1 = ps("PB1", [128, 1024], BF16)
    PB2 = ps("PB2", [128, 512], F32)
    PB3 = ps("PB3", [128, 512], F32)
    PB4 = ps("PB4", [128, 512], F32)
    PB5 = ps("PB5", [128, 512], F32)
    PB6 = ps("PB6", [128, 512], F32)
    PB7 = ps("PB7", [128, 512], F32)

    SD = Ctr(es, nc, "SD")
    SDC = Ctr(es, nc, "SDCc")
    SZ = Ctr(es, nc, "SZc")
    SO = Ctr(es, nc, "SOc")
    SC = Ctr(es, nc, "SCc")
    SGM = Ctr(es, nc, "SGMc")
    ST = Ctr(es, nc, "STc")
    SV = Ctr(es, nc, "SVc")
    SS = Ctr(es, nc, "SSc")

    bk = {name: Bank() for name in ["pb0", "pb1", "pb2", "pb3", "pb4", "pb5", "pb6", "pb7"]}
    zbanks = [PB4, PB5, PB6, PB7]
    zbknames = ["pb4", "pb5", "pb6", "pb7"]

    blk_main = nc.Block()
    blk_main.__enter__()
    blk = blk_main
    if True:
        @blk.sync
        def _(sy):
            for dst, src in [(IDT, idt_in), (TRI0, tri0_in), (TRI1F, tri1_in),
                             (OCB, ocb_in), (ORB, orb_in),
                             (ORF, orf_in), (PREVC, prev_in), (CVV, cvv_in)]:
                SDC.inc(sy.dma_start(out=dst[:, :], in_=src[:, :]), 16)

        @blk.gpsimd
        def _(g):
            g.memset(STE[:, :], 0.0)
            g.memset(VT[:, :].rearrange("p (h u) -> p h u", u=65)[:, :, 64:65],
                     1.0)
            g.memset(STO[:, :], 0.0)
            g.memset(STEB[:, :], 0.0)
            g.memset(STOB[:, :], 0.0)
            ins = g.memset(CNT[:, :], 0.0)
            SGM.inc(ins)

    def emit_weight_loads():
        def g_w(g):
            for dst, src in [(WQb, wq_in), (WKb, wk_in), (WVb, wv_in),
                             (WOb, wo_in), (WEb, we_in)]:
                SD.inc(g.dma_start(
                    out=dst[:, :].rearrange("p (c o) -> p c o", c=8),
                    in_=src.rearrange("(c p) o -> p c o", p=128)), 16)
            SD.inc(g.dma_start(
                out=WGb[:, :].rearrange("p (c e) -> p c e", c=8),
                in_=wg_in.rearrange("(c p) e -> p c e", p=128)), 16)
            SD.inc(g.dma_start(out=BOBE[0:1, 0:1024],
                               in_=bo_in.rearrange("(u o) -> u o", u=1)), 16)
            SD.inc(g.dma_start(out=BOBE[64:65, 0:1024],
                               in_=be_in.rearrange("(u o) -> u o", u=1)), 16)
        blk.gpsimd(g_w)

    SETUP_SD = SD.n + 8 * 16
    SETUP_SDC = SDC.n

    v_y = 0
    v_st = 0
    v_ln2_prev = 0
    t_tr3_prev = 0
    proj_last_fill_ms = 0
    nrm_last_ms = 0
    v_cnt = 0
    first = {"t": True, "v": True, "s": True}

    v_y_hist = {}
    pre_vals = {}

    def emit_prelude(pt):
        XTW = XT
        vy_gate = v_y_hist.get(pt - 1, 0)

        def dseg(sy, pt=pt, vy_gate=vy_gate):
            if vy_gate:
                sy.wait_ge(SV.h, vy_gate)
            SDC.inc(sy.dma_start(
                out=XT[:, :].rearrange("p (s c) -> p s c", s=2),
                in_=x_in[TT * pt:TT * (pt + 1), :].rearrange(
                    "(s p) c -> p s c", p=128)), 16)
        blk.sync(dseg)
        dx = SDC.n

        def g_sq1(g, dx=dx, tprev=t_tr3_prev, XTW=XTW):
            g.wait_ge(SDC.h, dx)
            if tprev:
                g.wait_ge(ST.h, tprev)
            g.tensor_tensor(NRM_TM[:, 0:1024], XTW[:, 0:1024],
                            XTW[:, 0:1024], ALU.mult)
            ins = g.tensor_tensor(NRM_TM[:, 1024:2048], XTW[:, 1024:2048],
                                  XTW[:, 1024:2048], ALU.mult)
            SGM.inc(ins)
        blk.gpsimd(g_sq1)
        g_sq1m = SGM.n

        def v_st1(v, dx=dx, vprev=v_ln2_prev, g_sq1m=g_sq1m, XTW=XTW):
            v.wait_ge(SDC.h, dx)
            if vprev:
                v.wait_ge(SV.h, vprev)
            v.wait_ge(SGM.h, g_sq1m)
            v.reduce_sum(SUMT[:, 0:1], XTW[:, 0:1024], axis=AX.X)
            v.reduce_sum(SUMT[:, 1:2], XTW[:, 1024:2048], axis=AX.X)
            v.reduce_sum(SQT[:, 0:1], NRM_TM[:, 0:1024], axis=AX.X)
            ins = v.reduce_sum(SQT[:, 1:2], NRM_TM[:, 1024:2048], axis=AX.X)
            v.wait_ge(SV.h, SV.inc(ins))
            ins = v.tensor_scalar(MT[:, 0:2], SUMT[:, 0:2],
                                  1.0 / 1024.0, None, ALU.mult)
            v.wait_ge(SV.h, SV.inc(ins))
            ins = v.tensor_tensor(T1C[:, 0:2], SUMT[:, 0:2], MT[:, 0:2],
                                  ALU.mult)
            v.wait_ge(SV.h, SV.inc(ins))
            ins = v.tensor_tensor(T2C[:, 0:2], SQT[:, 0:2], T1C[:, 0:2],
                                  ALU.subtract)
            SV.inc(ins)
        blk.vector(v_st1)
        v_ln1s = SV.n

        def s_rstd(se, v_ln1s=v_ln1s, fs=first["s"]):
            if fs:
                se.wait_ge(SDC.h, SETUP_SDC)
            se.wait_ge(SV.h, v_ln1s)
            ins = se.activation(RSTD[:, 0:2], T2C[:, 0:2], AF.Ln,
                                bias=CVV[:, 0:1], scale=CVV[:, 1:2])
            se.wait_ge(SS.h, SS.inc(ins))
            ins = se.activation(RSTD[:, 0:2], RSTD[:, 0:2],
                                AF.Exp, scale=CVV[:, 2:3])
            se.wait_ge(SS.h, SS.inc(ins))
            ins = se.copy(OUTS[:, 17:18], OUTS[:, 15:16])
            SS.inc(ins)
        blk.scalar(s_rstd)
        first["s"] = False
        s_rstd1 = SS.n

        def v_ap1(v, s_rstd1=s_rstd1, XTW=XTW):
            v.wait_ge(SS.h, s_rstd1)
            for s in (0, 1):
                ins = v.tensor_scalar(NRM_TM[:, 1024 * s:1024 * (s + 1)],
                                      XTW[:, 1024 * s:1024 * (s + 1)],
                                      MT[:, s:s + 1], RSTD[:, s:s + 1],
                                      ALU.subtract, ALU.mult)
            SV.inc(ins)
        blk.vector(v_ap1)
        pre_vals[pt] = SV.n

    if True:
        emit_prelude(0)
        emit_weight_loads()
        for t in range(nt):
            v_ln1 = pre_vals[t]
            if lvl < 2:
                if t + 1 < nt:
                    emit_prelude(t + 1)
                continue

            def t_tr1(te, v_ln1=v_ln1, ft=first["t"]):
                if ft:
                    te.wait_ge(SDC.h, SETUP_SDC)
                te.wait_ge(SV.h, v_ln1)
                for s in (0, 1):
                    pb = PB0 if s == 0 else PB1
                    bk["pb0" if s == 0 else "pb1"].use(te)
                    for ci in range(8):
                        ins = te.transpose(pb[:, 128 * ci:128 * (ci + 1)],
                                           NRM_TM[:, 1024 * s + 128 * ci:
                                                  1024 * s + 128 * (ci + 1)],
                                           IDT[:, :])
                    ST.inc(ins)
            blk.tensor(t_tr1)
            first["t"] = False
            t_tr1b = ST.n
            t_tr1a = ST.n - 1

            def v_fm0(v, t_tr1a=t_tr1a, pms=nrm_last_ms):
                v.wait_ge(ST.h, t_tr1a)
                if pms:
                    v.wait_ge(ST.h, pms)
                ins = v.tensor_copy(
                    NRM_FM[:, :].rearrange("p (c u) -> p c u", u=256)[:, :, 0:128],
                    PB0[:, :].rearrange("p (c u) -> p c u", u=128))
                SV.inc(ins)
            blk.vector(v_fm0)
            bk["pb0"].consumed(SV)
            v_fma = SV.n

            def s_fm1(v, t_tr1b=t_tr1b, pms=nrm_last_ms):
                v.wait_ge(ST.h, t_tr1b)
                if pms:
                    v.wait_ge(ST.h, pms)
                ins = v.tensor_copy(
                    NRM_FM[:, :].rearrange("p (c u) -> p c u", u=256)[:, :, 128:256],
                    PB1[:, :].rearrange("p (c u) -> p c u", u=128))
                SV.inc(ins)
            blk.vector(s_fm1)
            bk["pb1"].consumed(SV)
            s_fmb = SV.n
            if lvl < 3:
                continue

            fill_ms = []
            evac_ms = []
            for f in range(12):
                pidx = f // 4
                g = f % 4
                bank = PB2 if f % 2 == 0 else PB3
                bname = "pb2" if f % 2 == 0 else "pb3"
                wsrc = [WQb, WKb, WVb][pidx]

                def t_fill(te, f=f, g=g, bank=bank, bname=bname, wsrc=wsrc,
                           v_fma=v_fma, s_fmb=s_fmb, t=t):
                    if f == 0:
                        if t == 0:
                            te.wait_ge(SD.h, SETUP_SD)
                        te.wait_ge(SV.h, v_fma)
                        te.wait_ge(SV.h, s_fmb)
                    bk[bname].use(te)
                    for j in range(2):
                        oc = 2 * g + j
                        for ci in range(8):
                            ins = te.matmul(
                                bank[:, 256 * j:256 * (j + 1)],
                                wsrc[:, 1024 * ci + 128 * oc:
                                     1024 * ci + 128 * (oc + 1)],
                                NRM_FM[:, 256 * ci:256 * (ci + 1)],
                                start=(j == 0 and ci == 0),
                                stop=(j == 1 and ci == 7))
                    ST.inc(ins)
                blk.tensor(t_fill)
                fms = ST.n
                fill_ms.append(fms)

                dest = [QF, KF, SM][pidx]
                if pidx < 2:
                    def v_ev(v, fms=fms, bank=bank, dest=dest, g=g):
                        v.wait_ge(ST.h, fms)
                        ins = v.tensor_scalar(dest[:, 512 * g:512 * (g + 1)],
                                              bank[:, :], 0.0, EPS_ATT,
                                              ALU.max, ALU.add)
                        SV.inc(ins)
                    blk.vector(v_ev)
                    bk[bname].consumed(SV)
                    evac_ms.append(("v", SV.n))
                else:
                    def s_ev(v, fms=fms, bank=bank, dest=dest, g=g):
                        v.wait_ge(ST.h, fms)
                        ins = v.tensor_copy(dest[:, 512 * g:512 * (g + 1)],
                                            bank[:, :])
                        SV.inc(ins)
                    blk.vector(s_ev)
                    bk[bname].consumed(SV)
                    evac_ms.append(("v", SV.n))
            proj_last_fill_ms = fill_ms[-1]
            qf_sv = evac_ms[3][1]
            kf_sv = evac_ms[7][1]
            vf_ss = evac_ms[11][1]
            if lvl < 4:
                continue

            t2ms = []
            for which in range(4):
                src = KF if which < 2 else SM
                dst = KT if which < 2 else VT
                ss = which % 2
                pb = PB0 if ss == 0 else PB1
                bname = "pb0" if ss == 0 else "pb1"

                def t_t2(te, src=src, ss=ss, pb=pb, bname=bname, which=which,
                         kf_sv=kf_sv, vf_ss=vf_ss):
                    if which == 0:
                        te.wait_ge(SV.h, kf_sv)
                    if which == 2:
                        te.wait_ge(SV.h, vf_ss)
                    bk[bname].use(te)
                    for oc in range(8):
                        ins = te.transpose(pb[:, 128 * oc:128 * (oc + 1)],
                                           src[:, 256 * oc + 128 * ss:
                                               256 * oc + 128 * (ss + 1)],
                                           IDT[:, :])
                    ST.inc(ins)
                blk.tensor(t_t2)
                tms = ST.n
                t2ms.append(tms)

                if ss == 0:
                    def v_t2(v, tms=tms, pb=pb, dst=dst, which=which):
                        v.wait_ge(ST.h, tms)
                        if dst is VT:
                            out = VT[:, 0:1040].rearrange(
                                "p (h u) -> p h u", u=65)[:, :, 0:64]
                        else:
                            out = dst[:, 0:1024]
                        ins = v.tensor_copy(
                            out,
                            pb[:, :].rearrange("p (h u) -> p h u", u=64)
                            if dst is VT else pb[:, :])
                        SV.inc(ins)
                    blk.vector(v_t2)
                    bk[bname].consumed(SV)
                else:
                    def s_t2(v, tms=tms, pb=pb, dst=dst, which=which):
                        v.wait_ge(ST.h, tms)
                        if dst is VT:
                            out = VT[:, 1040:2080].rearrange(
                                "p (h u) -> p h u", u=65)[:, :, 0:64]
                        else:
                            out = dst[:, 1024:2048]
                        ins = v.tensor_copy(
                            out,
                            pb[:, :].rearrange("p (h u) -> p h u", u=64)
                            if dst is VT else pb[:, :])
                        SV.inc(ins)
                    blk.vector(s_t2)
                    bk[bname].consumed(SV)
                if which == 1:
                    kt_sv, kt_ss = SV.n, SV.n
                if which == 3:
                    vt_sv, vt_ss = SV.n, SV.n
            t2_last = t2ms[-1]
            if lvl < 5:
                continue

            if t == BATCH_TILES:
                def v_rst(v, tam=t_att_prev_batch):
                    v.wait_ge(ST.h, tam)
                    v.memset(STE[0:64, :], 0.0)
                    v.memset(STO[64:128, :], 0.0)
                    v.memset(STEB[0:64, :], 0.0)
                    ins = v.memset(STOB[64:128, :], 0.0)
                    SV.inc(ins)
                blk.vector(v_rst)
                v_st = SV.n

            t_att_ms = 0
            v_at = 0
            for ss in (0, 1):
                smv = []
                sc_groups = [(PB4, "pb4", [0, 2, 4, 6], False),
                             (PB5, "pb5", [8, 10, 12, 14], False),
                             (PB4, "pb4", [1, 3, 5, 7], True),
                             (PB5, "pb5", [9, 11, 13, 15], False)]
                for gi, (pb, bname, heads, sep) in enumerate(sc_groups):
                    def t_sc(te, gi=gi, pb=pb, bname=bname, heads=tuple(heads),
                             sep=sep, ss=ss, qf_sv=qf_sv, kf_sv=kf_sv):
                        if gi == 0 and ss == 0:
                            te.wait_ge(SV.h, qf_sv)
                            te.wait_ge(SV.h, kf_sv)
                        bk[bname].use(te)
                        if sep and t == 0 and ss == 0:
                            te.wait_ge(SGM.h, 1)
                        if sep:
                            te.matmul(pb[:, 0:1], IDT[:, :], OCB[:, :],
                                      start=False, stop=False,
                                      skip_group_check=True)
                        for i, h in enumerate(heads):
                            p0 = 64 * (h % 2)
                            c0 = 256 * (h // 2) + 128 * ss
                            ins = te.matmul(
                                pb[:, 128 * i:128 * (i + 1)],
                                KF[p0:p0 + 64, c0:c0 + 128],
                                QF[p0:p0 + 64, c0:c0 + 128],
                                start=(i == 0), stop=(i == 3))
                        ST.inc(ins)
                    blk.tensor(t_sc)
                    scms = ST.n
                    if lvl == 40:
                        bk[bname].pend = []
                        continue

                    def v_sm(v, scms=scms, pb=pb, gi=gi, ss=ss,
                             t2_last=t2_last):
                        v.wait_ge(ST.h, scms)
                        if gi == 0 and ss == 0:
                            v.wait_ge(ST.h, t2_last)
                        base = 1024 * (gi % 2)
                        half = 128 * (gi // 2)
                        ins = v.tensor_tensor(
                            SM[:, base:base + 1024].rearrange(
                                "p (g u) -> p g u", u=256)[:, :, half:half + 128],
                            pb[:, :].rearrange("p (g u) -> p g u", u=128),
                            bass.AP(TRI0, 0, [[128, 128], [0, 4], [1, 128]]),
                            ALU.mult)
                        SV.inc(ins)
                    blk.vector(v_sm)
                    bk[bname].consumed(SV)
                    smv.append(SV.n)

                if lvl in (40, 41):
                    continue
                def t_att(te, ss=ss, smv3=smv[3], v_st=v_st,
                          vt_sv=vt_sv, vt_ss=vt_ss):
                    te.wait_ge(SV.h, smv3)
                    if v_st > 0:
                        te.wait_ge(SV.h, v_st)
                    elif v_st < 0:
                        te.wait_ge(SGM.h, -v_st)
                    else:
                        te.wait_ge(SGM.h, 1)
                    if ss == 0:
                        te.wait_ge(SV.h, vt_sv)
                        te.wait_ge(SV.h, vt_ss)
                    bk["pb2"].use(te)
                    bk["pb6"].use(te)
                    bk["pb7"].use(te)
                    for h in range(NH):
                        if h < 7:
                            ob, oc0 = PB6, 65 * h
                        elif h < 14:
                            ob, oc0 = PB7, 65 * (h - 7)
                        else:
                            ob, oc0 = PB2, 65 * (h - 14)
                        g = h // 2
                        c0 = 256 * g + 128 * ss
                        qs = QF[:, c0:c0 + 128]
                        stb = STEB if h % 2 == 0 else STOB
                        sm = SM[:, 128 * h:128 * (h + 1)]
                        te.matmul(ob[:, oc0:oc0 + 65], sm,
                                  VT[:, 1040 * ss + 65 * h:1040 * ss + 65 * (h + 1)],
                                  start=(oc0 == 0), stop=False)
                        ins = te.matmul(ob[:, oc0:oc0 + 65], qs,
                                        stb[:, 65 * g:65 * g + 65],
                                        start=False,
                                        stop=(h in (6, 13, 15)))
                    ST.inc(ins)
                blk.tensor(t_att)
                t_att_ms = ST.n

                def v_dv(v, t_att_ms=t_att_ms, ss=ss):
                    v.wait_ge(ST.h, t_att_ms)
                    v.reciprocal(DINV[:, 0:7],
                                 PB6[:, 0:455].rearrange("p (h u) -> p h u",
                                                         u=65)[:, :, 64:65])
                    v.reciprocal(DINV[:, 7:14],
                                 PB7[:, 0:455].rearrange("p (h u) -> p h u",
                                                         u=65)[:, :, 64:65])
                    ins = v.reciprocal(DINV[:, 14:16],
                                       PB2[:, 0:130].rearrange(
                                           "p (h u) -> p h u",
                                           u=65)[:, 0:2, 64:65])
                    v.wait_ge(SV.h, SV.inc(ins))
                    for pb, h0, nh in ((PB6, 0, 7), (PB7, 7, 7), (PB2, 14, 2)):
                        ins = v.tensor_tensor(
                            AT[:, 1024 * ss + 64 * h0:
                               1024 * ss + 64 * (h0 + nh)].rearrange(
                                   "p (h e) -> p h e", e=64),
                            pb[:, 0:65 * nh].rearrange(
                                "p (h e) -> p h e", e=65)[:, :, 0:64],
                            bass.AP(DINV, h0, [[16, 128], [1, nh], [0, 64]]),
                            ALU.mult)
                    SV.inc(ins)
                blk.vector(v_dv)
                v_dinv = SV.n - 1
                v_at = SV.n
                bk["pb6"].consumed(SV)
                bk["pb7"].consumed(SV)
                bk["pb2"].consumed(SV)

                if lvl == 42:
                    continue
                def t_stu(te, ss=ss, v_dinv=v_dinv, v_at=v_at,
                          kt_sv=kt_sv, kt_ss=kt_ss):
                    te.wait_ge(SV.h, v_dinv)
                    te.wait_ge(SV.h, v_at)
                    if ss == 0:
                        te.wait_ge(SV.h, kt_sv)
                        te.wait_ge(SV.h, kt_ss)
                    bk["pb3"].use(te)
                    bk["pb6"].use(te)
                    for g in range(8):
                        ktp = KT[:, 1024 * ss + 128 * g:1024 * ss + 128 * (g + 1)]
                        ve = VT[:, 1040 * ss + 130 * g:1040 * ss + 130 * g + 64]
                        vo = VT[:, 1040 * ss + 130 * g + 65:
                                1040 * ss + 130 * g + 129]
                        te.matmul(PB3[:, 64 * g:64 * (g + 1)], ktp, ve,
                                  start=(g == 0), stop=False)
                        te.matmul(PB6[:, 64 * g:64 * (g + 1)], ktp, vo,
                                  start=(g == 0), stop=False)
                        ins = te.matmul(PB2[:, 256 + g:257 + g], ktp, OCB[:, :],
                                        start=False, stop=(g == 7))
                    ST.inc(ins)
                blk.tensor(t_stu)
                t_stu_ms = ST.n

                def v_sta(v, t_stu_ms=t_stu_ms, t=t, ss=ss):
                    if t == 0 and ss == 0:
                        v.wait_ge(SGM.h, 1)
                    v.wait_ge(ST.h, t_stu_ms)
                    ste = STE[:, :].rearrange("p (g u) -> p g u", u=65)
                    sto = STO[:, :].rearrange("p (g u) -> p g u", u=65)
                    v.tensor_tensor(ste[0:64, :, 0:64], ste[0:64, :, 0:64],
                                    PB3[0:64, :].rearrange("p (g u) -> p g u", u=64),
                                    ALU.add)
                    v.tensor_tensor(sto[64:128, :, 0:64], sto[64:128, :, 0:64],
                                    PB6[64:128, :].rearrange("p (g u) -> p g u", u=64),
                                    ALU.add)
                    v.tensor_tensor(ste[0:64, :, 64:65], ste[0:64, :, 64:65],
                                    PB2[0:64, 256:264].rearrange(
                                        "p (g u) -> p g u", u=1),
                                    ALU.add)
                    ins = v.tensor_tensor(sto[64:128, :, 64:65],
                                          sto[64:128, :, 64:65],
                                          PB2[64:128, 256:264].rearrange(
                                              "p (g u) -> p g u", u=1),
                                          ALU.add)
                    ins2 = ins
                    SV.inc(ins2)
                blk.vector(v_sta)
                v_sta_ms = SV.n

                def g_stc(g, v_sta_ms=v_sta_ms):
                    g.wait_ge(SV.h, v_sta_ms)
                    g.tensor_copy(STEB[0:64, :], STE[0:64, :])
                    ins = g.tensor_copy(STOB[64:128, :], STO[64:128, :])
                    g.wait_ge(SGM.h, SGM.inc(ins))
                    ins = g.tensor_copy(SCR8[0:1, 0:2], STE[0:1, 0:2])
                    SGM.inc(ins)
                blk.gpsimd(g_stc)
                v_st = -SGM.n
                bk["pb2"].consumed(SV)
                bk["pb3"].consumed(SV)
                bk["pb6"].consumed(SV)
            t_att_prev_batch = t_att_ms
            if lvl < 6:
                continue

            at3ms = []
            for ss in (0, 1):
                pb = PB0 if ss == 0 else PB1
                bname = "pb0" if ss == 0 else "pb1"

                def t_at3(te, ss=ss, pb=pb, bname=bname, v_at=v_at):
                    if ss == 0 and v_at:
                        te.wait_ge(SV.h, v_at)
                    bk[bname].use(te)
                    for ec in range(8):
                        ins = te.transpose(pb[:, 128 * ec:128 * (ec + 1)],
                                           AT[:, 1024 * ss + 128 * ec:
                                              1024 * ss + 128 * (ec + 1)],
                                           IDT[:, :])
                    ST.inc(ins)
                blk.tensor(t_at3)
                tms = ST.n
                at3ms.append(tms)

                if ss == 0:
                    def v_at3(v, tms=tms, pb=pb):
                        v.wait_ge(ST.h, tms)
                        ins = v.tensor_copy(
                            AFULL[:, :].rearrange("p (c u) -> p c u", u=256)[:, :, 0:128],
                            pb[:, :].rearrange("p (c u) -> p c u", u=128))
                        SV.inc(ins)
                    blk.vector(v_at3)
                    bk["pb0"].consumed(SV)
                    af_sv = SV.n
                else:
                    def s_at3(v, tms=tms, pb=pb):
                        v.wait_ge(ST.h, tms)
                        ins = v.tensor_copy(
                            AFULL[:, :].rearrange("p (c u) -> p c u", u=256)[:, :, 128:256],
                            pb[:, :].rearrange("p (c u) -> p c u", u=128))
                        SV.inc(ins)
                    blk.vector(s_at3)
                    bk["pb1"].consumed(SV)
                    af_ss = SV.n
            if lvl < 7:
                continue

            z_ms = []
            for k, (s, oh) in enumerate([(0, 0), (0, 1), (1, 0), (1, 1)]):
                bankz = zbanks[k]
                bnm = zbknames[k]

                def t_z(te, k=k, s=s, oh=oh, bankz=bankz, bnm=bnm,
                        af_sv=af_sv, af_ss=af_ss):
                    if k == 0:
                        te.wait_ge(SV.h, af_sv)
                        te.wait_ge(SV.h, af_ss)
                    bk[bnm].use(te)
                    for e in range(8):
                        te.matmul(bankz[:, :],
                                  AFULL[:, 256 * e + 128 * s:256 * e + 128 * (s + 1)],
                                  WOb[:, 1024 * e + 512 * oh:1024 * e + 512 * (oh + 1)],
                                  start=(e == 0), stop=False)
                    ins = te.matmul(bankz[:, :], ORB[0:1, :],
                                    BOBE[0:1, 512 * oh:512 * (oh + 1)],
                                    start=False, stop=True)
                    ST.inc(ins)
                blk.tensor(t_z)
                z_ms.append(ST.n)

            def v_yev(v, t=t, z_ms=tuple(z_ms)):
                for k, (s, oh) in enumerate([(0, 0), (0, 1), (1, 0), (1, 1)]):
                    v.wait_ge(ST.h, z_ms[k])
                    st_i = 2 * t + s
                    ins = v.scalar_tensor_tensor(
                        YB[:, 1024 * st_i + 512 * oh:1024 * st_i + 512 * (oh + 1)],
                        zbanks[k][:, :], 1.0,
                        XT[:, 1024 * s + 512 * oh:1024 * s + 512 * (oh + 1)],
                        ALU.mult, ALU.add)
                SV.inc(ins)
            blk.vector(v_yev)
            v_y = SV.n
            v_y_hist[t] = v_y
            for bnm in zbknames:
                bk[bnm].consumed(SV)
            if lvl < 8:
                continue

            def g_sq2(g, t=t, v_y=v_y):
                g.wait_ge(SV.h, v_y)
                yb0 = YB[:, 2048 * t:2048 * t + 1024]
                yb1 = YB[:, 2048 * t + 1024:2048 * t + 2048]
                g.tensor_tensor(NRM_TM[:, 0:1024], yb0, yb0, ALU.mult)
                ins = g.tensor_tensor(NRM_TM[:, 1024:2048], yb1, yb1, ALU.mult)
                SGM.inc(ins)
            blk.gpsimd(g_sq2)
            g_sq2m = SGM.n

            def v_st2(v, t=t, v_y=v_y, g_sq2m=g_sq2m):
                v.wait_ge(SV.h, v_y)
                v.wait_ge(SGM.h, g_sq2m)
                yb0 = YB[:, 2048 * t:2048 * t + 1024]
                yb1 = YB[:, 2048 * t + 1024:2048 * t + 2048]
                v.reduce_sum(SUMT[:, 0:1], yb0, axis=AX.X)
                v.reduce_sum(SUMT[:, 1:2], yb1, axis=AX.X)
                v.reduce_sum(SQT[:, 0:1], NRM_TM[:, 0:1024], axis=AX.X)
                ins = v.reduce_sum(SQT[:, 1:2], NRM_TM[:, 1024:2048], axis=AX.X)
                v.wait_ge(SV.h, SV.inc(ins))
                ins = v.tensor_scalar(M2C[:, 2 * t:2 * t + 2], SUMT[:, 0:2],
                                      1.0 / 1024.0, None, ALU.mult)
                v.wait_ge(SV.h, SV.inc(ins))
                ins = v.tensor_tensor(T1C[:, 0:2], SUMT[:, 0:2],
                                      M2C[:, 2 * t:2 * t + 2], ALU.mult)
                v.wait_ge(SV.h, SV.inc(ins))
                ins = v.tensor_tensor(T2C[:, 0:2], SQT[:, 0:2], T1C[:, 0:2],
                                      ALU.subtract)
                SV.inc(ins)
            blk.vector(v_st2)
            v_ln2s = SV.n

            def s_rstd2(se, t=t, v_ln2s=v_ln2s):
                se.wait_ge(SV.h, v_ln2s)
                ins = se.activation(R2C[:, 2 * t:2 * t + 2], T2C[:, 0:2], AF.Ln,
                                    bias=CVV[:, 0:1], scale=CVV[:, 1:2])
                se.wait_ge(SS.h, SS.inc(ins))
                ins = se.activation(R2C[:, 2 * t:2 * t + 2],
                                    R2C[:, 2 * t:2 * t + 2],
                                    AF.Exp, scale=CVV[:, 2:3])
                se.wait_ge(SS.h, SS.inc(ins))
                ins = se.copy(OUTS[:, 18:19], OUTS[:, 15:16])
                SS.inc(ins)
            blk.scalar(s_rstd2)
            s_rstd2m = SS.n

            def v_ap2(v, t=t, s_rstd2m=s_rstd2m, t_tr1b=t_tr1b):
                v.wait_ge(SS.h, s_rstd2m)
                v.wait_ge(ST.h, t_tr1b)
                for s in (0, 1):
                    st_i = 2 * t + s
                    ins = v.tensor_scalar(NRM_TM[:, 1024 * s:1024 * (s + 1)],
                                          YB[:, 1024 * st_i:1024 * (st_i + 1)],
                                          M2C[:, st_i:st_i + 1],
                                          R2C[:, st_i:st_i + 1],
                                          ALU.subtract, ALU.mult)
                SV.inc(ins)
            blk.vector(v_ap2)
            v_ln2 = SV.n
            v_ln2_prev = v_ln2
            if lvl < 9:
                continue

            for s in (0, 1):
                pb = PB0 if s == 0 else PB1
                bname = "pb0" if s == 0 else "pb1"

                def t_tr3(te, s=s, pb=pb, bname=bname, v_ln2=v_ln2):
                    if s == 0:
                        te.wait_ge(SV.h, v_ln2)
                    bk[bname].use(te)
                    for ci in range(8):
                        ins = te.transpose(pb[:, 128 * ci:128 * (ci + 1)],
                                           NRM_TM[:, 1024 * s + 128 * ci:
                                                  1024 * s + 128 * (ci + 1)],
                                           IDT[:, :])
                    ST.inc(ins)
                blk.tensor(t_tr3)
                tms = ST.n

                if s == 0:
                    def v_tr3(v, tms=tms, pb=pb, plf=proj_last_fill_ms):
                        v.wait_ge(ST.h, tms)
                        v.wait_ge(ST.h, plf)
                        ins = v.tensor_copy(
                            NRM_FM[:, :].rearrange("p (c u) -> p c u", u=256)[:, :, 0:128],
                            pb[:, :].rearrange("p (c u) -> p c u", u=128))
                        SV.inc(ins)
                    blk.vector(v_tr3)
                    bk["pb0"].consumed(SV)
                    fm3_sv = SV.n
                else:
                    def s_tr3(v, tms=tms, pb=pb, plf=proj_last_fill_ms):
                        v.wait_ge(ST.h, tms)
                        v.wait_ge(ST.h, plf)
                        ins = v.tensor_copy(
                            NRM_FM[:, :].rearrange("p (c u) -> p c u", u=256)[:, :, 128:256],
                            pb[:, :].rearrange("p (c u) -> p c u", u=128))
                        SV.inc(ins)
                    blk.vector(s_tr3)
                    bk["pb1"].consumed(SV)
                    fm3_ss = SV.n
            t_tr3_prev = ST.n
            if lvl < 10:
                continue

            rt_ms = {}
            for s in (0, 1):
                bankr = PB2 if s == 0 else PB3
                bnm = "pb2" if s == 0 else "pb3"

                def t_rt(te, s=s, bankr=bankr, bnm=bnm,
                         fm3_sv=fm3_sv, fm3_ss=fm3_ss):
                    if s == 0:
                        te.wait_ge(SV.h, fm3_sv)
                        te.wait_ge(SV.h, fm3_ss)
                    bk[bnm].use(te)
                    for ci in range(8):
                        ins = te.matmul(bankr[:, 0:8],
                                        NRM_FM[:, 256 * ci + 128 * s:
                                               256 * ci + 128 * (s + 1)],
                                        WGb[:, 8 * ci:8 * (ci + 1)],
                                        start=(ci == 0), stop=(ci == 7))
                    ST.inc(ins)
                blk.tensor(t_rt)
                rt_ms[s] = ST.n

            we_ms = []
            for k, (s2, oh) in enumerate([(0, 0), (0, 1), (1, 0), (1, 1)]):
                bankz = zbanks[k]
                bnm2 = zbknames[k]

                def t_we(te, k=k, s2=s2, oh=oh, bankz=bankz, bnm2=bnm2):
                    bk[bnm2].use(te)
                    for e in range(8):
                        te.matmul(bankz[:, :],
                                  NRM_FM[:, 256 * e + 128 * s2:
                                         256 * e + 128 * (s2 + 1)],
                                  WEb[:, 1024 * e + 512 * oh:
                                      1024 * e + 512 * (oh + 1)],
                                  start=(e == 0), stop=False)
                    ins = te.matmul(bankz[:, :], ORB[64:65, :],
                                    BOBE[64:65, 512 * oh:512 * (oh + 1)],
                                    start=False, stop=True)
                    ST.inc(ins)
                blk.tensor(t_we)
                we_ms.append(ST.n)
            nrm_last_ms = we_ms[-1]

            we_ms_t = tuple(we_ms)

            rte_v = {}
            rte_s = {}

            def v_mx(v, rtm=rt_ms[1]):
                v.wait_ge(ST.h, rtm)
                v.reduce_max(NEGM[:, 0:1], PB2[:, 0:8], axis=AX.X, negate=True)
                ins = v.reduce_max(NEGM[:, 1:2], PB3[:, 0:8], axis=AX.X,
                                   negate=True)
                v.wait_ge(SV.h, SV.inc(ins))
                ins = v.tensor_copy(NEGM[:, 0:2], NEGM[:, 0:2])
                SV.inc(ins)
            blk.vector(v_mx)
            v_negm = SV.n

            def s_ex(se, v_negm=v_negm):
                se.wait_ge(SV.h, v_negm)
                se.activation(OUTS[:, 0:8], PB2[:, 0:8], AF.Exp,
                              bias=NEGM[:, 0:1])
                ins = se.activation(OUTS[:, 8:16], PB3[:, 0:8], AF.Exp,
                                    bias=NEGM[:, 1:2])
                se.wait_ge(SS.h, SS.inc(ins))
                ins = se.copy(OUTS[:, 16:17], OUTS[:, 15:16])
                SS.inc(ins)
            blk.scalar(s_ex)
            rte_s[0] = rte_s[1] = SS.n

            def v_mh(v, t=t, sx=rte_s[1]):
                v.wait_ge(SS.h, sx)
                v.tensor_scalar(MHOT[:, 8 * (2 * t):8 * (2 * t + 1)],
                                PB2[:, 0:8], NEGM[:, 0:1], 0.0,
                                ALU.add, ALU.is_ge)
                v.tensor_scalar(MHOT[:, 8 * (2 * t + 1):8 * (2 * t + 2)],
                                PB3[:, 0:8], NEGM[:, 1:2], 0.0,
                                ALU.add, ALU.is_ge)
                v.reduce_sum(SEXPC[:, 0:1], OUTS[:, 0:8], axis=AX.X)
                ins = v.reduce_sum(SEXPC[:, 1:2], OUTS[:, 8:16], axis=AX.X)
                v.wait_ge(SV.h, SV.inc(ins))
                ins = v.reciprocal(GATE[:, 2 * t:2 * t + 2], SEXPC[:, 0:2])
                SV.inc(ins)
            blk.vector(v_mh)
            rte_v[0] = rte_v[1] = SV.n
            if lvl < 11:
                continue

            for s in (0, 1):
                bankr = PB2 if s == 0 else PB3
                bnm = "pb2" if s == 0 else "pb3"
                bk[bnm].consumed(SV)
                bk[bnm].pend.append((SS.h, rte_s[s]))

                def t_rk(te, s=s, t=t, bankr=bankr, bnm=bnm,
                         vr=rte_v[s], sxm=rte_s[s], v_cnt=v_cnt):
                    st_i = 2 * t + s
                    te.wait_ge(SV.h, vr)
                    te.wait_ge(SS.h, sxm)
                    if v_cnt:
                        te.wait_ge(SV.h, v_cnt)
                    else:
                        te.wait_ge(SGM.h, 1)
                    te.matmul(bankr[:, 8:16], TRI1F[:, :],
                              MHOT[:, 8 * st_i:8 * (st_i + 1)],
                              start=False, stop=False)
                    te.matmul(bankr[:, 8:16], ORF[0:1, :], CNT[0:1, :],
                              start=False, stop=False)
                    ins = te.matmul(bankr[0:1, 16:24], OCB[:, :],
                                    MHOT[:, 8 * st_i:8 * (st_i + 1)],
                                    start=False, stop=True)
                    ST.inc(ins)
                blk.tensor(t_rk)
                t_rk_ms = ST.n

                def v_rk(v, s=s, t=t, bankr=bankr, t_rk_ms=t_rk_ms):
                    st_i = 2 * t + s
                    if t == 0 and s == 0:
                        v.wait_ge(SGM.h, 1)
                    v.wait_ge(ST.h, t_rk_ms)
                    ins = v.tensor_tensor(SCR8[:, :], bankr[:, 8:16],
                                          MHOT[:, 8 * st_i:8 * (st_i + 1)],
                                          ALU.mult)
                    v.wait_ge(SV.h, SV.inc(ins))
                    v.reduce_sum(RANK[:, st_i:st_i + 1], SCR8[:, :], axis=AX.X)
                    ins = v.tensor_tensor(CNT[0:1, :], CNT[0:1, :],
                                          bankr[0:1, 16:24], ALU.add)
                    v.wait_ge(SV.h, SV.inc(ins))
                    ins = v.tensor_copy(SCR8[0:1, 0:8], CNT[0:1, :])
                    SV.inc(ins)
                blk.vector(v_rk)
                v_cnt = SV.n
                bk[bnm].consumed(SV)

            if t + 1 < nt:
                emit_prelude(t + 1)

            def v_z2(v, t=t, we_ms=we_ms_t):
                if t > 0:
                    v.wait_ge(SZ.h, 16 * t)
                for k in range(4):
                    v.wait_ge(ST.h, we_ms[k])
                    ins = v.tensor_copy(Z2B[:, 512 * k:512 * (k + 1)],
                                        zbanks[k][:, :])
                SV.inc(ins)
            blk.vector(v_z2)
            v_z2_ms = SV.n
            for bnm2 in zbknames:
                bk[bnm2].consumed(SV)

            def g_z2(g, t=t, v_z2_ms=v_z2_ms):
                g.wait_ge(SV.h, v_z2_ms)
                SZ.inc(g.dma_start(out=z2d.ap()[t], in_=Z2B[:, :]), 16)
            blk.gpsimd(g_z2)

        def g_cc(g, v_cnt=v_cnt):
            if lvl < 12:
                return
            g.wait_ge(SV.h, v_cnt)
            SD.inc(g.dma_start(out=cc_in.ap()[:, :], in_=CNT[:, :]), 16)
            g.wait_ge(SD.h, SD.n)
            SC.inc(g.collective_compute(
                "AllGather", ALU.bypass,
                replica_groups=[list(range(NCORES))],
                ins=[cc_in.ap()[:, :]],
                outs=[cc_out.ap()[:, :]]))
            g.wait_ge(SC.h, SC.n)
            SD.inc(g.dma_start(out=SG[:, :], in_=cc_out.ap()[:, :]), 16)
        blk.gpsimd(g_cc)
        sd_g = SD.n

    blk_main.__exit__(None, None, None)

    for b in bk.values():
        b.pend = []

    with nc.Block() as blk:
        if lvl < 12:
            pass
        def t_off(te, sd_g=sd_g):
            if lvl < 12:
                return
            te.wait_ge(SD.h, sd_g)
            ins = te.matmul(PB2[0:1, 0:8], PREVC[:, :], SG[:, :],
                            start=True, stop=True)
            ST.inc(ins)
        blk.tensor(t_off)
        t_off_ms = ST.n

        def v_off(v, t_off_ms=t_off_ms):
            if lvl < 12:
                return
            v.wait_ge(ST.h, t_off_ms)
            ins = v.tensor_copy(SOFF[:, :], PB2[0:1, 0:8])
            v.wait_ge(SV.h, SV.inc(ins))
            ins = v.tensor_copy(SCR8[0:1, 0:8], SOFF[:, :])
            SV.inc(ins)
        blk.vector(v_off)
        v_off_ms = SV.n

        def t_ob(te, v_off_ms=v_off_ms):
            if lvl < 12:
                return
            te.wait_ge(SV.h, v_off_ms)
            ins = te.matmul(PB3[:, 0:8], ORF[0:1, :], SOFF[0:1, :],
                            start=True, stop=True)
            ST.inc(ins)
        blk.tensor(t_ob)
        t_ob_ms = ST.n

        def v_ob(v, t_ob_ms=t_ob_ms):
            if lvl < 12:
                return
            v.wait_ge(ST.h, t_ob_ms)
            ins = v.tensor_copy(OFFB[:, :], PB3[:, 0:8])
            SV.inc(ins)
        blk.vector(v_ob)

    z2_stores = SZ.n
    v_fin_prev = 0
    with nc.Block() as blk:
        for t in (range(nt) if lvl >= 13 else []):
            def v_kg(v, t=t):
                for s in (0, 1):
                    st_i = 2 * t + s
                    ins = v.tensor_tensor(SCR8[:, :],
                                          MHOT[:, 8 * st_i:8 * (st_i + 1)],
                                          OFFB[:, :], ALU.mult)
                    v.wait_ge(SV.h, SV.inc(ins))
                    ins = v.reduce_sum(OFFC[:, :], SCR8[:, :], axis=AX.X)
                    v.wait_ge(SV.h, SV.inc(ins))
                    ins = v.tensor_tensor(OFFC[:, :], OFFC[:, :],
                                          RANK[:, st_i:st_i + 1], ALU.add)
                    v.wait_ge(SV.h, SV.inc(ins))
                    ins = v.tensor_scalar(KGC[:, s:s + 1], OFFC[:, :], CAPF,
                                          None, ALU.is_lt)
                    v.wait_ge(SV.h, SV.inc(ins))
                    ins = v.tensor_tensor(KGC[:, s:s + 1], KGC[:, s:s + 1],
                                          GATE[:, st_i:st_i + 1], ALU.mult)
                SV.inc(ins)
            blk.vector(v_kg)
            v_kg_ms = SV.n

            def d_z2l(sy, t=t, vprev=v_fin_prev):
                if vprev:
                    sy.wait_ge(SV.h, vprev)
                sy.wait_ge(SZ.h, z2_stores)
                SZ.inc(sy.dma_start(out=NRM_TM[:, :], in_=z2d.ap()[t]), 16)
            blk.sync(d_z2l)
            z2l_ms = SZ.n

            for s in (0, 1):
                def v_fin(v, t=t, s=s, z2l_ms=z2l_ms):
                    half = 2 * t + s
                    if half > 0:
                        v.wait_ge(SO.h, 16 * half)
                    if s == 0:
                        v.wait_ge(SZ.h, z2l_ms)
                    st_i = 2 * t + s
                    for oh in (0, 1):
                        ins = v.scalar_tensor_tensor(
                            OUTS[:, 512 * oh:512 * (oh + 1)],
                            NRM_TM[:, 1024 * s + 512 * oh:
                                   1024 * s + 512 * (oh + 1)],
                            KGC[:, s:s + 1],
                            YB[:, 1024 * st_i + 512 * oh:
                               1024 * st_i + 512 * (oh + 1)],
                            ALU.mult, ALU.add)
                    SV.inc(ins)
                blk.vector(v_fin)
                v_fin_ms = SV.n

                def d_out(sy, t=t, s=s, v_fin_ms=v_fin_ms):
                    sy.wait_ge(SV.h, v_fin_ms)
                    SO.inc(sy.dma_start(
                        out=y_out[TT * t + 128 * s:TT * t + 128 * (s + 1), :],
                        in_=OUTS[:, :]), 16)
                blk.sync(d_out)
            v_fin_prev = v_fin_ms

    with nc.Block() as blk:
        @blk.sync
        def _(sy):
            if SO.n:
                sy.wait_ge(SO.h, SO.n)
            sy.wait_ge(SD.h, SD.n)
            sy.wait_ge(ST.h, ST.n)
            sy.wait_ge(SV.h, SV.n)
            sy.wait_ge(SS.h, SS.n)

    if dump:
        with nc.Block() as blk:
            def g_dump(g):
                r0 = 0
                for name in dump:
                    if ":" in name:
                        name, cstr = name.split(":")
                        hnd = dumpd[name]
                        cols = int(cstr)
                    else:
                        hnd = dumpd[name]
                        cols = hnd.shape[1] if hasattr(hnd, 'shape') else None
                    nb = max(1, (cols + 1023) // 1024)
                    for b in range(nb):
                        c0 = 1024 * b
                        c1 = min(cols, c0 + 1024)
                        p = hnd.shape[0]
                        SO.inc(g.dma_start(
                            out=y_out[r0:r0 + p, 0:(c1 - c0)],
                            in_=hnd[0:p, c0:c1]), 16)
                        r0 += 128
                g.wait_ge(SO.h, SO.n)
            blk.gpsimd(g_dump)

    return nc


def _consts():
    idt = np.eye(128, dtype=BFNP)
    tri04 = np.triu(np.ones((128, 128), np.float32), k=0).astype(BFNP)
    tri1f = np.triu(np.ones((128, 128), np.float32), k=1).astype(BFNP)
    ocb = np.ones((128, 1), BFNP)
    orb = np.ones((65, 128), BFNP)
    orf = np.ones((1, 128), np.float32)
    cvv = np.zeros((128, 4), np.float32)
    cvv[:, 0] = 1e-5
    cvv[:, 1] = 1.0 / 1024.0
    cvv[:, 2] = -0.5
    return idt, tri04, tri1f, ocb, orb, orf, cvv


def kernel(L0_attn_x, L0_attn_Wq, L0_attn_Wk, L0_attn_Wv, L0_attn_Wo,
           L0_attn_bo, L0_ffn_Wg, L0_ffn_We, L0_ffn_be):
    x = np.ascontiguousarray(np.asarray(L0_attn_x, np.float32)).reshape(NCORES, TOK, D)
    wq = np.ascontiguousarray(np.asarray(L0_attn_Wq, np.float32))
    wk = np.ascontiguousarray(np.asarray(L0_attn_Wk, np.float32))
    wv = np.ascontiguousarray(np.asarray(L0_attn_Wv, np.float32))
    wo = np.ascontiguousarray(np.asarray(L0_attn_Wo, np.float32))
    bo = np.ascontiguousarray(np.asarray(L0_attn_bo, np.float32))
    wg = np.ascontiguousarray(np.asarray(L0_ffn_Wg, np.float32))
    we = np.ascontiguousarray(np.asarray(L0_ffn_We, np.float32))
    be = np.ascontiguousarray(np.asarray(L0_ffn_be, np.float32))
    idt, tri04, tri1f, ocb, orb, orf, cvv = _consts()

    nc = build_program()
    in_maps = []
    for c in range(NCORES):
        prevc = (np.arange(8) < c).astype(np.float32)[:, None].copy()
        in_maps.append({
            "x": x[c], "wq": wq, "wk": wk, "wv": wv, "wo": wo, "we": we,
            "wg": wg, "bo": bo, "be": be, "idt": idt, "triu04": tri04,
            "triu1f": tri1f, "ocb": ocb, "orb": orb, "orf": orf,
            "prevc": prevc, "cvv": cvv,
        })
    res = run_bass_kernel_spmd(nc, in_maps, core_ids=list(range(NCORES)))
    out = np.stack([np.asarray(res.results[c]["y"]) for c in range(NCORES)])
    return out.reshape(16, 2048, 1024)


# version 09
# speedup vs baseline: 1.0032x; 1.0032x over previous
import numpy as np
import ml_dtypes
from contextlib import ExitStack

import concourse.bass as bass
import concourse.mybir as mybir
from concourse.bass_utils import run_bass_kernel_spmd

F32 = mybir.dt.float32
BF16 = mybir.dt.bfloat16
AF = mybir.ActivationFunctionType
ALU = mybir.AluOpType
AX = mybir.AxisListType
BFNP = ml_dtypes.bfloat16

NCORES = 8
D = 1024
TOK = 4096
TT = 256
NT = TOK // TT
NST = 2 * NT
NH = 16
DH = 64
EPS_ATT = 1e-3
CAPF = 4096.0
BATCH_TILES = 8


class Ctr:
    def __init__(self, es, nc, name):
        self.h = es.enter_context(nc.semaphore(name))
        self.n = 0

    def inc(self, ins, k=1):
        ins.then_inc(self.h, k)
        self.n += k
        return self.n


class Bank:

    def __init__(self):
        self.pend = []

    def use(self, eng):
        for h, v in self.pend:
            eng.wait_ge(h, v)
        self.pend = []

    def consumed(self, ctr):
        self.pend.append((ctr.h, ctr.n))


def build_program(nt=NT, lvl=99, dump=None):
    nc = bass.Bass("TRN2", target_bir_lowering=False)

    x_in = nc.dram_tensor("x", [TOK, D], F32, kind="ExternalInput").ap()
    wq_in = nc.dram_tensor("wq", [D, D], F32, kind="ExternalInput").ap()
    wk_in = nc.dram_tensor("wk", [D, D], F32, kind="ExternalInput").ap()
    wv_in = nc.dram_tensor("wv", [D, D], F32, kind="ExternalInput").ap()
    wo_in = nc.dram_tensor("wo", [D, D], F32, kind="ExternalInput").ap()
    we_in = nc.dram_tensor("we", [D, D], F32, kind="ExternalInput").ap()
    wg_in = nc.dram_tensor("wg", [D, 8], F32, kind="ExternalInput").ap()
    bo_in = nc.dram_tensor("bo", [D], F32, kind="ExternalInput").ap()
    be_in = nc.dram_tensor("be", [D], F32, kind="ExternalInput").ap()
    idt_in = nc.dram_tensor("idt", [128, 128], BF16, kind="ExternalInput").ap()
    tri0_in = nc.dram_tensor("triu04", [128, 128], BF16, kind="ExternalInput").ap()
    tri1_in = nc.dram_tensor("triu1f", [128, 128], BF16, kind="ExternalInput").ap()
    ocb_in = nc.dram_tensor("ocb", [128, 1], BF16, kind="ExternalInput").ap()
    orb_in = nc.dram_tensor("orb", [65, 128], BF16, kind="ExternalInput").ap()
    orf_in = nc.dram_tensor("orf", [1, 128], F32, kind="ExternalInput").ap()
    prev_in = nc.dram_tensor("prevc", [8, 1], F32, kind="ExternalInput").ap()
    cvv_in = nc.dram_tensor("cvv", [128, 4], F32, kind="ExternalInput").ap()
    y_out = nc.dram_tensor("y", [TOK, D], F32, kind="ExternalOutput").ap()

    z2d = nc.dram_tensor("z2d", [NT, 128, 2048], BF16)
    cc_in = nc.dram_tensor("cc_in", [1, 8], F32)
    cc_out = nc.dram_tensor("cc_out", [8, 8], F32, addr_space="Shared")

    es = ExitStack()

    def sb(name, shape, dtype):
        return es.enter_context(nc.sbuf_tensor(name, shape, dtype))

    def ps(name, shape, dtype):
        return es.enter_context(nc.psum_tensor(name, shape, dtype))

    XT = sb("XT", [128, 2048], F32)
    OUTS = sb("OUTS", [128, 1024], F32)
    NRM_TM = sb("NRM_TM", [128, 2048], BF16)
    NRM_FM = sb("NRM_FM", [128, 2048], BF16)
    QF = sb("QF", [128, 2048], BF16)
    KF = sb("KF", [128, 2048], BF16)
    KT = sb("KT", [128, 2048], BF16)
    VT = sb("VT", [128, 2080], BF16)
    SM = sb("SM", [128, 2048], BF16)
    AT = sb("AT", [128, 2048], BF16)
    AFULL = sb("AFULL", [128, 2048], BF16)
    Z2B = sb("Z2B", [128, 2048], BF16)
    YB = sb("YB", [128, NST * 1024], BF16)
    WQb = sb("WQb", [128, 8192], BF16)
    WKb = sb("WKb", [128, 8192], BF16)
    WVb = sb("WVb", [128, 8192], BF16)
    WOb = sb("WOb", [128, 8192], BF16)
    WEb = sb("WEb", [128, 8192], BF16)
    WGb = sb("WGb", [128, 64], BF16)
    BOBE = sb("BOBE", [65, 1024], BF16)
    IDT = sb("IDT", [128, 128], BF16)
    TRI0 = sb("TRI0", [128, 128], BF16)
    TRI1F = sb("TRI1F", [128, 128], BF16)
    OCB = sb("OCB", [128, 1], BF16)
    ORB = sb("ORB", [65, 128], BF16)
    ORF = sb("ORF", [1, 128], F32)
    PREVC = sb("PREVC", [8, 1], F32)
    CVV = sb("CVV", [128, 4], F32)
    SG = sb("SGt", [8, 8], F32)
    SOFF = sb("SOFF", [1, 8], F32)
    OFFB = sb("OFFB", [128, 8], F32)
    STE = sb("STE", [128, 8 * 65], F32)
    STO = sb("STO", [128, 8 * 65], F32)
    STEB = sb("STEB", [128, 8 * 65], BF16)
    STOB = sb("STOB", [128, 8 * 65], BF16)
    CNT = sb("CNT", [1, 8], F32)
    MHOT = sb("MHOT", [128, NST * 8], BF16)
    GATE = sb("GATE", [128, NST], F32)
    RANK = sb("RANK", [128, NST], F32)
    M2C = sb("M2C", [128, NST], F32)
    R2C = sb("R2C", [128, NST], F32)
    DINV = sb("DINV", [128, 16], F32)
    SCR8 = sb("SCR8", [128, 8], F32)
    SEXPC = sb("SEXPC", [128, 2], F32)
    NEGM = sb("NEGM", [128, 2], F32)
    SUMT = sb("SUMT", [128, 2], F32)
    SQT = sb("SQT", [128, 2], F32)
    MT = sb("MT", [128, 2], F32)
    T1C = sb("T1C", [128, 2], F32)
    T2C = sb("T2C", [128, 2], F32)
    RSTD = sb("RSTD", [128, 2], F32)
    KGC = sb("KGC", [128, 2], F32)
    OFFC = sb("OFFC", [128, 1], F32)

    dumpd = dict(XT=XT, NRM_TM=NRM_TM, NRM_FM=NRM_FM, QF=QF, KF=KF, KT=KT,
                 VT=VT, SM=SM, AT=AT, AFULL=AFULL, YB=YB, MHOT=MHOT,
                 GATE=GATE, RANK=RANK, M2C=M2C, R2C=R2C, STE=STE, STO=STO,
                 CNT=CNT, OFFB=OFFB, DINV=DINV, SGt=SG, KGC=KGC, WQb=WQb,
                 SUMT=SUMT, SQT=SQT, MT=MT, RSTD=RSTD, T2C=T2C, Z2B=Z2B, OUTS=OUTS)

    PB0 = ps("PB0", [128, 1024], BF16)
    PB1 = ps("PB1", [128, 1024], BF16)
    PB2 = ps("PB2", [128, 512], F32)
    PB3 = ps("PB3", [128, 512], F32)
    PB4 = ps("PB4", [128, 512], F32)
    PB5 = ps("PB5", [128, 512], F32)
    PB6 = ps("PB6", [128, 512], F32)
    PB7 = ps("PB7", [128, 512], F32)

    SD = Ctr(es, nc, "SD")
    SDC = Ctr(es, nc, "SDCc")
    SZ = Ctr(es, nc, "SZc")
    SO = Ctr(es, nc, "SOc")
    SC = Ctr(es, nc, "SCc")
    SGM = Ctr(es, nc, "SGMc")
    ST = Ctr(es, nc, "STc")
    SV = Ctr(es, nc, "SVc")
    SS = Ctr(es, nc, "SSc")

    bk = {name: Bank() for name in ["pb0", "pb1", "pb2", "pb3", "pb4", "pb5", "pb6", "pb7"]}
    zbanks = [PB4, PB5, PB6, PB7]
    zbknames = ["pb4", "pb5", "pb6", "pb7"]

    blk_main = nc.Block()
    blk_main.__enter__()
    blk = blk_main
    if True:
        @blk.sync
        def _(sy):
            for dst, src in [(IDT, idt_in), (TRI0, tri0_in), (TRI1F, tri1_in),
                             (OCB, ocb_in), (ORB, orb_in),
                             (ORF, orf_in), (PREVC, prev_in), (CVV, cvv_in)]:
                SDC.inc(sy.dma_start(out=dst[:, :], in_=src[:, :]), 16)

        @blk.scalar
        def _(se):
            from concourse.hw_specs import get_activation_tables
            set_id = list(get_activation_tables(nc.m.arch)).index(
                "natural_log_exp_and_others")
            se.add_instruction(mybir.InstLoadActFuncSet(
                name=nc.get_next_instruction_name(),
                act_func_set_id=set_id, ins=[], outs=[]))

        @blk.gpsimd
        def _(g):
            g.memset(STE[:, :], 0.0)
            g.memset(VT[:, :].rearrange("p (h u) -> p h u", u=65)[:, :, 64:65],
                     1.0)
            g.memset(STO[:, :], 0.0)
            g.memset(STEB[:, :], 0.0)
            g.memset(STOB[:, :], 0.0)
            ins = g.memset(CNT[:, :], 0.0)
            SGM.inc(ins)

    def emit_weight_loads():
        def g_w(g):
            for dst, src in [(WQb, wq_in), (WKb, wk_in), (WVb, wv_in),
                             (WOb, wo_in), (WEb, we_in)]:
                SD.inc(g.dma_start(
                    out=dst[:, :].rearrange("p (c o) -> p c o", c=8),
                    in_=src.rearrange("(c p) o -> p c o", p=128)), 16)
            SD.inc(g.dma_start(
                out=WGb[:, :].rearrange("p (c e) -> p c e", c=8),
                in_=wg_in.rearrange("(c p) e -> p c e", p=128)), 16)
            SD.inc(g.dma_start(out=BOBE[0:1, 0:1024],
                               in_=bo_in.rearrange("(u o) -> u o", u=1)), 16)
            SD.inc(g.dma_start(out=BOBE[64:65, 0:1024],
                               in_=be_in.rearrange("(u o) -> u o", u=1)), 16)
        blk.gpsimd(g_w)

    SETUP_SD = SD.n + 8 * 16
    SETUP_SDC = SDC.n

    v_y = 0
    v_st = 0
    v_ln2_prev = 0
    t_tr3_prev = 0
    proj_last_fill_ms = 0
    nrm_last_ms = 0
    v_cnt = 0
    first = {"t": True, "v": True, "s": True}

    v_y_hist = {}
    pre_vals = {}

    def emit_prelude(pt):
        XTW = XT
        vy_gate = v_y_hist.get(pt - 1, 0)

        def dseg(sy, pt=pt, vy_gate=vy_gate):
            if vy_gate:
                sy.wait_ge(SV.h, vy_gate)
            SDC.inc(sy.dma_start(
                out=XT[:, :].rearrange("p (s c) -> p s c", s=2),
                in_=x_in[TT * pt:TT * (pt + 1), :].rearrange(
                    "(s p) c -> p s c", p=128)), 16)
        blk.sync(dseg)
        dx = SDC.n

        def g_sq1(g, dx=dx, tprev=t_tr3_prev, XTW=XTW):
            g.wait_ge(SDC.h, dx)
            if tprev:
                g.wait_ge(ST.h, tprev)
            g.tensor_tensor(NRM_TM[:, 0:1024], XTW[:, 0:1024],
                            XTW[:, 0:1024], ALU.mult)
            ins = g.tensor_tensor(NRM_TM[:, 1024:2048], XTW[:, 1024:2048],
                                  XTW[:, 1024:2048], ALU.mult)
            SGM.inc(ins)
        blk.gpsimd(g_sq1)
        g_sq1m = SGM.n

        def v_st1(v, dx=dx, vprev=v_ln2_prev, g_sq1m=g_sq1m, XTW=XTW):
            v.wait_ge(SDC.h, dx)
            if vprev:
                v.wait_ge(SV.h, vprev)
            v.wait_ge(SGM.h, g_sq1m)
            v.reduce_sum(SUMT[:, 0:1], XTW[:, 0:1024], axis=AX.X)
            v.reduce_sum(SUMT[:, 1:2], XTW[:, 1024:2048], axis=AX.X)
            v.reduce_sum(SQT[:, 0:1], NRM_TM[:, 0:1024], axis=AX.X)
            ins = v.reduce_sum(SQT[:, 1:2], NRM_TM[:, 1024:2048], axis=AX.X)
            v.wait_ge(SV.h, SV.inc(ins))
            v.tensor_scalar(MT[:, 0:2], SUMT[:, 0:2],
                            1.0 / 1024.0, None, ALU.mult)
            ins = v.tensor_tensor(T1C[:, 0:2], SUMT[:, 0:2], SUMT[:, 0:2],
                                  ALU.mult)
            v.wait_ge(SV.h, SV.inc(ins))
            ins = v.scalar_tensor_tensor(T2C[:, 0:2], T1C[:, 0:2],
                                         -1.0 / 1024.0, SQT[:, 0:2],
                                         ALU.mult, ALU.add)
            SV.inc(ins)
        blk.vector(v_st1)
        v_ln1s = SV.n

        def s_rstd(se, v_ln1s=v_ln1s, fs=first["s"]):
            if fs:
                se.wait_ge(SDC.h, SETUP_SDC)
            se.wait_ge(SV.h, v_ln1s)
            ins = se.activation(RSTD[:, 0:2], T2C[:, 0:2], AF.Ln,
                                bias=CVV[:, 0:1], scale=CVV[:, 1:2])
            se.wait_ge(SS.h, SS.inc(ins))
            ins = se.activation(RSTD[:, 0:2], RSTD[:, 0:2],
                                AF.Exp, scale=CVV[:, 2:3])
            se.wait_ge(SS.h, SS.inc(ins))
            ins = se.copy(OUTS[:, 17:18], OUTS[:, 15:16])
            SS.inc(ins)
        blk.scalar(s_rstd)
        first["s"] = False
        s_rstd1 = SS.n

        def v_ap1(v, s_rstd1=s_rstd1, XTW=XTW):
            v.wait_ge(SS.h, s_rstd1)
            for s in (0, 1):
                ins = v.tensor_scalar(NRM_TM[:, 1024 * s:1024 * (s + 1)],
                                      XTW[:, 1024 * s:1024 * (s + 1)],
                                      MT[:, s:s + 1], RSTD[:, s:s + 1],
                                      ALU.subtract, ALU.mult)
            SV.inc(ins)
        blk.vector(v_ap1)
        pre_vals[pt] = SV.n

    if True:
        emit_prelude(0)
        emit_weight_loads()
        for t in range(nt):
            v_ln1 = pre_vals[t]
            if lvl < 2:
                if t + 1 < nt:
                    emit_prelude(t + 1)
                continue

            def t_tr1(te, v_ln1=v_ln1, ft=first["t"]):
                if ft:
                    te.wait_ge(SDC.h, SETUP_SDC)
                te.wait_ge(SV.h, v_ln1)
                for s in (0, 1):
                    pb = PB0 if s == 0 else PB1
                    bk["pb0" if s == 0 else "pb1"].use(te)
                    for ci in range(8):
                        ins = te.transpose(pb[:, 128 * ci:128 * (ci + 1)],
                                           NRM_TM[:, 1024 * s + 128 * ci:
                                                  1024 * s + 128 * (ci + 1)],
                                           IDT[:, :])
                    ST.inc(ins)
            blk.tensor(t_tr1)
            first["t"] = False
            t_tr1b = ST.n
            t_tr1a = ST.n - 1

            def v_fm0(v, t_tr1a=t_tr1a, pms=nrm_last_ms):
                v.wait_ge(ST.h, t_tr1a)
                if pms:
                    v.wait_ge(ST.h, pms)
                ins = v.tensor_copy(
                    NRM_FM[:, :].rearrange("p (c u) -> p c u", u=256)[:, :, 0:128],
                    PB0[:, :].rearrange("p (c u) -> p c u", u=128))
                SV.inc(ins)
            blk.vector(v_fm0)
            bk["pb0"].consumed(SV)
            v_fma = SV.n

            def s_fm1(v, t_tr1b=t_tr1b, pms=nrm_last_ms):
                v.wait_ge(ST.h, t_tr1b)
                if pms:
                    v.wait_ge(ST.h, pms)
                ins = v.tensor_copy(
                    NRM_FM[:, :].rearrange("p (c u) -> p c u", u=256)[:, :, 128:256],
                    PB1[:, :].rearrange("p (c u) -> p c u", u=128))
                SV.inc(ins)
            blk.vector(s_fm1)
            bk["pb1"].consumed(SV)
            s_fmb = SV.n
            if lvl < 3:
                continue

            fill_ms = []
            evac_ms = []
            for f in range(12):
                pidx = f // 4
                g = f % 4
                bank = PB2 if f % 2 == 0 else PB3
                bname = "pb2" if f % 2 == 0 else "pb3"
                wsrc = [WQb, WKb, WVb][pidx]

                def t_fill(te, f=f, g=g, bank=bank, bname=bname, wsrc=wsrc,
                           v_fma=v_fma, s_fmb=s_fmb, t=t):
                    if f == 0:
                        if t == 0:
                            te.wait_ge(SD.h, SETUP_SD)
                        te.wait_ge(SV.h, v_fma)
                        te.wait_ge(SV.h, s_fmb)
                    bk[bname].use(te)
                    for j in range(2):
                        oc = 2 * g + j
                        for ci in range(8):
                            ins = te.matmul(
                                bank[:, 256 * j:256 * (j + 1)],
                                wsrc[:, 1024 * ci + 128 * oc:
                                     1024 * ci + 128 * (oc + 1)],
                                NRM_FM[:, 256 * ci:256 * (ci + 1)],
                                start=(j == 0 and ci == 0),
                                stop=(j == 1 and ci == 7))
                    ST.inc(ins)
                blk.tensor(t_fill)
                fms = ST.n
                fill_ms.append(fms)

                dest = [QF, KF, SM][pidx]
                if pidx < 2:
                    def v_ev(v, fms=fms, bank=bank, dest=dest, g=g):
                        v.wait_ge(ST.h, fms)
                        ins = v.tensor_scalar(dest[:, 512 * g:512 * (g + 1)],
                                              bank[:, :], 0.0, EPS_ATT,
                                              ALU.max, ALU.add)
                        SV.inc(ins)
                    blk.vector(v_ev)
                    bk[bname].consumed(SV)
                    evac_ms.append(("v", SV.n))
                else:
                    def s_ev(v, fms=fms, bank=bank, dest=dest, g=g):
                        v.wait_ge(ST.h, fms)
                        ins = v.tensor_copy(dest[:, 512 * g:512 * (g + 1)],
                                            bank[:, :])
                        SV.inc(ins)
                    blk.vector(s_ev)
                    bk[bname].consumed(SV)
                    evac_ms.append(("v", SV.n))
            proj_last_fill_ms = fill_ms[-1]
            qf_sv = evac_ms[3][1]
            kf_sv = evac_ms[7][1]
            vf_ss = evac_ms[11][1]
            if lvl < 4:
                continue

            t2ms = []
            for which in range(4):
                src = KF if which < 2 else SM
                dst = KT if which < 2 else VT
                ss = which % 2
                pb = PB0 if ss == 0 else PB1
                bname = "pb0" if ss == 0 else "pb1"

                def t_t2(te, src=src, ss=ss, pb=pb, bname=bname, which=which,
                         kf_sv=kf_sv, vf_ss=vf_ss):
                    if which == 0:
                        te.wait_ge(SV.h, kf_sv)
                    if which == 2:
                        te.wait_ge(SV.h, vf_ss)
                    bk[bname].use(te)
                    for oc in range(8):
                        ins = te.transpose(pb[:, 128 * oc:128 * (oc + 1)],
                                           src[:, 256 * oc + 128 * ss:
                                               256 * oc + 128 * (ss + 1)],
                                           IDT[:, :])
                    ST.inc(ins)
                blk.tensor(t_t2)
                tms = ST.n
                t2ms.append(tms)

                if ss == 0:
                    def v_t2(v, tms=tms, pb=pb, dst=dst, which=which):
                        v.wait_ge(ST.h, tms)
                        if dst is VT:
                            out = VT[:, 0:1040].rearrange(
                                "p (h u) -> p h u", u=65)[:, :, 0:64]
                        else:
                            out = dst[:, 0:1024]
                        ins = v.tensor_copy(
                            out,
                            pb[:, :].rearrange("p (h u) -> p h u", u=64)
                            if dst is VT else pb[:, :])
                        SV.inc(ins)
                    blk.vector(v_t2)
                    bk[bname].consumed(SV)
                else:
                    def s_t2(v, tms=tms, pb=pb, dst=dst, which=which):
                        v.wait_ge(ST.h, tms)
                        if dst is VT:
                            out = VT[:, 1040:2080].rearrange(
                                "p (h u) -> p h u", u=65)[:, :, 0:64]
                        else:
                            out = dst[:, 1024:2048]
                        ins = v.tensor_copy(
                            out,
                            pb[:, :].rearrange("p (h u) -> p h u", u=64)
                            if dst is VT else pb[:, :])
                        SV.inc(ins)
                    blk.vector(s_t2)
                    bk[bname].consumed(SV)
                if which == 1:
                    kt_sv, kt_ss = SV.n, SV.n
                if which == 3:
                    vt_sv, vt_ss = SV.n, SV.n
            t2_last = t2ms[-1]
            if lvl < 5:
                continue

            if t == BATCH_TILES:
                def v_rst(v, tam=t_att_prev_batch):
                    v.wait_ge(ST.h, tam)
                    v.memset(STE[0:64, :], 0.0)
                    v.memset(STO[64:128, :], 0.0)
                    v.memset(STEB[0:64, :], 0.0)
                    ins = v.memset(STOB[64:128, :], 0.0)
                    SV.inc(ins)
                blk.vector(v_rst)
                v_st = SV.n

            t_att_ms = 0
            v_at = 0
            for ss in (0, 1):
                smv = []
                sc_groups = [(PB4, "pb4", [0, 2, 4, 6], False),
                             (PB5, "pb5", [8, 10, 12, 14], False),
                             (PB4, "pb4", [1, 3, 5, 7], True),
                             (PB5, "pb5", [9, 11, 13, 15], False)]
                for gi, (pb, bname, heads, sep) in enumerate(sc_groups):
                    def t_sc(te, gi=gi, pb=pb, bname=bname, heads=tuple(heads),
                             sep=sep, ss=ss, qf_sv=qf_sv, kf_sv=kf_sv):
                        if gi == 0 and ss == 0:
                            te.wait_ge(SV.h, qf_sv)
                            te.wait_ge(SV.h, kf_sv)
                        bk[bname].use(te)
                        if sep and t == 0 and ss == 0:
                            te.wait_ge(SGM.h, 1)
                        if sep:
                            te.matmul(pb[:, 0:1], IDT[:, :], OCB[:, :],
                                      start=False, stop=False,
                                      skip_group_check=True)
                        for i, h in enumerate(heads):
                            p0 = 64 * (h % 2)
                            c0 = 256 * (h // 2) + 128 * ss
                            ins = te.matmul(
                                pb[:, 128 * i:128 * (i + 1)],
                                KF[p0:p0 + 64, c0:c0 + 128],
                                QF[p0:p0 + 64, c0:c0 + 128],
                                start=(i == 0), stop=(i == 3))
                        ST.inc(ins)
                    blk.tensor(t_sc)
                    scms = ST.n
                    if lvl == 40:
                        bk[bname].pend = []
                        continue

                    def v_sm(v, scms=scms, pb=pb, gi=gi, ss=ss,
                             t2_last=t2_last):
                        v.wait_ge(ST.h, scms)
                        if gi == 0 and ss == 0:
                            v.wait_ge(ST.h, t2_last)
                        base = 1024 * (gi % 2)
                        half = 128 * (gi // 2)
                        ins = v.tensor_tensor(
                            SM[:, base:base + 1024].rearrange(
                                "p (g u) -> p g u", u=256)[:, :, half:half + 128],
                            pb[:, :].rearrange("p (g u) -> p g u", u=128),
                            bass.AP(TRI0, 0, [[128, 128], [0, 4], [1, 128]]),
                            ALU.mult)
                        SV.inc(ins)
                    blk.vector(v_sm)
                    bk[bname].consumed(SV)
                    smv.append(SV.n)

                if lvl in (40, 41):
                    continue
                def t_att(te, ss=ss, smv3=smv[3], v_st=v_st,
                          vt_sv=vt_sv, vt_ss=vt_ss):
                    te.wait_ge(SV.h, smv3)
                    if v_st > 0:
                        te.wait_ge(SV.h, v_st)
                    elif v_st < 0:
                        te.wait_ge(SGM.h, -v_st)
                    else:
                        te.wait_ge(SGM.h, 1)
                    if ss == 0:
                        te.wait_ge(SV.h, vt_sv)
                        te.wait_ge(SV.h, vt_ss)
                    bk["pb2"].use(te)
                    bk["pb6"].use(te)
                    bk["pb7"].use(te)
                    for h in range(NH):
                        if h < 7:
                            ob, oc0 = PB6, 65 * h
                        elif h < 14:
                            ob, oc0 = PB7, 65 * (h - 7)
                        else:
                            ob, oc0 = PB2, 65 * (h - 14)
                        g = h // 2
                        c0 = 256 * g + 128 * ss
                        qs = QF[:, c0:c0 + 128]
                        stb = STEB if h % 2 == 0 else STOB
                        sm = SM[:, 128 * h:128 * (h + 1)]
                        te.matmul(ob[:, oc0:oc0 + 65], sm,
                                  VT[:, 1040 * ss + 65 * h:1040 * ss + 65 * (h + 1)],
                                  start=(oc0 == 0), stop=False)
                        ins = te.matmul(ob[:, oc0:oc0 + 65], qs,
                                        stb[:, 65 * g:65 * g + 65],
                                        start=False,
                                        stop=(h in (6, 13, 15)))
                    ST.inc(ins)
                blk.tensor(t_att)
                t_att_ms = ST.n

                def v_dv(v, t_att_ms=t_att_ms, ss=ss):
                    v.wait_ge(ST.h, t_att_ms)
                    v.reciprocal(DINV[:, 0:7],
                                 PB6[:, 0:455].rearrange("p (h u) -> p h u",
                                                         u=65)[:, :, 64:65])
                    v.reciprocal(DINV[:, 7:14],
                                 PB7[:, 0:455].rearrange("p (h u) -> p h u",
                                                         u=65)[:, :, 64:65])
                    ins = v.reciprocal(DINV[:, 14:16],
                                       PB2[:, 0:130].rearrange(
                                           "p (h u) -> p h u",
                                           u=65)[:, 0:2, 64:65])
                    v.wait_ge(SV.h, SV.inc(ins))
                    for pb, h0, nh in ((PB6, 0, 7), (PB7, 7, 7), (PB2, 14, 2)):
                        ins = v.tensor_tensor(
                            AT[:, 1024 * ss + 64 * h0:
                               1024 * ss + 64 * (h0 + nh)].rearrange(
                                   "p (h e) -> p h e", e=64),
                            pb[:, 0:65 * nh].rearrange(
                                "p (h e) -> p h e", e=65)[:, :, 0:64],
                            bass.AP(DINV, h0, [[16, 128], [1, nh], [0, 64]]),
                            ALU.mult)
                    SV.inc(ins)
                blk.vector(v_dv)
                v_dinv = SV.n - 1
                v_at = SV.n
                bk["pb6"].consumed(SV)
                bk["pb7"].consumed(SV)
                bk["pb2"].consumed(SV)

                if lvl == 42:
                    continue
                def t_stu(te, ss=ss, v_dinv=v_dinv, v_at=v_at,
                          kt_sv=kt_sv, kt_ss=kt_ss):
                    te.wait_ge(SV.h, v_dinv)
                    te.wait_ge(SV.h, v_at)
                    if ss == 0:
                        te.wait_ge(SV.h, kt_sv)
                        te.wait_ge(SV.h, kt_ss)
                    bk["pb3"].use(te)
                    bk["pb6"].use(te)
                    for g in range(8):
                        ktp = KT[:, 1024 * ss + 128 * g:1024 * ss + 128 * (g + 1)]
                        ve = VT[:, 1040 * ss + 130 * g:1040 * ss + 130 * g + 64]
                        vo = VT[:, 1040 * ss + 130 * g + 65:
                                1040 * ss + 130 * g + 129]
                        te.matmul(PB3[:, 64 * g:64 * (g + 1)], ktp, ve,
                                  start=(g == 0), stop=False)
                        te.matmul(PB6[:, 64 * g:64 * (g + 1)], ktp, vo,
                                  start=(g == 0), stop=False)
                        ins = te.matmul(PB2[:, 256 + g:257 + g], ktp, OCB[:, :],
                                        start=False, stop=(g == 7))
                    ST.inc(ins)
                blk.tensor(t_stu)
                t_stu_ms = ST.n

                def v_sta(v, t_stu_ms=t_stu_ms, t=t, ss=ss):
                    if t == 0 and ss == 0:
                        v.wait_ge(SGM.h, 1)
                    v.wait_ge(ST.h, t_stu_ms)
                    ste = STE[:, :].rearrange("p (g u) -> p g u", u=65)
                    sto = STO[:, :].rearrange("p (g u) -> p g u", u=65)
                    v.tensor_tensor(ste[0:64, :, 0:64], ste[0:64, :, 0:64],
                                    PB3[0:64, :].rearrange("p (g u) -> p g u", u=64),
                                    ALU.add)
                    v.tensor_tensor(sto[64:128, :, 0:64], sto[64:128, :, 0:64],
                                    PB6[64:128, :].rearrange("p (g u) -> p g u", u=64),
                                    ALU.add)
                    v.tensor_tensor(ste[0:64, :, 64:65], ste[0:64, :, 64:65],
                                    PB2[0:64, 256:264].rearrange(
                                        "p (g u) -> p g u", u=1),
                                    ALU.add)
                    ins = v.tensor_tensor(sto[64:128, :, 64:65],
                                          sto[64:128, :, 64:65],
                                          PB2[64:128, 256:264].rearrange(
                                              "p (g u) -> p g u", u=1),
                                          ALU.add)
                    ins2 = ins
                    SV.inc(ins2)
                blk.vector(v_sta)
                v_sta_ms = SV.n

                def g_stc(g, v_sta_ms=v_sta_ms):
                    g.wait_ge(SV.h, v_sta_ms)
                    g.tensor_copy(STEB[0:64, :], STE[0:64, :])
                    ins = g.tensor_copy(STOB[64:128, :], STO[64:128, :])
                    g.wait_ge(SGM.h, SGM.inc(ins))
                    ins = g.tensor_copy(SCR8[0:1, 0:2], STE[0:1, 0:2])
                    SGM.inc(ins)
                blk.gpsimd(g_stc)
                v_st = -SGM.n
                bk["pb2"].consumed(SV)
                bk["pb3"].consumed(SV)
                bk["pb6"].consumed(SV)
            t_att_prev_batch = t_att_ms
            if lvl < 6:
                continue

            at3ms = []
            for ss in (0, 1):
                pb = PB0 if ss == 0 else PB1
                bname = "pb0" if ss == 0 else "pb1"

                def t_at3(te, ss=ss, pb=pb, bname=bname, v_at=v_at):
                    if ss == 0 and v_at:
                        te.wait_ge(SV.h, v_at)
                    bk[bname].use(te)
                    for ec in range(8):
                        ins = te.transpose(pb[:, 128 * ec:128 * (ec + 1)],
                                           AT[:, 1024 * ss + 128 * ec:
                                              1024 * ss + 128 * (ec + 1)],
                                           IDT[:, :])
                    ST.inc(ins)
                blk.tensor(t_at3)
                tms = ST.n
                at3ms.append(tms)

                if ss == 0:
                    def v_at3(v, tms=tms, pb=pb):
                        v.wait_ge(ST.h, tms)
                        ins = v.tensor_copy(
                            AFULL[:, :].rearrange("p (c u) -> p c u", u=256)[:, :, 0:128],
                            pb[:, :].rearrange("p (c u) -> p c u", u=128))
                        SV.inc(ins)
                    blk.vector(v_at3)
                    bk["pb0"].consumed(SV)
                    af_sv = SV.n
                else:
                    def s_at3(v, tms=tms, pb=pb):
                        v.wait_ge(ST.h, tms)
                        ins = v.tensor_copy(
                            AFULL[:, :].rearrange("p (c u) -> p c u", u=256)[:, :, 128:256],
                            pb[:, :].rearrange("p (c u) -> p c u", u=128))
                        SV.inc(ins)
                    blk.vector(s_at3)
                    bk["pb1"].consumed(SV)
                    af_ss = SV.n
            if lvl < 7:
                continue

            z_ms = []
            for k, (s, oh) in enumerate([(0, 0), (0, 1), (1, 0), (1, 1)]):
                bankz = zbanks[k]
                bnm = zbknames[k]

                def t_z(te, k=k, s=s, oh=oh, bankz=bankz, bnm=bnm,
                        af_sv=af_sv, af_ss=af_ss):
                    if k == 0:
                        te.wait_ge(SV.h, af_sv)
                        te.wait_ge(SV.h, af_ss)
                    bk[bnm].use(te)
                    for e in range(8):
                        te.matmul(bankz[:, :],
                                  AFULL[:, 256 * e + 128 * s:256 * e + 128 * (s + 1)],
                                  WOb[:, 1024 * e + 512 * oh:1024 * e + 512 * (oh + 1)],
                                  start=(e == 0), stop=False)
                    ins = te.matmul(bankz[:, :], ORB[0:1, :],
                                    BOBE[0:1, 512 * oh:512 * (oh + 1)],
                                    start=False, stop=True)
                    ST.inc(ins)
                blk.tensor(t_z)
                z_ms.append(ST.n)

            def v_yev(v, t=t, z_ms=tuple(z_ms)):
                for k, (s, oh) in enumerate([(0, 0), (0, 1), (1, 0), (1, 1)]):
                    v.wait_ge(ST.h, z_ms[k])
                    st_i = 2 * t + s
                    ins = v.scalar_tensor_tensor(
                        YB[:, 1024 * st_i + 512 * oh:1024 * st_i + 512 * (oh + 1)],
                        zbanks[k][:, :], 1.0,
                        XT[:, 1024 * s + 512 * oh:1024 * s + 512 * (oh + 1)],
                        ALU.mult, ALU.add)
                SV.inc(ins)
            blk.vector(v_yev)
            v_y = SV.n
            v_y_hist[t] = v_y
            for bnm in zbknames:
                bk[bnm].consumed(SV)
            if lvl < 8:
                continue

            def g_sq2(g, t=t, v_y=v_y):
                g.wait_ge(SV.h, v_y)
                yb0 = YB[:, 2048 * t:2048 * t + 1024]
                yb1 = YB[:, 2048 * t + 1024:2048 * t + 2048]
                g.tensor_tensor(NRM_TM[:, 0:1024], yb0, yb0, ALU.mult)
                ins = g.tensor_tensor(NRM_TM[:, 1024:2048], yb1, yb1, ALU.mult)
                SGM.inc(ins)
            blk.gpsimd(g_sq2)
            g_sq2m = SGM.n

            def v_st2(v, t=t, v_y=v_y, g_sq2m=g_sq2m):
                v.wait_ge(SV.h, v_y)
                v.wait_ge(SGM.h, g_sq2m)
                yb0 = YB[:, 2048 * t:2048 * t + 1024]
                yb1 = YB[:, 2048 * t + 1024:2048 * t + 2048]
                v.reduce_sum(SUMT[:, 0:1], yb0, axis=AX.X)
                v.reduce_sum(SUMT[:, 1:2], yb1, axis=AX.X)
                v.reduce_sum(SQT[:, 0:1], NRM_TM[:, 0:1024], axis=AX.X)
                ins = v.reduce_sum(SQT[:, 1:2], NRM_TM[:, 1024:2048], axis=AX.X)
                v.wait_ge(SV.h, SV.inc(ins))
                v.tensor_scalar(M2C[:, 2 * t:2 * t + 2], SUMT[:, 0:2],
                                1.0 / 1024.0, None, ALU.mult)
                ins = v.tensor_tensor(T1C[:, 0:2], SUMT[:, 0:2], SUMT[:, 0:2],
                                      ALU.mult)
                v.wait_ge(SV.h, SV.inc(ins))
                ins = v.scalar_tensor_tensor(T2C[:, 0:2], T1C[:, 0:2],
                                             -1.0 / 1024.0, SQT[:, 0:2],
                                             ALU.mult, ALU.add)
                SV.inc(ins)
            blk.vector(v_st2)
            v_ln2s = SV.n

            def s_rstd2(se, t=t, v_ln2s=v_ln2s):
                se.wait_ge(SV.h, v_ln2s)
                ins = se.activation(R2C[:, 2 * t:2 * t + 2], T2C[:, 0:2], AF.Ln,
                                    bias=CVV[:, 0:1], scale=CVV[:, 1:2])
                se.wait_ge(SS.h, SS.inc(ins))
                ins = se.activation(R2C[:, 2 * t:2 * t + 2],
                                    R2C[:, 2 * t:2 * t + 2],
                                    AF.Exp, scale=CVV[:, 2:3])
                se.wait_ge(SS.h, SS.inc(ins))
                ins = se.copy(OUTS[:, 18:19], OUTS[:, 15:16])
                SS.inc(ins)
            blk.scalar(s_rstd2)
            s_rstd2m = SS.n

            def v_ap2(v, t=t, s_rstd2m=s_rstd2m, t_tr1b=t_tr1b):
                v.wait_ge(SS.h, s_rstd2m)
                v.wait_ge(ST.h, t_tr1b)
                for s in (0, 1):
                    st_i = 2 * t + s
                    ins = v.tensor_scalar(NRM_TM[:, 1024 * s:1024 * (s + 1)],
                                          YB[:, 1024 * st_i:1024 * (st_i + 1)],
                                          M2C[:, st_i:st_i + 1],
                                          R2C[:, st_i:st_i + 1],
                                          ALU.subtract, ALU.mult)
                SV.inc(ins)
            blk.vector(v_ap2)
            v_ln2 = SV.n
            v_ln2_prev = v_ln2
            if lvl < 9:
                continue

            for s in (0, 1):
                pb = PB0 if s == 0 else PB1
                bname = "pb0" if s == 0 else "pb1"

                def t_tr3(te, s=s, pb=pb, bname=bname, v_ln2=v_ln2):
                    if s == 0:
                        te.wait_ge(SV.h, v_ln2)
                    bk[bname].use(te)
                    for ci in range(8):
                        ins = te.transpose(pb[:, 128 * ci:128 * (ci + 1)],
                                           NRM_TM[:, 1024 * s + 128 * ci:
                                                  1024 * s + 128 * (ci + 1)],
                                           IDT[:, :])
                    ST.inc(ins)
                blk.tensor(t_tr3)
                tms = ST.n

                if s == 0:
                    def v_tr3(v, tms=tms, pb=pb, plf=proj_last_fill_ms):
                        v.wait_ge(ST.h, tms)
                        v.wait_ge(ST.h, plf)
                        ins = v.tensor_copy(
                            NRM_FM[:, :].rearrange("p (c u) -> p c u", u=256)[:, :, 0:128],
                            pb[:, :].rearrange("p (c u) -> p c u", u=128))
                        SV.inc(ins)
                    blk.vector(v_tr3)
                    bk["pb0"].consumed(SV)
                    fm3_sv = SV.n
                else:
                    def s_tr3(v, tms=tms, pb=pb, plf=proj_last_fill_ms):
                        v.wait_ge(ST.h, tms)
                        v.wait_ge(ST.h, plf)
                        ins = v.tensor_copy(
                            NRM_FM[:, :].rearrange("p (c u) -> p c u", u=256)[:, :, 128:256],
                            pb[:, :].rearrange("p (c u) -> p c u", u=128))
                        SV.inc(ins)
                    blk.vector(s_tr3)
                    bk["pb1"].consumed(SV)
                    fm3_ss = SV.n
            t_tr3_prev = ST.n
            if lvl < 10:
                continue

            rt_ms = {}
            for s in (0, 1):
                bankr = PB2 if s == 0 else PB3
                bnm = "pb2" if s == 0 else "pb3"

                def t_rt(te, s=s, bankr=bankr, bnm=bnm,
                         fm3_sv=fm3_sv, fm3_ss=fm3_ss):
                    if s == 0:
                        te.wait_ge(SV.h, fm3_sv)
                        te.wait_ge(SV.h, fm3_ss)
                    bk[bnm].use(te)
                    for ci in range(8):
                        ins = te.matmul(bankr[:, 0:8],
                                        NRM_FM[:, 256 * ci + 128 * s:
                                               256 * ci + 128 * (s + 1)],
                                        WGb[:, 8 * ci:8 * (ci + 1)],
                                        start=(ci == 0), stop=(ci == 7))
                    ST.inc(ins)
                blk.tensor(t_rt)
                rt_ms[s] = ST.n

            we_ms = []
            for k, (s2, oh) in enumerate([(0, 0), (0, 1), (1, 0), (1, 1)]):
                bankz = zbanks[k]
                bnm2 = zbknames[k]

                def t_we(te, k=k, s2=s2, oh=oh, bankz=bankz, bnm2=bnm2):
                    bk[bnm2].use(te)
                    for e in range(8):
                        te.matmul(bankz[:, :],
                                  NRM_FM[:, 256 * e + 128 * s2:
                                         256 * e + 128 * (s2 + 1)],
                                  WEb[:, 1024 * e + 512 * oh:
                                      1024 * e + 512 * (oh + 1)],
                                  start=(e == 0), stop=False)
                    ins = te.matmul(bankz[:, :], ORB[64:65, :],
                                    BOBE[64:65, 512 * oh:512 * (oh + 1)],
                                    start=False, stop=True)
                    ST.inc(ins)
                blk.tensor(t_we)
                we_ms.append(ST.n)
            nrm_last_ms = we_ms[-1]

            we_ms_t = tuple(we_ms)

            rte_v = {}
            rte_s = {}

            def v_mx(v, rtm=rt_ms[1]):
                v.wait_ge(ST.h, rtm)
                v.reduce_max(NEGM[:, 0:1], PB2[:, 0:8], axis=AX.X, negate=True)
                ins = v.reduce_max(NEGM[:, 1:2], PB3[:, 0:8], axis=AX.X,
                                   negate=True)
                v.wait_ge(SV.h, SV.inc(ins))
                ins = v.tensor_copy(NEGM[:, 0:2], NEGM[:, 0:2])
                SV.inc(ins)
            blk.vector(v_mx)
            v_negm = SV.n

            def s_ex(se, v_negm=v_negm):
                se.wait_ge(SV.h, v_negm)
                se.activation(OUTS[:, 0:8], PB2[:, 0:8], AF.Exp,
                              bias=NEGM[:, 0:1])
                ins = se.activation(OUTS[:, 8:16], PB3[:, 0:8], AF.Exp,
                                    bias=NEGM[:, 1:2])
                se.wait_ge(SS.h, SS.inc(ins))
                ins = se.copy(OUTS[:, 16:17], OUTS[:, 15:16])
                SS.inc(ins)
            blk.scalar(s_ex)
            rte_s[0] = rte_s[1] = SS.n

            def v_mh(v, t=t, sx=rte_s[1]):
                v.wait_ge(SS.h, sx)
                v.tensor_scalar(MHOT[:, 8 * (2 * t):8 * (2 * t + 1)],
                                PB2[:, 0:8], NEGM[:, 0:1], 0.0,
                                ALU.add, ALU.is_ge)
                v.tensor_scalar(MHOT[:, 8 * (2 * t + 1):8 * (2 * t + 2)],
                                PB3[:, 0:8], NEGM[:, 1:2], 0.0,
                                ALU.add, ALU.is_ge)
                v.reduce_sum(SEXPC[:, 0:1], OUTS[:, 0:8], axis=AX.X)
                ins = v.reduce_sum(SEXPC[:, 1:2], OUTS[:, 8:16], axis=AX.X)
                v.wait_ge(SV.h, SV.inc(ins))
                ins = v.reciprocal(GATE[:, 2 * t:2 * t + 2], SEXPC[:, 0:2])
                SV.inc(ins)
            blk.vector(v_mh)
            rte_v[0] = rte_v[1] = SV.n
            if lvl < 11:
                continue

            for s in (0, 1):
                bankr = PB2 if s == 0 else PB3
                bnm = "pb2" if s == 0 else "pb3"
                bk[bnm].consumed(SV)
                bk[bnm].pend.append((SS.h, rte_s[s]))

                def t_rk(te, s=s, t=t, bankr=bankr, bnm=bnm,
                         vr=rte_v[s], sxm=rte_s[s], v_cnt=v_cnt):
                    st_i = 2 * t + s
                    te.wait_ge(SV.h, vr)
                    te.wait_ge(SS.h, sxm)
                    if v_cnt:
                        te.wait_ge(SV.h, v_cnt)
                    else:
                        te.wait_ge(SGM.h, 1)
                    te.matmul(bankr[:, 8:16], TRI1F[:, :],
                              MHOT[:, 8 * st_i:8 * (st_i + 1)],
                              start=False, stop=False)
                    te.matmul(bankr[:, 8:16], ORF[0:1, :], CNT[0:1, :],
                              start=False, stop=False)
                    ins = te.matmul(bankr[0:1, 16:24], OCB[:, :],
                                    MHOT[:, 8 * st_i:8 * (st_i + 1)],
                                    start=False, stop=True)
                    ST.inc(ins)
                blk.tensor(t_rk)
                t_rk_ms = ST.n

                def v_rk(v, s=s, t=t, bankr=bankr, t_rk_ms=t_rk_ms):
                    st_i = 2 * t + s
                    if t == 0 and s == 0:
                        v.wait_ge(SGM.h, 1)
                    v.wait_ge(ST.h, t_rk_ms)
                    ins = v.tensor_tensor(SCR8[:, :], bankr[:, 8:16],
                                          MHOT[:, 8 * st_i:8 * (st_i + 1)],
                                          ALU.mult)
                    v.wait_ge(SV.h, SV.inc(ins))
                    v.reduce_sum(RANK[:, st_i:st_i + 1], SCR8[:, :], axis=AX.X)
                    ins = v.tensor_tensor(CNT[0:1, :], CNT[0:1, :],
                                          bankr[0:1, 16:24], ALU.add)
                    v.wait_ge(SV.h, SV.inc(ins))
                    ins = v.tensor_copy(SCR8[0:1, 0:8], CNT[0:1, :])
                    SV.inc(ins)
                blk.vector(v_rk)
                v_cnt = SV.n
                bk[bnm].consumed(SV)

            if t + 1 < nt:
                emit_prelude(t + 1)

            def v_z2(v, t=t, we_ms=we_ms_t):
                if t > 0:
                    v.wait_ge(SZ.h, 16 * t)
                for k in range(4):
                    v.wait_ge(ST.h, we_ms[k])
                    ins = v.tensor_copy(Z2B[:, 512 * k:512 * (k + 1)],
                                        zbanks[k][:, :])
                SV.inc(ins)
            blk.vector(v_z2)
            v_z2_ms = SV.n
            for bnm2 in zbknames:
                bk[bnm2].consumed(SV)

            def g_z2(g, t=t, v_z2_ms=v_z2_ms):
                g.wait_ge(SV.h, v_z2_ms)
                SZ.inc(g.dma_start(out=z2d.ap()[t], in_=Z2B[:, :]), 16)
            blk.gpsimd(g_z2)

        def g_cc(g, v_cnt=v_cnt):
            if lvl < 12:
                return
            g.wait_ge(SV.h, v_cnt)
            SD.inc(g.dma_start(out=cc_in.ap()[:, :], in_=CNT[:, :]), 16)
            g.wait_ge(SD.h, SD.n)
            SC.inc(g.collective_compute(
                "AllGather", ALU.bypass,
                replica_groups=[list(range(NCORES))],
                ins=[cc_in.ap()[:, :]],
                outs=[cc_out.ap()[:, :]]))
            g.wait_ge(SC.h, SC.n)
            SD.inc(g.dma_start(out=SG[:, :], in_=cc_out.ap()[:, :]), 16)
        blk.gpsimd(g_cc)
        sd_g = SD.n

    blk_main.__exit__(None, None, None)

    for b in bk.values():
        b.pend = []

    with nc.Block() as blk:
        if lvl < 12:
            pass
        def t_off(te, sd_g=sd_g):
            if lvl < 12:
                return
            te.wait_ge(SD.h, sd_g)
            ins = te.matmul(PB2[0:1, 0:8], PREVC[:, :], SG[:, :],
                            start=True, stop=True)
            ST.inc(ins)
        blk.tensor(t_off)
        t_off_ms = ST.n

        def v_off(v, t_off_ms=t_off_ms):
            if lvl < 12:
                return
            v.wait_ge(ST.h, t_off_ms)
            ins = v.tensor_copy(SOFF[:, :], PB2[0:1, 0:8])
            v.wait_ge(SV.h, SV.inc(ins))
            ins = v.tensor_copy(SCR8[0:1, 0:8], SOFF[:, :])
            SV.inc(ins)
        blk.vector(v_off)
        v_off_ms = SV.n

        def t_ob(te, v_off_ms=v_off_ms):
            if lvl < 12:
                return
            te.wait_ge(SV.h, v_off_ms)
            ins = te.matmul(PB3[:, 0:8], ORF[0:1, :], SOFF[0:1, :],
                            start=True, stop=True)
            ST.inc(ins)
        blk.tensor(t_ob)
        t_ob_ms = ST.n

        def v_ob(v, t_ob_ms=t_ob_ms):
            if lvl < 12:
                return
            v.wait_ge(ST.h, t_ob_ms)
            ins = v.tensor_copy(OFFB[:, :], PB3[:, 0:8])
            SV.inc(ins)
        blk.vector(v_ob)

    z2_stores = SZ.n
    v_fin_prev = 0
    with nc.Block() as blk:
        for t in (range(nt) if lvl >= 13 else []):
            def v_kg(v, t=t):
                for s in (0, 1):
                    st_i = 2 * t + s
                    ins = v.tensor_tensor(SCR8[:, :],
                                          MHOT[:, 8 * st_i:8 * (st_i + 1)],
                                          OFFB[:, :], ALU.mult)
                    v.wait_ge(SV.h, SV.inc(ins))
                    ins = v.reduce_sum(OFFC[:, :], SCR8[:, :], axis=AX.X)
                    v.wait_ge(SV.h, SV.inc(ins))
                    ins = v.tensor_tensor(OFFC[:, :], OFFC[:, :],
                                          RANK[:, st_i:st_i + 1], ALU.add)
                    v.wait_ge(SV.h, SV.inc(ins))
                    ins = v.tensor_scalar(KGC[:, s:s + 1], OFFC[:, :], CAPF,
                                          None, ALU.is_lt)
                    v.wait_ge(SV.h, SV.inc(ins))
                    ins = v.tensor_tensor(KGC[:, s:s + 1], KGC[:, s:s + 1],
                                          GATE[:, st_i:st_i + 1], ALU.mult)
                SV.inc(ins)
            blk.vector(v_kg)
            v_kg_ms = SV.n

            def d_z2l(sy, t=t, vprev=v_fin_prev):
                if vprev:
                    sy.wait_ge(SV.h, vprev)
                sy.wait_ge(SZ.h, z2_stores)
                SZ.inc(sy.dma_start(out=NRM_TM[:, :], in_=z2d.ap()[t]), 16)
            blk.sync(d_z2l)
            z2l_ms = SZ.n

            for s in (0, 1):
                def v_fin(v, t=t, s=s, z2l_ms=z2l_ms):
                    half = 2 * t + s
                    if half > 0:
                        v.wait_ge(SO.h, 16 * half)
                    if s == 0:
                        v.wait_ge(SZ.h, z2l_ms)
                    st_i = 2 * t + s
                    for oh in (0, 1):
                        ins = v.scalar_tensor_tensor(
                            OUTS[:, 512 * oh:512 * (oh + 1)],
                            NRM_TM[:, 1024 * s + 512 * oh:
                                   1024 * s + 512 * (oh + 1)],
                            KGC[:, s:s + 1],
                            YB[:, 1024 * st_i + 512 * oh:
                               1024 * st_i + 512 * (oh + 1)],
                            ALU.mult, ALU.add)
                    SV.inc(ins)
                blk.vector(v_fin)
                v_fin_ms = SV.n

                def d_out(sy, t=t, s=s, v_fin_ms=v_fin_ms):
                    sy.wait_ge(SV.h, v_fin_ms)
                    SO.inc(sy.dma_start(
                        out=y_out[TT * t + 128 * s:TT * t + 128 * (s + 1), :],
                        in_=OUTS[:, :]), 16)
                blk.sync(d_out)
            v_fin_prev = v_fin_ms

    with nc.Block() as blk:
        @blk.sync
        def _(sy):
            if SO.n:
                sy.wait_ge(SO.h, SO.n)
            sy.wait_ge(SD.h, SD.n)
            sy.wait_ge(ST.h, ST.n)
            sy.wait_ge(SV.h, SV.n)
            sy.wait_ge(SS.h, SS.n)

    if dump:
        with nc.Block() as blk:
            def g_dump(g):
                r0 = 0
                for name in dump:
                    if ":" in name:
                        name, cstr = name.split(":")
                        hnd = dumpd[name]
                        cols = int(cstr)
                    else:
                        hnd = dumpd[name]
                        cols = hnd.shape[1] if hasattr(hnd, 'shape') else None
                    nb = max(1, (cols + 1023) // 1024)
                    for b in range(nb):
                        c0 = 1024 * b
                        c1 = min(cols, c0 + 1024)
                        p = hnd.shape[0]
                        SO.inc(g.dma_start(
                            out=y_out[r0:r0 + p, 0:(c1 - c0)],
                            in_=hnd[0:p, c0:c1]), 16)
                        r0 += 128
                g.wait_ge(SO.h, SO.n)
            blk.gpsimd(g_dump)

    return nc


def _consts():
    idt = np.eye(128, dtype=BFNP)
    tri04 = np.triu(np.ones((128, 128), np.float32), k=0).astype(BFNP)
    tri1f = np.triu(np.ones((128, 128), np.float32), k=1).astype(BFNP)
    ocb = np.ones((128, 1), BFNP)
    orb = np.ones((65, 128), BFNP)
    orf = np.ones((1, 128), np.float32)
    cvv = np.zeros((128, 4), np.float32)
    cvv[:, 0] = 1e-5
    cvv[:, 1] = 1.0 / 1024.0
    cvv[:, 2] = -0.5
    return idt, tri04, tri1f, ocb, orb, orf, cvv


def kernel(L0_attn_x, L0_attn_Wq, L0_attn_Wk, L0_attn_Wv, L0_attn_Wo,
           L0_attn_bo, L0_ffn_Wg, L0_ffn_We, L0_ffn_be):
    x = np.ascontiguousarray(np.asarray(L0_attn_x, np.float32)).reshape(NCORES, TOK, D)
    wq = np.ascontiguousarray(np.asarray(L0_attn_Wq, np.float32))
    wk = np.ascontiguousarray(np.asarray(L0_attn_Wk, np.float32))
    wv = np.ascontiguousarray(np.asarray(L0_attn_Wv, np.float32))
    wo = np.ascontiguousarray(np.asarray(L0_attn_Wo, np.float32))
    bo = np.ascontiguousarray(np.asarray(L0_attn_bo, np.float32))
    wg = np.ascontiguousarray(np.asarray(L0_ffn_Wg, np.float32))
    we = np.ascontiguousarray(np.asarray(L0_ffn_We, np.float32))
    be = np.ascontiguousarray(np.asarray(L0_ffn_be, np.float32))
    idt, tri04, tri1f, ocb, orb, orf, cvv = _consts()

    nc = build_program()
    in_maps = []
    for c in range(NCORES):
        prevc = (np.arange(8) < c).astype(np.float32)[:, None].copy()
        in_maps.append({
            "x": x[c], "wq": wq, "wk": wk, "wv": wv, "wo": wo, "we": we,
            "wg": wg, "bo": bo, "be": be, "idt": idt, "triu04": tri04,
            "triu1f": tri1f, "ocb": ocb, "orb": orb, "orf": orf,
            "prevc": prevc, "cvv": cvv,
        })
    res = run_bass_kernel_spmd(nc, in_maps, core_ids=list(range(NCORES)))
    out = np.stack([np.asarray(res.results[c]["y"]) for c in range(NCORES)])
    return out.reshape(16, 2048, 1024)


# version 10
# speedup vs baseline: 1.0677x; 1.0643x over previous
import numpy as np
import ml_dtypes
from contextlib import ExitStack

import concourse.bass as bass
import concourse.mybir as mybir
from concourse.bass_utils import run_bass_kernel_spmd

F32 = mybir.dt.float32
BF16 = mybir.dt.bfloat16
AF = mybir.ActivationFunctionType
ALU = mybir.AluOpType
AX = mybir.AxisListType
BFNP = ml_dtypes.bfloat16

NCORES = 8
D = 1024
TOK = 4096
TT = 256
NT = TOK // TT
NST = 2 * NT
NH = 16
DH = 64
EPS_ATT = 1e-3
CAPF = 4096.0
BATCH_TILES = 8


class Ctr:
    def __init__(self, es, nc, name):
        self.h = es.enter_context(nc.semaphore(name))
        self.n = 0

    def inc(self, ins, k=1):
        ins.then_inc(self.h, k)
        self.n += k
        return self.n


class Bank:

    def __init__(self):
        self.pend = []

    def use(self, eng):
        for h, v in self.pend:
            eng.wait_ge(h, v)
        self.pend = []

    def consumed(self, ctr):
        self.pend.append((ctr.h, ctr.n))


def build_program(nt=NT, lvl=99, dump=None):
    nc = bass.Bass("TRN2", target_bir_lowering=False)

    x_in = nc.dram_tensor("x", [TOK, D], F32, kind="ExternalInput").ap()
    wq_in = nc.dram_tensor("wq", [D, D], F32, kind="ExternalInput").ap()
    wk_in = nc.dram_tensor("wk", [D, D], F32, kind="ExternalInput").ap()
    wv_in = nc.dram_tensor("wv", [D, D], F32, kind="ExternalInput").ap()
    wo_in = nc.dram_tensor("wo", [D, D], F32, kind="ExternalInput").ap()
    we_in = nc.dram_tensor("we", [D, D], F32, kind="ExternalInput").ap()
    wg_in = nc.dram_tensor("wg", [D, 8], F32, kind="ExternalInput").ap()
    bo_in = nc.dram_tensor("bo", [D], F32, kind="ExternalInput").ap()
    be_in = nc.dram_tensor("be", [D], F32, kind="ExternalInput").ap()
    idt_in = nc.dram_tensor("idt", [128, 128], BF16, kind="ExternalInput").ap()
    tri0_in = nc.dram_tensor("triu04", [128, 128], BF16, kind="ExternalInput").ap()
    tri1_in = nc.dram_tensor("triu1f", [128, 128], BF16, kind="ExternalInput").ap()
    ocb_in = nc.dram_tensor("ocb", [128, 1], BF16, kind="ExternalInput").ap()
    orb_in = nc.dram_tensor("orb", [65, 128], BF16, kind="ExternalInput").ap()
    orf_in = nc.dram_tensor("orf", [1, 128], F32, kind="ExternalInput").ap()
    prev_in = nc.dram_tensor("prevc", [8, 1], F32, kind="ExternalInput").ap()
    cvv_in = nc.dram_tensor("cvv", [128, 4], F32, kind="ExternalInput").ap()
    y_out = nc.dram_tensor("y", [TOK, D], F32, kind="ExternalOutput").ap()

    z2d = nc.dram_tensor("z2d", [NT, 128, 2048], BF16)
    cc_in = nc.dram_tensor("cc_in", [1, 8], F32)
    cc_out = nc.dram_tensor("cc_out", [8, 8], F32, addr_space="Shared")

    es = ExitStack()

    def sb(name, shape, dtype):
        return es.enter_context(nc.sbuf_tensor(name, shape, dtype))

    def ps(name, shape, dtype):
        return es.enter_context(nc.psum_tensor(name, shape, dtype))

    XT = sb("XT", [128, 2048], F32)
    OUTS = sb("OUTS", [128, 1024], F32)
    NRM_TM = sb("NRM_TM", [128, 2048], BF16)
    NRM_FM = sb("NRM_FM", [128, 2048], BF16)
    QF = sb("QF", [128, 2048], BF16)
    KF = sb("KF", [128, 2048], BF16)
    KT = sb("KT", [128, 2048], BF16)
    VT = sb("VT", [128, 2080], BF16)
    SM = sb("SM", [128, 2048], BF16)
    AT = sb("AT", [128, 2048], BF16)
    AFULL = sb("AFULL", [128, 2048], BF16)
    Z2B = sb("Z2B", [128, 2048], BF16)
    YB = sb("YB", [128, NST * 1024], BF16)
    WQb = sb("WQb", [128, 8192], BF16)
    WKb = sb("WKb", [128, 8192], BF16)
    WVb = sb("WVb", [128, 8192], BF16)
    WOb = sb("WOb", [128, 8192], BF16)
    WEb = sb("WEb", [128, 8192], BF16)
    WGb = sb("WGb", [128, 64], BF16)
    BOBE = sb("BOBE", [65, 1024], BF16)
    IDT = sb("IDT", [128, 128], BF16)
    TRI0 = sb("TRI0", [128, 128], BF16)
    TRI1F = sb("TRI1F", [128, 128], BF16)
    OCB = sb("OCB", [128, 1], BF16)
    ORB = sb("ORB", [65, 128], BF16)
    ORF = sb("ORF", [1, 128], F32)
    PREVC = sb("PREVC", [8, 1], F32)
    CVV = sb("CVV", [128, 4], F32)
    SG = sb("SGt", [8, 8], F32)
    SOFF = sb("SOFF", [1, 8], F32)
    OFFB = sb("OFFB", [128, 8], F32)
    STE = sb("STE", [128, 8 * 65], F32)
    STO = sb("STO", [128, 8 * 65], F32)
    STEB = sb("STEB", [128, 8 * 65], BF16)
    STOB = sb("STOB", [128, 8 * 65], BF16)
    CNT = sb("CNT", [1, 8], F32)
    MHOT = sb("MHOT", [128, NST * 8], BF16)
    GATE = sb("GATE", [128, NST], F32)
    RANK = sb("RANK", [128, NST], F32)
    M2C = sb("M2C", [128, NST], F32)
    R2C = sb("R2C", [128, NST], F32)
    DINV = sb("DINV", [128, 16], F32)
    SCR8 = sb("SCR8", [128, 8], F32)
    SEXPC = sb("SEXPC", [128, 2], F32)
    NEGM = sb("NEGM", [128, 2], F32)
    SUMT = sb("SUMT", [128, 2], F32)
    SQT = sb("SQT", [128, 2], F32)
    MT = sb("MT", [128, 2], F32)
    T1C = sb("T1C", [128, 2], F32)
    T2C = sb("T2C", [128, 2], F32)
    RSTD = sb("RSTD", [128, 2], F32)
    KGC = sb("KGC", [128, 2], F32)
    OFFC = sb("OFFC", [128, 1], F32)

    dumpd = dict(XT=XT, NRM_TM=NRM_TM, NRM_FM=NRM_FM, QF=QF, KF=KF, KT=KT,
                 VT=VT, SM=SM, AT=AT, AFULL=AFULL, YB=YB, MHOT=MHOT,
                 GATE=GATE, RANK=RANK, M2C=M2C, R2C=R2C, STE=STE, STO=STO,
                 CNT=CNT, OFFB=OFFB, DINV=DINV, SGt=SG, KGC=KGC, WQb=WQb,
                 SUMT=SUMT, SQT=SQT, MT=MT, RSTD=RSTD, T2C=T2C, Z2B=Z2B, OUTS=OUTS)

    PB0 = ps("PB0", [128, 1024], BF16)
    PB1 = ps("PB1", [128, 1024], BF16)
    PB2 = ps("PB2", [128, 512], F32)
    PB3 = ps("PB3", [128, 512], F32)
    PB4 = ps("PB4", [128, 512], F32)
    PB5 = ps("PB5", [128, 512], F32)
    PB6 = ps("PB6", [128, 512], F32)
    PB7 = ps("PB7", [128, 512], F32)

    SD = Ctr(es, nc, "SD")
    SDC = Ctr(es, nc, "SDCc")
    SZ = Ctr(es, nc, "SZc")
    SO = Ctr(es, nc, "SOc")
    SC = Ctr(es, nc, "SCc")
    SGM = Ctr(es, nc, "SGMc")
    ST = Ctr(es, nc, "STc")
    SV = Ctr(es, nc, "SVc")
    SS = Ctr(es, nc, "SSc")

    bk = {name: Bank() for name in ["pb0", "pb1", "pb2", "pb3", "pb4", "pb5", "pb6", "pb7"]}
    zbanks = [PB4, PB5, PB6, PB7]
    zbknames = ["pb4", "pb5", "pb6", "pb7"]

    blk_main = nc.Block()
    blk_main.__enter__()
    blk = blk_main
    if True:
        @blk.sync
        def _(sy):
            for dst, src in [(IDT, idt_in), (TRI0, tri0_in), (TRI1F, tri1_in),
                             (OCB, ocb_in), (ORB, orb_in),
                             (ORF, orf_in), (PREVC, prev_in), (CVV, cvv_in)]:
                SDC.inc(sy.dma_start(out=dst[:, :], in_=src[:, :]), 16)

        @blk.scalar
        def _(se):
            from concourse.hw_specs import get_activation_tables
            set_id = list(get_activation_tables(nc.m.arch)).index(
                "natural_log_exp_and_others")
            se.add_instruction(mybir.InstLoadActFuncSet(
                name=nc.get_next_instruction_name(),
                act_func_set_id=set_id, ins=[], outs=[]))

        @blk.gpsimd
        def _(g):
            g.memset(STE[:, :], 0.0)
            g.memset(VT[:, :].rearrange("p (h u) -> p h u", u=65)[:, :, 64:65],
                     1.0)
            g.memset(STO[:, :], 0.0)
            g.memset(STEB[:, :], 0.0)
            g.memset(STOB[:, :], 0.0)
            ins = g.memset(CNT[:, :], 0.0)
            SGM.inc(ins)

    def emit_weight_loads():
        def g_w(g):
            for dst, src in [(WQb, wq_in), (WKb, wk_in), (WVb, wv_in),
                             (WOb, wo_in), (WEb, we_in)]:
                SD.inc(g.dma_start(
                    out=dst[:, :].rearrange("p (c o) -> p c o", c=8),
                    in_=src.rearrange("(c p) o -> p c o", p=128)), 16)
            SD.inc(g.dma_start(
                out=WGb[:, :].rearrange("p (c e) -> p c e", c=8),
                in_=wg_in.rearrange("(c p) e -> p c e", p=128)), 16)
            SD.inc(g.dma_start(out=BOBE[0:1, 0:1024],
                               in_=bo_in.rearrange("(u o) -> u o", u=1)), 16)
            SD.inc(g.dma_start(out=BOBE[64:65, 0:1024],
                               in_=be_in.rearrange("(u o) -> u o", u=1)), 16)
        blk.gpsimd(g_w)

    SETUP_SD = SD.n + 8 * 16
    SETUP_SDC = SDC.n

    v_y = 0
    v_st = 0
    v_ln2_prev = 0
    t_tr3_prev = 0
    proj_last_fill_ms = 0
    nrm_last_ms = 0
    v_cnt = 0
    first = {"t": True, "v": True, "s": True}

    v_y_hist = {}
    pre_vals = {}

    def emit_prelude(pt):
        XTW = XT
        vy_gate = v_y_hist.get(pt - 1, 0)

        def dseg(sy, pt=pt, vy_gate=vy_gate):
            if vy_gate:
                sy.wait_ge(SV.h, vy_gate)
            SDC.inc(sy.dma_start(
                out=XT[:, :].rearrange("p (s c) -> p s c", s=2),
                in_=x_in[TT * pt:TT * (pt + 1), :].rearrange(
                    "(s p) c -> p s c", p=128)), 16)
        blk.sync(dseg)
        dx = SDC.n

        def g_sq1(g, dx=dx, tprev=t_tr3_prev, XTW=XTW):
            g.wait_ge(SDC.h, dx)
            if tprev:
                g.wait_ge(ST.h, tprev)
            g.tensor_tensor(NRM_TM[:, 0:1024], XTW[:, 0:1024],
                            XTW[:, 0:1024], ALU.mult)
            ins = g.tensor_tensor(NRM_TM[:, 1024:2048], XTW[:, 1024:2048],
                                  XTW[:, 1024:2048], ALU.mult)
            SGM.inc(ins)
        blk.gpsimd(g_sq1)
        g_sq1m = SGM.n

        def v_st1(v, dx=dx, vprev=v_ln2_prev, g_sq1m=g_sq1m, XTW=XTW):
            v.wait_ge(SDC.h, dx)
            if vprev:
                v.wait_ge(SV.h, vprev)
            v.wait_ge(SGM.h, g_sq1m)
            v.reduce_sum(SUMT[:, 0:1], XTW[:, 0:1024], axis=AX.X)
            v.reduce_sum(SUMT[:, 1:2], XTW[:, 1024:2048], axis=AX.X)
            v.reduce_sum(SQT[:, 0:1], NRM_TM[:, 0:1024], axis=AX.X)
            ins = v.reduce_sum(SQT[:, 1:2], NRM_TM[:, 1024:2048], axis=AX.X)
            v.wait_ge(SV.h, SV.inc(ins))
            v.tensor_scalar(MT[:, 0:2], SUMT[:, 0:2],
                            1.0 / 1024.0, None, ALU.mult)
            ins = v.tensor_tensor(T1C[:, 0:2], SUMT[:, 0:2], SUMT[:, 0:2],
                                  ALU.mult)
            v.wait_ge(SV.h, SV.inc(ins))
            ins = v.scalar_tensor_tensor(T2C[:, 0:2], T1C[:, 0:2],
                                         -1.0 / 1024.0, SQT[:, 0:2],
                                         ALU.mult, ALU.add)
            SV.inc(ins)
        blk.vector(v_st1)
        v_ln1s = SV.n

        def s_rstd(se, v_ln1s=v_ln1s, fs=first["s"]):
            if fs:
                se.wait_ge(SDC.h, SETUP_SDC)
            se.wait_ge(SV.h, v_ln1s)
            ins = se.activation(RSTD[:, 0:2], T2C[:, 0:2], AF.Ln,
                                bias=CVV[:, 0:1], scale=CVV[:, 1:2])
            se.wait_ge(SS.h, SS.inc(ins))
            ins = se.activation(RSTD[:, 0:2], RSTD[:, 0:2],
                                AF.Exp, scale=CVV[:, 2:3])
            se.wait_ge(SS.h, SS.inc(ins))
            ins = se.copy(OUTS[:, 17:18], OUTS[:, 15:16])
            SS.inc(ins)
        blk.scalar(s_rstd)
        first["s"] = False
        s_rstd1 = SS.n

        def v_ap1(v, s_rstd1=s_rstd1, XTW=XTW):
            v.wait_ge(SS.h, s_rstd1)
            for s in (0, 1):
                ins = v.tensor_scalar(NRM_TM[:, 1024 * s:1024 * (s + 1)],
                                      XTW[:, 1024 * s:1024 * (s + 1)],
                                      MT[:, s:s + 1], RSTD[:, s:s + 1],
                                      ALU.subtract, ALU.mult)
            SV.inc(ins)
        blk.vector(v_ap1)
        pre_vals[pt] = SV.n

    if True:
        emit_prelude(0)
        emit_weight_loads()
        for t in range(nt):
            v_ln1 = pre_vals[t]
            if lvl < 2:
                if t + 1 < nt:
                    emit_prelude(t + 1)
                continue

            def t_tr1(te, v_ln1=v_ln1, ft=first["t"]):
                if ft:
                    te.wait_ge(SDC.h, SETUP_SDC)
                te.wait_ge(SV.h, v_ln1)
                for s in (0, 1):
                    pb = PB0 if s == 0 else PB1
                    bk["pb0" if s == 0 else "pb1"].use(te)
                    for ci in range(8):
                        ins = te.transpose(pb[:, 128 * ci:128 * (ci + 1)],
                                           NRM_TM[:, 1024 * s + 128 * ci:
                                                  1024 * s + 128 * (ci + 1)],
                                           IDT[:, :])
                    ST.inc(ins)
            blk.tensor(t_tr1)
            first["t"] = False
            t_tr1b = ST.n
            t_tr1a = ST.n - 1

            def v_fm0(v, t_tr1a=t_tr1a, pms=nrm_last_ms):
                v.wait_ge(ST.h, t_tr1a)
                if pms:
                    v.wait_ge(ST.h, pms)
                ins = v.tensor_copy(
                    NRM_FM[:, :].rearrange("p (c u) -> p c u", u=256)[:, :, 0:128],
                    PB0[:, :].rearrange("p (c u) -> p c u", u=128))
                SV.inc(ins)
            blk.vector(v_fm0)
            bk["pb0"].consumed(SV)
            v_fma = SV.n

            def s_fm1(v, t_tr1b=t_tr1b, pms=nrm_last_ms):
                v.wait_ge(ST.h, t_tr1b)
                if pms:
                    v.wait_ge(ST.h, pms)
                ins = v.tensor_copy(
                    NRM_FM[:, :].rearrange("p (c u) -> p c u", u=256)[:, :, 128:256],
                    PB1[:, :].rearrange("p (c u) -> p c u", u=128))
                SV.inc(ins)
            blk.vector(s_fm1)
            bk["pb1"].consumed(SV)
            s_fmb = SV.n
            if lvl < 3:
                continue

            fill_ms = []
            evac_ms = []
            for f in range(12):
                pidx = f // 4
                g = f % 4
                bank = PB2 if f % 2 == 0 else PB3
                bname = "pb2" if f % 2 == 0 else "pb3"
                wsrc = [WQb, WKb, WVb][pidx]

                def t_fill(te, f=f, g=g, bank=bank, bname=bname, wsrc=wsrc,
                           v_fma=v_fma, s_fmb=s_fmb, t=t):
                    if f == 0:
                        if t == 0:
                            te.wait_ge(SD.h, SETUP_SD)
                        te.wait_ge(SV.h, v_fma)
                        te.wait_ge(SV.h, s_fmb)
                    bk[bname].use(te)
                    for j in range(2):
                        oc = 2 * g + j
                        for ci in range(8):
                            ins = te.matmul(
                                bank[:, 256 * j:256 * (j + 1)],
                                wsrc[:, 1024 * ci + 128 * oc:
                                     1024 * ci + 128 * (oc + 1)],
                                NRM_FM[:, 256 * ci:256 * (ci + 1)],
                                start=(j == 0 and ci == 0),
                                stop=(j == 1 and ci == 7))
                    ST.inc(ins)
                blk.tensor(t_fill)
                fms = ST.n
                fill_ms.append(fms)

                dest = [QF, KF, SM][pidx]
                if pidx < 2:
                    def v_ev(v, fms=fms, bank=bank, dest=dest, g=g):
                        v.wait_ge(ST.h, fms)
                        ins = v.tensor_scalar(dest[:, 512 * g:512 * (g + 1)],
                                              bank[:, :], 0.0, EPS_ATT,
                                              ALU.max, ALU.add)
                        SV.inc(ins)
                    blk.vector(v_ev)
                    bk[bname].consumed(SV)
                    evac_ms.append(("v", SV.n))
                else:
                    def s_ev(v, fms=fms, bank=bank, dest=dest, g=g):
                        v.wait_ge(ST.h, fms)
                        ins = v.tensor_copy(dest[:, 512 * g:512 * (g + 1)],
                                            bank[:, :])
                        SV.inc(ins)
                    blk.vector(s_ev)
                    bk[bname].consumed(SV)
                    evac_ms.append(("v", SV.n))
            proj_last_fill_ms = fill_ms[-1]
            qf_sv = evac_ms[3][1]
            kf_sv = evac_ms[7][1]
            vf_ss = evac_ms[11][1]
            if lvl < 4:
                continue

            t2ms = []
            for which in range(4):
                src = KF if which < 2 else SM
                dst = KT if which < 2 else VT
                ss = which % 2
                pb = PB0 if ss == 0 else PB1
                bname = "pb0" if ss == 0 else "pb1"

                def t_t2(te, src=src, ss=ss, pb=pb, bname=bname, which=which,
                         kf_sv=kf_sv, vf_ss=vf_ss):
                    if which == 0:
                        te.wait_ge(SV.h, kf_sv)
                    if which == 2:
                        te.wait_ge(SV.h, vf_ss)
                    bk[bname].use(te)
                    for oc in range(8):
                        ins = te.transpose(pb[:, 128 * oc:128 * (oc + 1)],
                                           src[:, 256 * oc + 128 * ss:
                                               256 * oc + 128 * (ss + 1)],
                                           IDT[:, :])
                    ST.inc(ins)
                blk.tensor(t_t2)
                tms = ST.n
                t2ms.append(tms)

                if ss == 0:
                    def v_t2(v, tms=tms, pb=pb, dst=dst, which=which):
                        v.wait_ge(ST.h, tms)
                        if dst is VT:
                            out = VT[:, 0:1040].rearrange(
                                "p (h u) -> p h u", u=65)[:, :, 0:64]
                        else:
                            out = dst[:, 0:1024]
                        ins = v.tensor_copy(
                            out,
                            pb[:, :].rearrange("p (h u) -> p h u", u=64)
                            if dst is VT else pb[:, :])
                        SV.inc(ins)
                    blk.vector(v_t2)
                    bk[bname].consumed(SV)
                else:
                    def s_t2(v, tms=tms, pb=pb, dst=dst, which=which):
                        v.wait_ge(ST.h, tms)
                        if dst is VT:
                            out = VT[:, 1040:2080].rearrange(
                                "p (h u) -> p h u", u=65)[:, :, 0:64]
                        else:
                            out = dst[:, 1024:2048]
                        ins = v.tensor_copy(
                            out,
                            pb[:, :].rearrange("p (h u) -> p h u", u=64)
                            if dst is VT else pb[:, :])
                        SV.inc(ins)
                    blk.vector(s_t2)
                    bk[bname].consumed(SV)
                if which == 1:
                    kt_sv, kt_ss = SV.n, SV.n
                if which == 3:
                    vt_sv, vt_ss = SV.n, SV.n
            t2_last = t2ms[-1]
            if lvl < 5:
                continue

            if t == BATCH_TILES:
                def v_rst(v, tam=t_att_prev_batch):
                    v.wait_ge(ST.h, tam)
                    v.memset(STE[0:64, :], 0.0)
                    v.memset(STO[64:128, :], 0.0)
                    v.memset(STEB[0:64, :], 0.0)
                    ins = v.memset(STOB[64:128, :], 0.0)
                    SV.inc(ins)
                blk.vector(v_rst)
                v_st = SV.n

            t_att_ms = 0
            v_at = 0
            for ss in (0, 1):
                smv = []
                sc_groups = [(PB4, "pb4", [0, 2, 4, 6], False),
                             (PB5, "pb5", [8, 10, 12, 14], False),
                             (PB4, "pb4", [1, 3, 5, 7], True),
                             (PB5, "pb5", [9, 11, 13, 15], False)]
                for gi, (pb, bname, heads, sep) in enumerate(sc_groups):
                    def t_sc(te, gi=gi, pb=pb, bname=bname, heads=tuple(heads),
                             sep=sep, ss=ss, qf_sv=qf_sv, kf_sv=kf_sv):
                        if gi == 0 and ss == 0:
                            te.wait_ge(SV.h, qf_sv)
                            te.wait_ge(SV.h, kf_sv)
                        bk[bname].use(te)
                        if sep and t == 0 and ss == 0:
                            te.wait_ge(SGM.h, 1)
                        if sep:
                            te.matmul(pb[:, 0:1], IDT[:, :], OCB[:, :],
                                      start=False, stop=False,
                                      skip_group_check=True)
                        for i, h in enumerate(heads):
                            p0 = 64 * (h % 2)
                            c0 = 256 * (h // 2) + 128 * ss
                            ins = te.matmul(
                                pb[:, 128 * i:128 * (i + 1)],
                                KF[p0:p0 + 64, c0:c0 + 128],
                                QF[p0:p0 + 64, c0:c0 + 128],
                                start=(i == 0), stop=(i == 3))
                        ST.inc(ins)
                    blk.tensor(t_sc)
                    scms = ST.n
                    if lvl == 40:
                        bk[bname].pend = []
                        continue

                    def v_sm(v, scms=scms, pb=pb, gi=gi, ss=ss,
                             t2_last=t2_last):
                        v.wait_ge(ST.h, scms)
                        if gi == 0 and ss == 0:
                            v.wait_ge(ST.h, t2_last)
                        base = 1024 * (gi % 2)
                        half = 128 * (gi // 2)
                        ins = v.tensor_tensor(
                            SM[:, base:base + 1024].rearrange(
                                "p (g u) -> p g u", u=256)[:, :, half:half + 128],
                            pb[:, :].rearrange("p (g u) -> p g u", u=128),
                            bass.AP(TRI0, 0, [[128, 128], [0, 4], [1, 128]]),
                            ALU.mult)
                        SV.inc(ins)
                    blk.vector(v_sm)
                    bk[bname].consumed(SV)
                    smv.append(SV.n)

                if lvl in (40, 41):
                    continue
                def t_att(te, ss=ss, smv3=smv[3], v_st=v_st,
                          vt_sv=vt_sv, vt_ss=vt_ss):
                    te.wait_ge(SV.h, smv3)
                    if v_st > 0:
                        te.wait_ge(SV.h, v_st)
                    elif v_st < 0:
                        te.wait_ge(SGM.h, -v_st)
                    else:
                        te.wait_ge(SGM.h, 1)
                    if ss == 0:
                        te.wait_ge(SV.h, vt_sv)
                        te.wait_ge(SV.h, vt_ss)
                    bk["pb2"].use(te)
                    bk["pb6"].use(te)
                    bk["pb7"].use(te)
                    for h in range(NH):
                        if h < 7:
                            ob, oc0 = PB6, 65 * h
                        elif h < 14:
                            ob, oc0 = PB7, 65 * (h - 7)
                        else:
                            ob, oc0 = PB2, 65 * (h - 14)
                        g = h // 2
                        c0 = 256 * g + 128 * ss
                        qs = QF[:, c0:c0 + 128]
                        stb = STEB if h % 2 == 0 else STOB
                        sm = SM[:, 128 * h:128 * (h + 1)]
                        te.matmul(ob[:, oc0:oc0 + 65], sm,
                                  VT[:, 1040 * ss + 65 * h:1040 * ss + 65 * (h + 1)],
                                  start=(oc0 == 0), stop=False)
                        ins = te.matmul(ob[:, oc0:oc0 + 65], qs,
                                        stb[:, 65 * g:65 * g + 65],
                                        start=False,
                                        stop=(h in (6, 13, 15)))
                    ST.inc(ins)
                blk.tensor(t_att)
                t_att_ms = ST.n

                def v_dv(v, t_att_ms=t_att_ms, ss=ss):
                    v.wait_ge(ST.h, t_att_ms)
                    v.reciprocal(DINV[:, 0:7],
                                 PB6[:, 0:455].rearrange("p (h u) -> p h u",
                                                         u=65)[:, :, 64:65])
                    v.reciprocal(DINV[:, 7:14],
                                 PB7[:, 0:455].rearrange("p (h u) -> p h u",
                                                         u=65)[:, :, 64:65])
                    ins = v.reciprocal(DINV[:, 14:16],
                                       PB2[:, 0:130].rearrange(
                                           "p (h u) -> p h u",
                                           u=65)[:, 0:2, 64:65])
                    v.wait_ge(SV.h, SV.inc(ins))
                    for pb, h0, nh in ((PB6, 0, 7), (PB7, 7, 7), (PB2, 14, 2)):
                        ins = v.tensor_tensor(
                            AT[:, 1024 * ss + 64 * h0:
                               1024 * ss + 64 * (h0 + nh)].rearrange(
                                   "p (h e) -> p h e", e=64),
                            pb[:, 0:65 * nh].rearrange(
                                "p (h e) -> p h e", e=65)[:, :, 0:64],
                            bass.AP(DINV, h0, [[16, 128], [1, nh], [0, 64]]),
                            ALU.mult)
                    SV.inc(ins)
                blk.vector(v_dv)
                v_dinv = SV.n - 1
                v_at = SV.n
                bk["pb6"].consumed(SV)
                bk["pb7"].consumed(SV)
                bk["pb2"].consumed(SV)

                if lvl == 42:
                    continue
                def t_stu(te, ss=ss, v_dinv=v_dinv, v_at=v_at,
                          kt_sv=kt_sv, kt_ss=kt_ss):
                    te.wait_ge(SV.h, v_dinv)
                    te.wait_ge(SV.h, v_at)
                    if ss == 0:
                        te.wait_ge(SV.h, kt_sv)
                        te.wait_ge(SV.h, kt_ss)
                    bk["pb3"].use(te)
                    bk["pb6"].use(te)
                    for g in range(8):
                        ktp = KT[:, 1024 * ss + 128 * g:1024 * ss + 128 * (g + 1)]
                        ve = VT[:, 1040 * ss + 130 * g:1040 * ss + 130 * g + 64]
                        vo = VT[:, 1040 * ss + 130 * g + 65:
                                1040 * ss + 130 * g + 129]
                        te.matmul(PB3[:, 64 * g:64 * (g + 1)], ktp, ve,
                                  start=(g == 0), stop=False)
                        te.matmul(PB6[:, 64 * g:64 * (g + 1)], ktp, vo,
                                  start=(g == 0), stop=False)
                        ins = te.matmul(PB2[:, 256 + g:257 + g], ktp, OCB[:, :],
                                        start=False, stop=(g == 7))
                    ST.inc(ins)
                blk.tensor(t_stu)
                t_stu_ms = ST.n

                def v_sta(v, t_stu_ms=t_stu_ms, t=t, ss=ss):
                    if t == 0 and ss == 0:
                        v.wait_ge(SGM.h, 1)
                    v.wait_ge(ST.h, t_stu_ms)
                    ste = STE[:, :].rearrange("p (g u) -> p g u", u=65)
                    sto = STO[:, :].rearrange("p (g u) -> p g u", u=65)
                    v.tensor_tensor(ste[0:64, :, 0:64], ste[0:64, :, 0:64],
                                    PB3[0:64, :].rearrange("p (g u) -> p g u", u=64),
                                    ALU.add)
                    v.tensor_tensor(sto[64:128, :, 0:64], sto[64:128, :, 0:64],
                                    PB6[64:128, :].rearrange("p (g u) -> p g u", u=64),
                                    ALU.add)
                    v.tensor_tensor(ste[0:64, :, 64:65], ste[0:64, :, 64:65],
                                    PB2[0:64, 256:264].rearrange(
                                        "p (g u) -> p g u", u=1),
                                    ALU.add)
                    ins = v.tensor_tensor(sto[64:128, :, 64:65],
                                          sto[64:128, :, 64:65],
                                          PB2[64:128, 256:264].rearrange(
                                              "p (g u) -> p g u", u=1),
                                          ALU.add)
                    ins2 = ins
                    SV.inc(ins2)
                blk.vector(v_sta)
                v_sta_ms = SV.n

                def g_stc(g, v_sta_ms=v_sta_ms):
                    g.wait_ge(SV.h, v_sta_ms)
                    g.tensor_copy(STEB[0:64, :], STE[0:64, :])
                    ins = g.tensor_copy(STOB[64:128, :], STO[64:128, :])
                    g.wait_ge(SGM.h, SGM.inc(ins))
                    ins = g.tensor_copy(SCR8[0:1, 0:2], STE[0:1, 0:2])
                    SGM.inc(ins)
                blk.gpsimd(g_stc)
                v_st = -SGM.n
                bk["pb2"].consumed(SV)
                bk["pb3"].consumed(SV)
                bk["pb6"].consumed(SV)
            t_att_prev_batch = t_att_ms
            if lvl < 6:
                continue

            at3ms = []
            for ss in (0, 1):
                pb = PB0 if ss == 0 else PB1
                bname = "pb0" if ss == 0 else "pb1"

                def t_at3(te, ss=ss, pb=pb, bname=bname, v_at=v_at):
                    if ss == 0 and v_at:
                        te.wait_ge(SV.h, v_at)
                    bk[bname].use(te)
                    for ec in range(8):
                        ins = te.transpose(pb[:, 128 * ec:128 * (ec + 1)],
                                           AT[:, 1024 * ss + 128 * ec:
                                              1024 * ss + 128 * (ec + 1)],
                                           IDT[:, :])
                    ST.inc(ins)
                blk.tensor(t_at3)
                tms = ST.n
                at3ms.append(tms)

                if ss == 0:
                    def v_at3(v, tms=tms, pb=pb):
                        v.wait_ge(ST.h, tms)
                        ins = v.tensor_copy(
                            AFULL[:, :].rearrange("p (c u) -> p c u", u=256)[:, :, 0:128],
                            pb[:, :].rearrange("p (c u) -> p c u", u=128))
                        SV.inc(ins)
                    blk.vector(v_at3)
                    bk["pb0"].consumed(SV)
                    af_sv = SV.n
                else:
                    def s_at3(v, tms=tms, pb=pb):
                        v.wait_ge(ST.h, tms)
                        ins = v.tensor_copy(
                            AFULL[:, :].rearrange("p (c u) -> p c u", u=256)[:, :, 128:256],
                            pb[:, :].rearrange("p (c u) -> p c u", u=128))
                        SV.inc(ins)
                    blk.vector(s_at3)
                    bk["pb1"].consumed(SV)
                    af_ss = SV.n
            if lvl < 7:
                continue

            z_ms = []
            for k, (s, oh) in enumerate([(0, 0), (0, 1), (1, 0), (1, 1)]):
                bankz = zbanks[k]
                bnm = zbknames[k]

                def t_z(te, k=k, s=s, oh=oh, bankz=bankz, bnm=bnm,
                        af_sv=af_sv, af_ss=af_ss):
                    if k == 0:
                        te.wait_ge(SV.h, af_sv)
                        te.wait_ge(SV.h, af_ss)
                    bk[bnm].use(te)
                    for e in range(8):
                        te.matmul(bankz[:, :],
                                  AFULL[:, 256 * e + 128 * s:256 * e + 128 * (s + 1)],
                                  WOb[:, 1024 * e + 512 * oh:1024 * e + 512 * (oh + 1)],
                                  start=(e == 0), stop=False)
                    ins = te.matmul(bankz[:, :], ORB[0:1, :],
                                    BOBE[0:1, 512 * oh:512 * (oh + 1)],
                                    start=False, stop=True)
                    ST.inc(ins)
                blk.tensor(t_z)
                z_ms.append(ST.n)

            def v_yev(v, t=t, z_ms=tuple(z_ms)):
                for k, (s, oh) in enumerate([(0, 0), (0, 1), (1, 0), (1, 1)]):
                    v.wait_ge(ST.h, z_ms[k])
                    st_i = 2 * t + s
                    ins = v.scalar_tensor_tensor(
                        YB[:, 1024 * st_i + 512 * oh:1024 * st_i + 512 * (oh + 1)],
                        zbanks[k][:, :], 1.0,
                        XT[:, 1024 * s + 512 * oh:1024 * s + 512 * (oh + 1)],
                        ALU.mult, ALU.add)
                SV.inc(ins)
            blk.vector(v_yev)
            v_y = SV.n
            v_y_hist[t] = v_y
            for bnm in zbknames:
                bk[bnm].consumed(SV)
            if lvl < 8:
                if t + 1 < nt:
                    emit_prelude(t + 1)
                continue

            def g_sq2(g, t=t, v_y=v_y):
                g.wait_ge(SV.h, v_y)
                yb0 = YB[:, 2048 * t:2048 * t + 1024]
                yb1 = YB[:, 2048 * t + 1024:2048 * t + 2048]
                g.tensor_tensor(NRM_TM[:, 0:1024], yb0, yb0, ALU.mult)
                ins = g.tensor_tensor(NRM_TM[:, 1024:2048], yb1, yb1, ALU.mult)
                SGM.inc(ins)
            blk.gpsimd(g_sq2)
            g_sq2m = SGM.n

            def v_st2(v, t=t, v_y=v_y, g_sq2m=g_sq2m):
                v.wait_ge(SV.h, v_y)
                v.wait_ge(SGM.h, g_sq2m)
                yb0 = YB[:, 2048 * t:2048 * t + 1024]
                yb1 = YB[:, 2048 * t + 1024:2048 * t + 2048]
                v.reduce_sum(SUMT[:, 0:1], yb0, axis=AX.X)
                v.reduce_sum(SUMT[:, 1:2], yb1, axis=AX.X)
                v.reduce_sum(SQT[:, 0:1], NRM_TM[:, 0:1024], axis=AX.X)
                ins = v.reduce_sum(SQT[:, 1:2], NRM_TM[:, 1024:2048], axis=AX.X)
                v.wait_ge(SV.h, SV.inc(ins))
                v.tensor_scalar(M2C[:, 2 * t:2 * t + 2], SUMT[:, 0:2],
                                1.0 / 1024.0, None, ALU.mult)
                ins = v.tensor_tensor(T1C[:, 0:2], SUMT[:, 0:2], SUMT[:, 0:2],
                                      ALU.mult)
                v.wait_ge(SV.h, SV.inc(ins))
                ins = v.scalar_tensor_tensor(T2C[:, 0:2], T1C[:, 0:2],
                                             -1.0 / 1024.0, SQT[:, 0:2],
                                             ALU.mult, ALU.add)
                SV.inc(ins)
            blk.vector(v_st2)
            v_ln2s = SV.n

            def s_rstd2(se, t=t, v_ln2s=v_ln2s):
                se.wait_ge(SV.h, v_ln2s)
                ins = se.activation(R2C[:, 2 * t:2 * t + 2], T2C[:, 0:2], AF.Ln,
                                    bias=CVV[:, 0:1], scale=CVV[:, 1:2])
                se.wait_ge(SS.h, SS.inc(ins))
                ins = se.activation(R2C[:, 2 * t:2 * t + 2],
                                    R2C[:, 2 * t:2 * t + 2],
                                    AF.Exp, scale=CVV[:, 2:3])
                se.wait_ge(SS.h, SS.inc(ins))
                ins = se.copy(OUTS[:, 18:19], OUTS[:, 15:16])
                SS.inc(ins)
            blk.scalar(s_rstd2)
            s_rstd2m = SS.n

            def v_ap2(v, t=t, s_rstd2m=s_rstd2m, t_tr1b=t_tr1b):
                v.wait_ge(SS.h, s_rstd2m)
                v.wait_ge(ST.h, t_tr1b)
                for s in (0, 1):
                    st_i = 2 * t + s
                    ins = v.tensor_scalar(NRM_TM[:, 1024 * s:1024 * (s + 1)],
                                          YB[:, 1024 * st_i:1024 * (st_i + 1)],
                                          M2C[:, st_i:st_i + 1],
                                          R2C[:, st_i:st_i + 1],
                                          ALU.subtract, ALU.mult)
                SV.inc(ins)
            blk.vector(v_ap2)
            v_ln2 = SV.n
            v_ln2_prev = v_ln2
            if lvl < 9:
                if t + 1 < nt:
                    emit_prelude(t + 1)
                continue

            for s in (0, 1):
                pb = PB0 if s == 0 else PB1
                bname = "pb0" if s == 0 else "pb1"

                def t_tr3(te, s=s, pb=pb, bname=bname, v_ln2=v_ln2):
                    if s == 0:
                        te.wait_ge(SV.h, v_ln2)
                    bk[bname].use(te)
                    for ci in range(8):
                        ins = te.transpose(pb[:, 128 * ci:128 * (ci + 1)],
                                           NRM_TM[:, 1024 * s + 128 * ci:
                                                  1024 * s + 128 * (ci + 1)],
                                           IDT[:, :])
                    ST.inc(ins)
                blk.tensor(t_tr3)
                tms = ST.n

                if s == 0:
                    def v_tr3(v, tms=tms, pb=pb, plf=proj_last_fill_ms):
                        v.wait_ge(ST.h, tms)
                        v.wait_ge(ST.h, plf)
                        ins = v.tensor_copy(
                            NRM_FM[:, :].rearrange("p (c u) -> p c u", u=256)[:, :, 0:128],
                            pb[:, :].rearrange("p (c u) -> p c u", u=128))
                        SV.inc(ins)
                    blk.vector(v_tr3)
                    bk["pb0"].consumed(SV)
                    fm3_sv = SV.n
                else:
                    def s_tr3(v, tms=tms, pb=pb, plf=proj_last_fill_ms):
                        v.wait_ge(ST.h, tms)
                        v.wait_ge(ST.h, plf)
                        ins = v.tensor_copy(
                            NRM_FM[:, :].rearrange("p (c u) -> p c u", u=256)[:, :, 128:256],
                            pb[:, :].rearrange("p (c u) -> p c u", u=128))
                        SV.inc(ins)
                    blk.vector(s_tr3)
                    bk["pb1"].consumed(SV)
                    fm3_ss = SV.n
            t_tr3_prev = ST.n
            if lvl < 10:
                if t + 1 < nt:
                    emit_prelude(t + 1)
                continue

            rt_ms = {}
            for s in (0, 1):
                bankr = PB2 if s == 0 else PB3
                bnm = "pb2" if s == 0 else "pb3"

                def t_rt(te, s=s, bankr=bankr, bnm=bnm,
                         fm3_sv=fm3_sv, fm3_ss=fm3_ss):
                    if s == 0:
                        te.wait_ge(SV.h, fm3_sv)
                        te.wait_ge(SV.h, fm3_ss)
                    bk[bnm].use(te)
                    for ci in range(8):
                        ins = te.matmul(bankr[:, 0:8],
                                        NRM_FM[:, 256 * ci + 128 * s:
                                               256 * ci + 128 * (s + 1)],
                                        WGb[:, 8 * ci:8 * (ci + 1)],
                                        start=(ci == 0), stop=(ci == 7))
                    ST.inc(ins)
                blk.tensor(t_rt)
                rt_ms[s] = ST.n

            we_ms = []
            for k, (s2, oh) in enumerate([(0, 0), (0, 1), (1, 0), (1, 1)]):
                bankz = zbanks[k]
                bnm2 = zbknames[k]

                def t_we(te, k=k, s2=s2, oh=oh, bankz=bankz, bnm2=bnm2):
                    bk[bnm2].use(te)
                    for e in range(8):
                        te.matmul(bankz[:, :],
                                  NRM_FM[:, 256 * e + 128 * s2:
                                         256 * e + 128 * (s2 + 1)],
                                  WEb[:, 1024 * e + 512 * oh:
                                      1024 * e + 512 * (oh + 1)],
                                  start=(e == 0), stop=False)
                    ins = te.matmul(bankz[:, :], ORB[64:65, :],
                                    BOBE[64:65, 512 * oh:512 * (oh + 1)],
                                    start=False, stop=True)
                    ST.inc(ins)
                blk.tensor(t_we)
                we_ms.append(ST.n)
            nrm_last_ms = we_ms[-1]

            we_ms_t = tuple(we_ms)

            if t + 1 < nt:
                emit_prelude(t + 1)

            rte_v = {}
            rte_s = {}

            def v_mx(v, rtm=rt_ms[1]):
                v.wait_ge(ST.h, rtm)
                v.reduce_max(NEGM[:, 0:1], PB2[:, 0:8], axis=AX.X, negate=True)
                ins = v.reduce_max(NEGM[:, 1:2], PB3[:, 0:8], axis=AX.X,
                                   negate=True)
                v.wait_ge(SV.h, SV.inc(ins))
                ins = v.tensor_copy(NEGM[:, 0:2], NEGM[:, 0:2])
                SV.inc(ins)
            blk.vector(v_mx)
            v_negm = SV.n

            def s_ex(se, v_negm=v_negm):
                se.wait_ge(SV.h, v_negm)
                se.activation(OUTS[:, 0:8], PB2[:, 0:8], AF.Exp,
                              bias=NEGM[:, 0:1])
                ins = se.activation(OUTS[:, 8:16], PB3[:, 0:8], AF.Exp,
                                    bias=NEGM[:, 1:2])
                se.wait_ge(SS.h, SS.inc(ins))
                ins = se.copy(OUTS[:, 16:17], OUTS[:, 15:16])
                SS.inc(ins)
            blk.scalar(s_ex)
            rte_s[0] = rte_s[1] = SS.n

            def v_mh(v, t=t, sx=rte_s[1]):
                v.wait_ge(SS.h, sx)
                v.tensor_scalar(MHOT[:, 8 * (2 * t):8 * (2 * t + 1)],
                                PB2[:, 0:8], NEGM[:, 0:1], 0.0,
                                ALU.add, ALU.is_ge)
                v.tensor_scalar(MHOT[:, 8 * (2 * t + 1):8 * (2 * t + 2)],
                                PB3[:, 0:8], NEGM[:, 1:2], 0.0,
                                ALU.add, ALU.is_ge)
                v.reduce_sum(SEXPC[:, 0:1], OUTS[:, 0:8], axis=AX.X)
                ins = v.reduce_sum(SEXPC[:, 1:2], OUTS[:, 8:16], axis=AX.X)
                v.wait_ge(SV.h, SV.inc(ins))
                ins = v.reciprocal(GATE[:, 2 * t:2 * t + 2], SEXPC[:, 0:2])
                SV.inc(ins)
            blk.vector(v_mh)
            rte_v[0] = rte_v[1] = SV.n
            if lvl < 11:
                continue

            for s in (0, 1):
                bankr = PB2 if s == 0 else PB3
                bnm = "pb2" if s == 0 else "pb3"
                bk[bnm].consumed(SV)
                bk[bnm].pend.append((SS.h, rte_s[s]))

                def t_rk(te, s=s, t=t, bankr=bankr, bnm=bnm,
                         vr=rte_v[s], sxm=rte_s[s], v_cnt=v_cnt):
                    st_i = 2 * t + s
                    te.wait_ge(SV.h, vr)
                    te.wait_ge(SS.h, sxm)
                    if v_cnt:
                        te.wait_ge(SV.h, v_cnt)
                    else:
                        te.wait_ge(SGM.h, 1)
                    te.matmul(bankr[:, 8:16], TRI1F[:, :],
                              MHOT[:, 8 * st_i:8 * (st_i + 1)],
                              start=False, stop=False)
                    te.matmul(bankr[:, 8:16], ORF[0:1, :], CNT[0:1, :],
                              start=False, stop=False)
                    ins = te.matmul(bankr[0:1, 16:24], OCB[:, :],
                                    MHOT[:, 8 * st_i:8 * (st_i + 1)],
                                    start=False, stop=True)
                    ST.inc(ins)
                blk.tensor(t_rk)
                t_rk_ms = ST.n

                def v_rk(v, s=s, t=t, bankr=bankr, t_rk_ms=t_rk_ms):
                    st_i = 2 * t + s
                    if t == 0 and s == 0:
                        v.wait_ge(SGM.h, 1)
                    v.wait_ge(ST.h, t_rk_ms)
                    ins = v.tensor_tensor(SCR8[:, :], bankr[:, 8:16],
                                          MHOT[:, 8 * st_i:8 * (st_i + 1)],
                                          ALU.mult)
                    v.wait_ge(SV.h, SV.inc(ins))
                    v.reduce_sum(RANK[:, st_i:st_i + 1], SCR8[:, :], axis=AX.X)
                    ins = v.tensor_tensor(CNT[0:1, :], CNT[0:1, :],
                                          bankr[0:1, 16:24], ALU.add)
                    v.wait_ge(SV.h, SV.inc(ins))
                    ins = v.tensor_copy(SCR8[0:1, 0:8], CNT[0:1, :])
                    SV.inc(ins)
                blk.vector(v_rk)
                v_cnt = SV.n
                bk[bnm].consumed(SV)

            def v_z2(v, t=t, we_ms=we_ms_t):
                if t > 0:
                    v.wait_ge(SZ.h, 16 * t)
                for k in range(4):
                    v.wait_ge(ST.h, we_ms[k])
                    ins = v.tensor_copy(Z2B[:, 512 * k:512 * (k + 1)],
                                        zbanks[k][:, :])
                SV.inc(ins)
            blk.vector(v_z2)
            v_z2_ms = SV.n
            for bnm2 in zbknames:
                bk[bnm2].consumed(SV)

            def g_z2(g, t=t, v_z2_ms=v_z2_ms):
                g.wait_ge(SV.h, v_z2_ms)
                SZ.inc(g.dma_start(out=z2d.ap()[t], in_=Z2B[:, :]), 16)
            blk.gpsimd(g_z2)

        def g_cc(g, v_cnt=v_cnt):
            if lvl < 12:
                return
            g.wait_ge(SV.h, v_cnt)
            SD.inc(g.dma_start(out=cc_in.ap()[:, :], in_=CNT[:, :]), 16)
            g.wait_ge(SD.h, SD.n)
            SC.inc(g.collective_compute(
                "AllGather", ALU.bypass,
                replica_groups=[list(range(NCORES))],
                ins=[cc_in.ap()[:, :]],
                outs=[cc_out.ap()[:, :]]))
            g.wait_ge(SC.h, SC.n)
            SD.inc(g.dma_start(out=SG[:, :], in_=cc_out.ap()[:, :]), 16)
        blk.gpsimd(g_cc)
        sd_g = SD.n

    blk_main.__exit__(None, None, None)

    for b in bk.values():
        b.pend = []

    with nc.Block() as blk:
        if lvl < 12:
            pass
        def t_off(te, sd_g=sd_g):
            if lvl < 12:
                return
            te.wait_ge(SD.h, sd_g)
            ins = te.matmul(PB2[0:1, 0:8], PREVC[:, :], SG[:, :],
                            start=True, stop=True)
            ST.inc(ins)
        blk.tensor(t_off)
        t_off_ms = ST.n

        def v_off(v, t_off_ms=t_off_ms):
            if lvl < 12:
                return
            v.wait_ge(ST.h, t_off_ms)
            ins = v.tensor_copy(SOFF[:, :], PB2[0:1, 0:8])
            v.wait_ge(SV.h, SV.inc(ins))
            ins = v.tensor_copy(SCR8[0:1, 0:8], SOFF[:, :])
            SV.inc(ins)
        blk.vector(v_off)
        v_off_ms = SV.n

        def t_ob(te, v_off_ms=v_off_ms):
            if lvl < 12:
                return
            te.wait_ge(SV.h, v_off_ms)
            ins = te.matmul(PB3[:, 0:8], ORF[0:1, :], SOFF[0:1, :],
                            start=True, stop=True)
            ST.inc(ins)
        blk.tensor(t_ob)
        t_ob_ms = ST.n

        def v_ob(v, t_ob_ms=t_ob_ms):
            if lvl < 12:
                return
            v.wait_ge(ST.h, t_ob_ms)
            ins = v.tensor_copy(OFFB[:, :], PB3[:, 0:8])
            SV.inc(ins)
        blk.vector(v_ob)

    z2_stores = SZ.n
    v_fin_prev = 0
    with nc.Block() as blk:
        for t in (range(nt) if lvl >= 13 else []):
            def v_kg(v, t=t):
                for s in (0, 1):
                    st_i = 2 * t + s
                    ins = v.tensor_tensor(SCR8[:, :],
                                          MHOT[:, 8 * st_i:8 * (st_i + 1)],
                                          OFFB[:, :], ALU.mult)
                    v.wait_ge(SV.h, SV.inc(ins))
                    ins = v.reduce_sum(OFFC[:, :], SCR8[:, :], axis=AX.X)
                    v.wait_ge(SV.h, SV.inc(ins))
                    ins = v.tensor_tensor(OFFC[:, :], OFFC[:, :],
                                          RANK[:, st_i:st_i + 1], ALU.add)
                    v.wait_ge(SV.h, SV.inc(ins))
                    ins = v.tensor_scalar(KGC[:, s:s + 1], OFFC[:, :], CAPF,
                                          None, ALU.is_lt)
                    v.wait_ge(SV.h, SV.inc(ins))
                    ins = v.tensor_tensor(KGC[:, s:s + 1], KGC[:, s:s + 1],
                                          GATE[:, st_i:st_i + 1], ALU.mult)
                SV.inc(ins)
            blk.vector(v_kg)
            v_kg_ms = SV.n

            def d_z2l(sy, t=t, vprev=v_fin_prev):
                if vprev:
                    sy.wait_ge(SV.h, vprev)
                sy.wait_ge(SZ.h, z2_stores)
                SZ.inc(sy.dma_start(out=NRM_TM[:, :], in_=z2d.ap()[t]), 16)
            blk.sync(d_z2l)
            z2l_ms = SZ.n

            for s in (0, 1):
                def v_fin(v, t=t, s=s, z2l_ms=z2l_ms):
                    half = 2 * t + s
                    if half > 0:
                        v.wait_ge(SO.h, 16 * half)
                    if s == 0:
                        v.wait_ge(SZ.h, z2l_ms)
                    st_i = 2 * t + s
                    for oh in (0, 1):
                        ins = v.scalar_tensor_tensor(
                            OUTS[:, 512 * oh:512 * (oh + 1)],
                            NRM_TM[:, 1024 * s + 512 * oh:
                                   1024 * s + 512 * (oh + 1)],
                            KGC[:, s:s + 1],
                            YB[:, 1024 * st_i + 512 * oh:
                               1024 * st_i + 512 * (oh + 1)],
                            ALU.mult, ALU.add)
                    SV.inc(ins)
                blk.vector(v_fin)
                v_fin_ms = SV.n

                def d_out(sy, t=t, s=s, v_fin_ms=v_fin_ms):
                    sy.wait_ge(SV.h, v_fin_ms)
                    SO.inc(sy.dma_start(
                        out=y_out[TT * t + 128 * s:TT * t + 128 * (s + 1), :],
                        in_=OUTS[:, :]), 16)
                blk.sync(d_out)
            v_fin_prev = v_fin_ms

    with nc.Block() as blk:
        @blk.sync
        def _(sy):
            if SO.n:
                sy.wait_ge(SO.h, SO.n)
            sy.wait_ge(SD.h, SD.n)
            sy.wait_ge(ST.h, ST.n)
            sy.wait_ge(SV.h, SV.n)
            sy.wait_ge(SS.h, SS.n)

    if dump:
        with nc.Block() as blk:
            def g_dump(g):
                r0 = 0
                for name in dump:
                    if ":" in name:
                        name, cstr = name.split(":")
                        hnd = dumpd[name]
                        cols = int(cstr)
                    else:
                        hnd = dumpd[name]
                        cols = hnd.shape[1] if hasattr(hnd, 'shape') else None
                    nb = max(1, (cols + 1023) // 1024)
                    for b in range(nb):
                        c0 = 1024 * b
                        c1 = min(cols, c0 + 1024)
                        p = hnd.shape[0]
                        SO.inc(g.dma_start(
                            out=y_out[r0:r0 + p, 0:(c1 - c0)],
                            in_=hnd[0:p, c0:c1]), 16)
                        r0 += 128
                g.wait_ge(SO.h, SO.n)
            blk.gpsimd(g_dump)

    return nc


def _consts():
    idt = np.eye(128, dtype=BFNP)
    tri04 = np.triu(np.ones((128, 128), np.float32), k=0).astype(BFNP)
    tri1f = np.triu(np.ones((128, 128), np.float32), k=1).astype(BFNP)
    ocb = np.ones((128, 1), BFNP)
    orb = np.ones((65, 128), BFNP)
    orf = np.ones((1, 128), np.float32)
    cvv = np.zeros((128, 4), np.float32)
    cvv[:, 0] = 1e-5
    cvv[:, 1] = 1.0 / 1024.0
    cvv[:, 2] = -0.5
    return idt, tri04, tri1f, ocb, orb, orf, cvv


def kernel(L0_attn_x, L0_attn_Wq, L0_attn_Wk, L0_attn_Wv, L0_attn_Wo,
           L0_attn_bo, L0_ffn_Wg, L0_ffn_We, L0_ffn_be):
    x = np.ascontiguousarray(np.asarray(L0_attn_x, np.float32)).reshape(NCORES, TOK, D)
    wq = np.ascontiguousarray(np.asarray(L0_attn_Wq, np.float32))
    wk = np.ascontiguousarray(np.asarray(L0_attn_Wk, np.float32))
    wv = np.ascontiguousarray(np.asarray(L0_attn_Wv, np.float32))
    wo = np.ascontiguousarray(np.asarray(L0_attn_Wo, np.float32))
    bo = np.ascontiguousarray(np.asarray(L0_attn_bo, np.float32))
    wg = np.ascontiguousarray(np.asarray(L0_ffn_Wg, np.float32))
    we = np.ascontiguousarray(np.asarray(L0_ffn_We, np.float32))
    be = np.ascontiguousarray(np.asarray(L0_ffn_be, np.float32))
    idt, tri04, tri1f, ocb, orb, orf, cvv = _consts()

    nc = build_program()
    in_maps = []
    for c in range(NCORES):
        prevc = (np.arange(8) < c).astype(np.float32)[:, None].copy()
        in_maps.append({
            "x": x[c], "wq": wq, "wk": wk, "wv": wv, "wo": wo, "we": we,
            "wg": wg, "bo": bo, "be": be, "idt": idt, "triu04": tri04,
            "triu1f": tri1f, "ocb": ocb, "orb": orb, "orf": orf,
            "prevc": prevc, "cvv": cvv,
        })
    res = run_bass_kernel_spmd(nc, in_maps, core_ids=list(range(NCORES)))
    out = np.stack([np.asarray(res.results[c]["y"]) for c in range(NCORES)])
    return out.reshape(16, 2048, 1024)


# version 11
# speedup vs baseline: 1.0934x; 1.0241x over previous
import numpy as np
import ml_dtypes
from contextlib import ExitStack

import concourse.bass as bass
import concourse.mybir as mybir
from concourse.bass_utils import run_bass_kernel_spmd

F32 = mybir.dt.float32
BF16 = mybir.dt.bfloat16
AF = mybir.ActivationFunctionType
ALU = mybir.AluOpType
AX = mybir.AxisListType
BFNP = ml_dtypes.bfloat16

NCORES = 8
D = 1024
TOK = 4096
TT = 256
NT = TOK // TT
NST = 2 * NT
NH = 16
DH = 64
EPS_ATT = 1e-3
CAPF = 4096.0
BATCH_TILES = 8


class Ctr:
    def __init__(self, es, nc, name):
        self.h = es.enter_context(nc.semaphore(name))
        self.n = 0

    def inc(self, ins, k=1):
        ins.then_inc(self.h, k)
        self.n += k
        return self.n


class Bank:

    def __init__(self):
        self.pend = []

    def use(self, eng):
        for h, v in self.pend:
            eng.wait_ge(h, v)
        self.pend = []

    def consumed(self, ctr):
        self.pend.append((ctr.h, ctr.n))


def build_program(nt=NT, lvl=99, dump=None):
    nc = bass.Bass("TRN2", target_bir_lowering=False)

    x_in = nc.dram_tensor("x", [TOK, D], F32, kind="ExternalInput").ap()
    wq_in = nc.dram_tensor("wq", [D, D], F32, kind="ExternalInput").ap()
    wk_in = nc.dram_tensor("wk", [D, D], F32, kind="ExternalInput").ap()
    wv_in = nc.dram_tensor("wv", [D, D], F32, kind="ExternalInput").ap()
    wo_in = nc.dram_tensor("wo", [D, D], F32, kind="ExternalInput").ap()
    we_in = nc.dram_tensor("we", [D, D], F32, kind="ExternalInput").ap()
    wg_in = nc.dram_tensor("wg", [D, 8], F32, kind="ExternalInput").ap()
    bo_in = nc.dram_tensor("bo", [D], F32, kind="ExternalInput").ap()
    be_in = nc.dram_tensor("be", [D], F32, kind="ExternalInput").ap()
    idt_in = nc.dram_tensor("idt", [128, 128], BF16, kind="ExternalInput").ap()
    tri0_in = nc.dram_tensor("triu04", [128, 128], BF16, kind="ExternalInput").ap()
    tri1_in = nc.dram_tensor("triu1f", [128, 128], BF16, kind="ExternalInput").ap()
    ocb_in = nc.dram_tensor("ocb", [128, 1], BF16, kind="ExternalInput").ap()
    orb_in = nc.dram_tensor("orb", [65, 128], BF16, kind="ExternalInput").ap()
    orf_in = nc.dram_tensor("orf", [1, 128], F32, kind="ExternalInput").ap()
    prev_in = nc.dram_tensor("prevc", [8, 1], F32, kind="ExternalInput").ap()
    cvv_in = nc.dram_tensor("cvv", [128, 4], F32, kind="ExternalInput").ap()
    y_out = nc.dram_tensor("y", [TOK, D], F32, kind="ExternalOutput").ap()

    z2d = nc.dram_tensor("z2d", [NT, 128, 2048], BF16)
    cc_in = nc.dram_tensor("cc_in", [1, 8], F32)
    cc_out = nc.dram_tensor("cc_out", [8, 8], F32, addr_space="Shared")

    es = ExitStack()

    def sb(name, shape, dtype):
        return es.enter_context(nc.sbuf_tensor(name, shape, dtype))

    def ps(name, shape, dtype):
        return es.enter_context(nc.psum_tensor(name, shape, dtype))

    XT = sb("XT", [128, 2048], F32)
    OUTS = sb("OUTS", [128, 1024], F32)
    NRM_TM = sb("NRM_TM", [128, 2048], BF16)
    NRM_FM = sb("NRM_FM", [128, 2048], BF16)
    QF = sb("QF", [128, 2048], BF16)
    KF = sb("KF", [128, 2048], BF16)
    KT = sb("KT", [128, 2048], BF16)
    VT = sb("VT", [128, 2080], BF16)
    SM = sb("SM", [128, 2048], BF16)
    AT = sb("AT", [128, 2048], BF16)
    AFULL = sb("AFULL", [128, 2048], BF16)
    Z2B = sb("Z2B", [128, 2048], BF16)
    YB = sb("YB", [128, NST * 1024], BF16)
    WQb = sb("WQb", [128, 8192], BF16)
    WKb = sb("WKb", [128, 8192], BF16)
    WVb = sb("WVb", [128, 8192], BF16)
    WOb = sb("WOb", [128, 8192], BF16)
    WEb = sb("WEb", [128, 8192], BF16)
    WGb = sb("WGb", [128, 64], BF16)
    BOBE = sb("BOBE", [65, 1024], BF16)
    IDT = sb("IDT", [128, 128], BF16)
    TRI0 = sb("TRI0", [128, 128], BF16)
    TRI1F = sb("TRI1F", [128, 128], BF16)
    OCB = sb("OCB", [128, 1], BF16)
    ORB = sb("ORB", [65, 128], BF16)
    ORF = sb("ORF", [1, 128], F32)
    PREVC = sb("PREVC", [8, 1], F32)
    CVV = sb("CVV", [128, 4], F32)
    SG = sb("SGt", [8, 8], F32)
    SOFF = sb("SOFF", [1, 8], F32)
    OFFB = sb("OFFB", [128, 8], F32)
    STE = sb("STE", [128, 8 * 65], F32)
    STO = sb("STO", [128, 8 * 65], F32)
    STEB = sb("STEB", [128, 8 * 65], BF16)
    STOB = sb("STOB", [128, 8 * 65], BF16)
    CNT = sb("CNT", [1, 8], F32)
    MHOT = sb("MHOT", [128, NST * 8], BF16)
    GATE = sb("GATE", [128, NST], F32)
    RANK = sb("RANK", [128, NST], F32)
    M2C = sb("M2C", [128, NST], F32)
    R2C = sb("R2C", [128, NST], F32)
    DINV = sb("DINV", [128, 16], F32)
    SCR8 = sb("SCR8", [128, 8], F32)
    SEXPC = sb("SEXPC", [128, 2], F32)
    NEGM = sb("NEGM", [128, 2], F32)
    SUMT = sb("SUMT", [128, 2], F32)
    SQT = sb("SQT", [128, 2], F32)
    MT = sb("MT", [128, 2], F32)
    T1C = sb("T1C", [128, 2], F32)
    T2C = sb("T2C", [128, 2], F32)
    RSTD = sb("RSTD", [128, 2], F32)
    KGC = sb("KGC", [128, 2], F32)
    OFFC = sb("OFFC", [128, 1], F32)

    dumpd = dict(XT=XT, NRM_TM=NRM_TM, NRM_FM=NRM_FM, QF=QF, KF=KF, KT=KT,
                 VT=VT, SM=SM, AT=AT, AFULL=AFULL, YB=YB, MHOT=MHOT,
                 GATE=GATE, RANK=RANK, M2C=M2C, R2C=R2C, STE=STE, STO=STO,
                 CNT=CNT, OFFB=OFFB, DINV=DINV, SGt=SG, KGC=KGC, WQb=WQb,
                 SUMT=SUMT, SQT=SQT, MT=MT, RSTD=RSTD, T2C=T2C, Z2B=Z2B, OUTS=OUTS)

    PB0 = ps("PB0", [128, 1024], BF16)
    PB1 = ps("PB1", [128, 1024], BF16)
    PB2 = ps("PB2", [128, 512], F32)
    PB3 = ps("PB3", [128, 512], F32)
    PB4 = ps("PB4", [128, 512], F32)
    PB5 = ps("PB5", [128, 512], F32)
    PB6 = ps("PB6", [128, 512], F32)
    PB7 = ps("PB7", [128, 512], F32)

    SD = Ctr(es, nc, "SD")
    SDC = Ctr(es, nc, "SDCc")
    SZ = Ctr(es, nc, "SZc")
    SO = Ctr(es, nc, "SOc")
    SC = Ctr(es, nc, "SCc")
    SGM = Ctr(es, nc, "SGMc")
    ST = Ctr(es, nc, "STc")
    SV = Ctr(es, nc, "SVc")
    SS = Ctr(es, nc, "SSc")

    bk = {name: Bank() for name in ["pb0", "pb1", "pb2", "pb3", "pb4", "pb5", "pb6", "pb7"]}
    zbanks = [PB4, PB5, PB6, PB7]
    zbknames = ["pb4", "pb5", "pb6", "pb7"]

    blk_main = nc.Block()
    blk_main.__enter__()
    blk = blk_main
    if True:
        @blk.sync
        def _(sy):
            for dst, src in [(IDT, idt_in), (TRI0, tri0_in), (TRI1F, tri1_in),
                             (OCB, ocb_in), (ORB, orb_in),
                             (ORF, orf_in), (PREVC, prev_in), (CVV, cvv_in)]:
                SDC.inc(sy.dma_start(out=dst[:, :], in_=src[:, :]), 16)

        @blk.scalar
        def _(se):
            from concourse.hw_specs import get_activation_tables
            set_id = list(get_activation_tables(nc.m.arch)).index(
                "natural_log_exp_and_others")
            se.add_instruction(mybir.InstLoadActFuncSet(
                name=nc.get_next_instruction_name(),
                act_func_set_id=set_id, ins=[], outs=[]))

        @blk.gpsimd
        def _(g):
            g.memset(STE[:, :], 0.0)
            g.memset(VT[:, :].rearrange("p (h u) -> p h u", u=65)[:, :, 64:65],
                     1.0)
            g.memset(STO[:, :], 0.0)
            g.memset(STEB[:, :], 0.0)
            g.memset(STOB[:, :], 0.0)
            ins = g.memset(CNT[:, :], 0.0)
            SGM.inc(ins)

    def emit_weight_loads():
        def g_w(g):
            for dst, src in [(WQb, wq_in), (WKb, wk_in), (WVb, wv_in),
                             (WOb, wo_in), (WEb, we_in)]:
                SD.inc(g.dma_start(
                    out=dst[:, :].rearrange("p (c o) -> p c o", c=8),
                    in_=src.rearrange("(c p) o -> p c o", p=128)), 16)
            SD.inc(g.dma_start(
                out=WGb[:, :].rearrange("p (c e) -> p c e", c=8),
                in_=wg_in.rearrange("(c p) e -> p c e", p=128)), 16)
            SD.inc(g.dma_start(out=BOBE[0:1, 0:1024],
                               in_=bo_in.rearrange("(u o) -> u o", u=1)), 16)
            SD.inc(g.dma_start(out=BOBE[64:65, 0:1024],
                               in_=be_in.rearrange("(u o) -> u o", u=1)), 16)
        blk.gpsimd(g_w)

    SETUP_SD = SD.n + 8 * 16
    SETUP_SDC = SDC.n

    v_y = 0
    v_st = 0
    v_ln2_prev = 0
    t_tr3_prev = 0
    proj_last_fill_ms = 0
    nrm_last_ms = 0
    v_cnt = 0
    first = {"t": True, "v": True, "s": True}

    v_y_hist = {}
    pre_vals = {}

    def emit_prelude(pt):
        XTW = XT
        vy_gate = v_y_hist.get(pt - 1, 0)

        def dseg(sy, pt=pt, vy_gate=vy_gate):
            if vy_gate:
                sy.wait_ge(SV.h, vy_gate)
            SDC.inc(sy.dma_start(
                out=XT[:, :].rearrange("p (s c) -> p s c", s=2),
                in_=x_in[TT * pt:TT * (pt + 1), :].rearrange(
                    "(s p) c -> p s c", p=128)), 16)
        blk.sync(dseg)
        dx = SDC.n

        def g_sq1(g, dx=dx, tprev=t_tr3_prev, XTW=XTW):
            g.wait_ge(SDC.h, dx)
            if tprev:
                g.wait_ge(ST.h, tprev)
            g.tensor_tensor(NRM_TM[:, 0:1024], XTW[:, 0:1024],
                            XTW[:, 0:1024], ALU.mult)
            ins = g.tensor_tensor(NRM_TM[:, 1024:2048], XTW[:, 1024:2048],
                                  XTW[:, 1024:2048], ALU.mult)
            SGM.inc(ins)
        blk.gpsimd(g_sq1)
        g_sq1m = SGM.n

        def v_st1(v, dx=dx, vprev=v_ln2_prev, g_sq1m=g_sq1m, XTW=XTW):
            v.wait_ge(SDC.h, dx)
            if vprev:
                v.wait_ge(SV.h, vprev)
            v.wait_ge(SGM.h, g_sq1m)
            v.reduce_sum(SUMT[:, 0:1], XTW[:, 0:1024], axis=AX.X)
            v.reduce_sum(SUMT[:, 1:2], XTW[:, 1024:2048], axis=AX.X)
            v.reduce_sum(SQT[:, 0:1], NRM_TM[:, 0:1024], axis=AX.X)
            ins = v.reduce_sum(SQT[:, 1:2], NRM_TM[:, 1024:2048], axis=AX.X)
            v.wait_ge(SV.h, SV.inc(ins))
            v.tensor_scalar(MT[:, 0:2], SUMT[:, 0:2],
                            1.0 / 1024.0, None, ALU.mult)
            ins = v.tensor_tensor(T1C[:, 0:2], SUMT[:, 0:2], SUMT[:, 0:2],
                                  ALU.mult)
            v.wait_ge(SV.h, SV.inc(ins))
            ins = v.scalar_tensor_tensor(T2C[:, 0:2], T1C[:, 0:2],
                                         -1.0 / 1024.0, SQT[:, 0:2],
                                         ALU.mult, ALU.add)
            SV.inc(ins)
        blk.vector(v_st1)
        v_ln1s = SV.n

        def s_rstd(se, v_ln1s=v_ln1s, fs=first["s"]):
            if fs:
                se.wait_ge(SDC.h, SETUP_SDC)
            se.wait_ge(SV.h, v_ln1s)
            ins = se.activation(RSTD[:, 0:2], T2C[:, 0:2], AF.Ln,
                                bias=CVV[:, 0:1], scale=CVV[:, 1:2])
            se.wait_ge(SS.h, SS.inc(ins))
            ins = se.activation(RSTD[:, 0:2], RSTD[:, 0:2],
                                AF.Exp, scale=CVV[:, 2:3])
            se.wait_ge(SS.h, SS.inc(ins))
            ins = se.copy(OUTS[:, 17:18], OUTS[:, 15:16])
            SS.inc(ins)
        blk.scalar(s_rstd)
        first["s"] = False
        s_rstd1 = SS.n

        def v_ap1(v, s_rstd1=s_rstd1, XTW=XTW):
            v.wait_ge(SS.h, s_rstd1)
            for s in (0, 1):
                ins = v.tensor_scalar(NRM_TM[:, 1024 * s:1024 * (s + 1)],
                                      XTW[:, 1024 * s:1024 * (s + 1)],
                                      MT[:, s:s + 1], RSTD[:, s:s + 1],
                                      ALU.subtract, ALU.mult)
            SV.inc(ins)
        blk.vector(v_ap1)
        pre_vals[pt] = SV.n

    if True:
        emit_prelude(0)
        emit_weight_loads()
        for t in range(nt):
            v_ln1 = pre_vals[t]
            if lvl < 2:
                if t + 1 < nt:
                    emit_prelude(t + 1)
                continue

            def t_tr1(te, v_ln1=v_ln1, ft=first["t"]):
                if ft:
                    te.wait_ge(SDC.h, SETUP_SDC)
                te.wait_ge(SV.h, v_ln1)
                for s in (0, 1):
                    pb = PB0 if s == 0 else PB1
                    bk["pb0" if s == 0 else "pb1"].use(te)
                    for ci in range(8):
                        ins = te.transpose(pb[:, 128 * ci:128 * (ci + 1)],
                                           NRM_TM[:, 1024 * s + 128 * ci:
                                                  1024 * s + 128 * (ci + 1)],
                                           IDT[:, :])
                    ST.inc(ins)
            blk.tensor(t_tr1)
            first["t"] = False
            t_tr1b = ST.n
            t_tr1a = ST.n - 1

            def v_fm0(v, t_tr1a=t_tr1a, pms=nrm_last_ms):
                v.wait_ge(ST.h, t_tr1a)
                if pms:
                    v.wait_ge(ST.h, pms)
                ins = v.tensor_copy(
                    NRM_FM[:, :].rearrange("p (c u) -> p c u", u=256)[:, :, 0:128],
                    PB0[:, :].rearrange("p (c u) -> p c u", u=128))
                SV.inc(ins)
            blk.vector(v_fm0)
            bk["pb0"].consumed(SV)
            v_fma = SV.n

            def s_fm1(v, t_tr1b=t_tr1b, pms=nrm_last_ms):
                v.wait_ge(ST.h, t_tr1b)
                if pms:
                    v.wait_ge(ST.h, pms)
                ins = v.tensor_copy(
                    NRM_FM[:, :].rearrange("p (c u) -> p c u", u=256)[:, :, 128:256],
                    PB1[:, :].rearrange("p (c u) -> p c u", u=128))
                SV.inc(ins)
            blk.vector(s_fm1)
            bk["pb1"].consumed(SV)
            s_fmb = SV.n
            if lvl < 3:
                continue

            fill_ms = []
            evac_ms = []
            for f in range(12):
                pidx = f // 4
                g = f % 4
                bank = PB2 if f % 2 == 0 else PB3
                bname = "pb2" if f % 2 == 0 else "pb3"
                wsrc = [WQb, WKb, WVb][pidx]

                def t_fill(te, f=f, g=g, bank=bank, bname=bname, wsrc=wsrc,
                           v_fma=v_fma, s_fmb=s_fmb, t=t):
                    if f == 0:
                        if t == 0:
                            te.wait_ge(SD.h, SETUP_SD)
                        te.wait_ge(SV.h, v_fma)
                        te.wait_ge(SV.h, s_fmb)
                    bk[bname].use(te)
                    for j in range(2):
                        oc = 2 * g + j
                        for ci in range(8):
                            ins = te.matmul(
                                bank[:, 256 * j:256 * (j + 1)],
                                wsrc[:, 1024 * ci + 128 * oc:
                                     1024 * ci + 128 * (oc + 1)],
                                NRM_FM[:, 256 * ci:256 * (ci + 1)],
                                start=(j == 0 and ci == 0),
                                stop=(j == 1 and ci == 7))
                    ST.inc(ins)
                blk.tensor(t_fill)
                fms = ST.n
                fill_ms.append(fms)

                dest = [QF, KF, SM][pidx]
                if pidx < 2:
                    def v_ev(v, fms=fms, bank=bank, dest=dest, g=g):
                        v.wait_ge(ST.h, fms)
                        ins = v.tensor_scalar(dest[:, 512 * g:512 * (g + 1)],
                                              bank[:, :], 0.0, EPS_ATT,
                                              ALU.max, ALU.add)
                        SV.inc(ins)
                    blk.vector(v_ev)
                    bk[bname].consumed(SV)
                    evac_ms.append(("v", SV.n))
                else:
                    def s_ev(v, fms=fms, bank=bank, dest=dest, g=g):
                        v.wait_ge(ST.h, fms)
                        ins = v.tensor_copy(dest[:, 512 * g:512 * (g + 1)],
                                            bank[:, :])
                        SV.inc(ins)
                    blk.vector(s_ev)
                    bk[bname].consumed(SV)
                    evac_ms.append(("v", SV.n))
            proj_last_fill_ms = fill_ms[-1]
            qf_sv = evac_ms[3][1]
            kf_sv = evac_ms[7][1]
            vf_ss = evac_ms[11][1]
            if lvl < 4:
                continue

            t2ms = []
            for which in range(4):
                src = KF if which < 2 else SM
                dst = KT if which < 2 else VT
                ss = which % 2
                pb = PB0 if ss == 0 else PB1
                bname = "pb0" if ss == 0 else "pb1"

                def t_t2(te, src=src, ss=ss, pb=pb, bname=bname, which=which,
                         kf_sv=kf_sv, vf_ss=vf_ss):
                    if which == 0:
                        te.wait_ge(SV.h, kf_sv)
                    if which == 2:
                        te.wait_ge(SV.h, vf_ss)
                    bk[bname].use(te)
                    for oc in range(8):
                        ins = te.transpose(pb[:, 128 * oc:128 * (oc + 1)],
                                           src[:, 256 * oc + 128 * ss:
                                               256 * oc + 128 * (ss + 1)],
                                           IDT[:, :])
                    ST.inc(ins)
                blk.tensor(t_t2)
                tms = ST.n
                t2ms.append(tms)

                if ss == 0:
                    def v_t2(v, tms=tms, pb=pb, dst=dst, which=which):
                        v.wait_ge(ST.h, tms)
                        if dst is VT:
                            out = VT[:, 0:1040].rearrange(
                                "p (h u) -> p h u", u=65)[:, :, 0:64]
                        else:
                            out = dst[:, 0:1024]
                        ins = v.tensor_copy(
                            out,
                            pb[:, :].rearrange("p (h u) -> p h u", u=64)
                            if dst is VT else pb[:, :])
                        SV.inc(ins)
                    blk.vector(v_t2)
                    bk[bname].consumed(SV)
                else:
                    def s_t2(v, tms=tms, pb=pb, dst=dst, which=which):
                        v.wait_ge(ST.h, tms)
                        if dst is VT:
                            out = VT[:, 1040:2080].rearrange(
                                "p (h u) -> p h u", u=65)[:, :, 0:64]
                        else:
                            out = dst[:, 1024:2048]
                        ins = v.tensor_copy(
                            out,
                            pb[:, :].rearrange("p (h u) -> p h u", u=64)
                            if dst is VT else pb[:, :])
                        SV.inc(ins)
                    blk.vector(s_t2)
                    bk[bname].consumed(SV)
                if which == 1:
                    kt_sv, kt_ss = SV.n, SV.n
                if which == 3:
                    vt_sv, vt_ss = SV.n, SV.n
            t2_last = t2ms[-1]
            if lvl < 5:
                continue

            if t == BATCH_TILES:
                def v_rst(v, tam=t_att_prev_batch):
                    v.wait_ge(ST.h, tam)
                    v.memset(STE[0:64, :], 0.0)
                    v.memset(STO[64:128, :], 0.0)
                    v.memset(STEB[0:64, :], 0.0)
                    ins = v.memset(STOB[64:128, :], 0.0)
                    SV.inc(ins)
                blk.vector(v_rst)
                v_st = SV.n

            t_att_ms = 0
            v_at = 0
            for ss in (0, 1):
                smv = []
                sc_groups = [(PB4, "pb4", [0, 2, 4, 6], False),
                             (PB5, "pb5", [8, 10, 12, 14], False),
                             (PB4, "pb4", [1, 3, 5, 7], True),
                             (PB5, "pb5", [9, 11, 13, 15], False)]
                for gi, (pb, bname, heads, sep) in enumerate(sc_groups):
                    def t_sc(te, gi=gi, pb=pb, bname=bname, heads=tuple(heads),
                             sep=sep, ss=ss, qf_sv=qf_sv, kf_sv=kf_sv):
                        if gi == 0 and ss == 0:
                            te.wait_ge(SV.h, qf_sv)
                            te.wait_ge(SV.h, kf_sv)
                        bk[bname].use(te)
                        if sep and t == 0 and ss == 0:
                            te.wait_ge(SGM.h, 1)
                        if sep:
                            te.matmul(pb[:, 0:1], IDT[:, :], OCB[:, :],
                                      start=False, stop=False,
                                      skip_group_check=True)
                        for i, h in enumerate(heads):
                            p0 = 64 * (h % 2)
                            c0 = 256 * (h // 2) + 128 * ss
                            ins = te.matmul(
                                pb[:, 128 * i:128 * (i + 1)],
                                KF[p0:p0 + 64, c0:c0 + 128],
                                QF[p0:p0 + 64, c0:c0 + 128],
                                start=(i == 0), stop=(i == 3))
                        ST.inc(ins)
                    blk.tensor(t_sc)
                    scms = ST.n
                    if lvl == 40:
                        bk[bname].pend = []
                        continue

                    def v_sm(v, scms=scms, pb=pb, gi=gi, ss=ss,
                             t2_last=t2_last):
                        v.wait_ge(ST.h, scms)
                        if gi == 0 and ss == 0:
                            v.wait_ge(ST.h, t2_last)
                        base = 1024 * (gi % 2)
                        half = 128 * (gi // 2)
                        ins = v.tensor_tensor(
                            SM[:, base:base + 1024].rearrange(
                                "p (g u) -> p g u", u=256)[:, :, half:half + 128],
                            pb[:, :].rearrange("p (g u) -> p g u", u=128),
                            bass.AP(TRI0, 0, [[128, 128], [0, 4], [1, 128]]),
                            ALU.mult)
                        SV.inc(ins)
                    blk.vector(v_sm)
                    bk[bname].consumed(SV)
                    smv.append(SV.n)

                if lvl in (40, 41):
                    continue
                def t_att(te, ss=ss, smv3=smv[3], v_st=v_st,
                          vt_sv=vt_sv, vt_ss=vt_ss):
                    te.wait_ge(SV.h, smv3)
                    if v_st > 0:
                        te.wait_ge(SV.h, v_st)
                    elif v_st < 0:
                        te.wait_ge(SGM.h, -v_st)
                    else:
                        te.wait_ge(SGM.h, 1)
                    if ss == 0:
                        te.wait_ge(SV.h, vt_sv)
                        te.wait_ge(SV.h, vt_ss)
                    bk["pb2"].use(te)
                    bk["pb6"].use(te)
                    bk["pb7"].use(te)
                    for h in range(NH):
                        if h < 7:
                            ob, oc0 = PB6, 65 * h
                        elif h < 14:
                            ob, oc0 = PB7, 65 * (h - 7)
                        else:
                            ob, oc0 = PB2, 65 * (h - 14)
                        g = h // 2
                        c0 = 256 * g + 128 * ss
                        qs = QF[:, c0:c0 + 128]
                        stb = STEB if h % 2 == 0 else STOB
                        sm = SM[:, 128 * h:128 * (h + 1)]
                        te.matmul(ob[:, oc0:oc0 + 65], sm,
                                  VT[:, 1040 * ss + 65 * h:1040 * ss + 65 * (h + 1)],
                                  start=(oc0 == 0), stop=False)
                        ins = te.matmul(ob[:, oc0:oc0 + 65], qs,
                                        stb[:, 65 * g:65 * g + 65],
                                        start=False,
                                        stop=(h in (6, 13, 15)))
                    ST.inc(ins)
                blk.tensor(t_att)
                t_att_ms = ST.n

                def v_dv(v, t_att_ms=t_att_ms, ss=ss):
                    v.wait_ge(ST.h, t_att_ms)
                    v.reciprocal(DINV[:, 0:7],
                                 PB6[:, 0:455].rearrange("p (h u) -> p h u",
                                                         u=65)[:, :, 64:65])
                    v.reciprocal(DINV[:, 7:14],
                                 PB7[:, 0:455].rearrange("p (h u) -> p h u",
                                                         u=65)[:, :, 64:65])
                    ins = v.reciprocal(DINV[:, 14:16],
                                       PB2[:, 0:130].rearrange(
                                           "p (h u) -> p h u",
                                           u=65)[:, 0:2, 64:65])
                    v.wait_ge(SV.h, SV.inc(ins))
                    for pb, h0, nh in ((PB6, 0, 7), (PB7, 7, 7), (PB2, 14, 2)):
                        ins = v.tensor_tensor(
                            AT[:, 1024 * ss + 64 * h0:
                               1024 * ss + 64 * (h0 + nh)].rearrange(
                                   "p (h e) -> p h e", e=64),
                            pb[:, 0:65 * nh].rearrange(
                                "p (h e) -> p h e", e=65)[:, :, 0:64],
                            bass.AP(DINV, h0, [[16, 128], [1, nh], [0, 64]]),
                            ALU.mult)
                    SV.inc(ins)
                blk.vector(v_dv)
                v_dinv = SV.n - 1
                v_at = SV.n
                bk["pb6"].consumed(SV)
                bk["pb7"].consumed(SV)
                bk["pb2"].consumed(SV)

                if lvl == 42:
                    continue
                def t_stu(te, ss=ss, v_dinv=v_dinv, v_at=v_at,
                          kt_sv=kt_sv, kt_ss=kt_ss):
                    te.wait_ge(SV.h, v_dinv)
                    te.wait_ge(SV.h, v_at)
                    if ss == 0:
                        te.wait_ge(SV.h, kt_sv)
                        te.wait_ge(SV.h, kt_ss)
                    bk["pb3"].use(te)
                    bk["pb6"].use(te)
                    for g in range(8):
                        ktp = KT[:, 1024 * ss + 128 * g:1024 * ss + 128 * (g + 1)]
                        ve = VT[:, 1040 * ss + 130 * g:1040 * ss + 130 * g + 64]
                        vo = VT[:, 1040 * ss + 130 * g + 65:
                                1040 * ss + 130 * g + 129]
                        te.matmul(PB3[:, 64 * g:64 * (g + 1)], ktp, ve,
                                  start=(g == 0), stop=False)
                        te.matmul(PB6[:, 64 * g:64 * (g + 1)], ktp, vo,
                                  start=(g == 0), stop=False)
                        ins = te.matmul(PB2[:, 256 + g:257 + g], ktp, OCB[:, :],
                                        start=False, stop=(g == 7))
                    ST.inc(ins)
                blk.tensor(t_stu)
                t_stu_ms = ST.n

                def v_sta(v, t_stu_ms=t_stu_ms, t=t, ss=ss):
                    if t == 0 and ss == 0:
                        v.wait_ge(SGM.h, 1)
                    v.wait_ge(ST.h, t_stu_ms)
                    ste = STE[:, :].rearrange("p (g u) -> p g u", u=65)
                    sto = STO[:, :].rearrange("p (g u) -> p g u", u=65)
                    v.tensor_tensor(ste[0:64, :, 0:64], ste[0:64, :, 0:64],
                                    PB3[0:64, :].rearrange("p (g u) -> p g u", u=64),
                                    ALU.add)
                    v.tensor_tensor(sto[64:128, :, 0:64], sto[64:128, :, 0:64],
                                    PB6[64:128, :].rearrange("p (g u) -> p g u", u=64),
                                    ALU.add)
                    v.tensor_tensor(ste[0:64, :, 64:65], ste[0:64, :, 64:65],
                                    PB2[0:64, 256:264].rearrange(
                                        "p (g u) -> p g u", u=1),
                                    ALU.add)
                    ins = v.tensor_tensor(sto[64:128, :, 64:65],
                                          sto[64:128, :, 64:65],
                                          PB2[64:128, 256:264].rearrange(
                                              "p (g u) -> p g u", u=1),
                                          ALU.add)
                    ins2 = ins
                    SV.inc(ins2)
                blk.vector(v_sta)
                v_sta_ms = SV.n

                def g_stc(g, v_sta_ms=v_sta_ms):
                    g.wait_ge(SV.h, v_sta_ms)
                    g.tensor_copy(STEB[0:64, :], STE[0:64, :])
                    ins = g.tensor_copy(STOB[64:128, :], STO[64:128, :])
                    g.wait_ge(SGM.h, SGM.inc(ins))
                    ins = g.tensor_copy(SCR8[0:1, 0:2], STE[0:1, 0:2])
                    SGM.inc(ins)
                blk.gpsimd(g_stc)
                v_st = -SGM.n
                bk["pb2"].consumed(SV)
                bk["pb3"].consumed(SV)
                bk["pb6"].consumed(SV)
            t_att_prev_batch = t_att_ms
            if lvl < 6:
                continue

            at3ms = []
            for ss in (0, 1):
                pb = PB0 if ss == 0 else PB1
                bname = "pb0" if ss == 0 else "pb1"

                def t_at3(te, ss=ss, pb=pb, bname=bname, v_at=v_at):
                    if ss == 0 and v_at:
                        te.wait_ge(SV.h, v_at)
                    bk[bname].use(te)
                    for ec in range(8):
                        ins = te.transpose(pb[:, 128 * ec:128 * (ec + 1)],
                                           AT[:, 1024 * ss + 128 * ec:
                                              1024 * ss + 128 * (ec + 1)],
                                           IDT[:, :])
                    ST.inc(ins)
                blk.tensor(t_at3)
                tms = ST.n
                at3ms.append(tms)

                if ss == 0:
                    def v_at3(v, tms=tms, pb=pb):
                        v.wait_ge(ST.h, tms)
                        ins = v.tensor_copy(
                            AFULL[:, :].rearrange("p (c u) -> p c u", u=256)[:, :, 0:128],
                            pb[:, :].rearrange("p (c u) -> p c u", u=128))
                        SV.inc(ins)
                    blk.vector(v_at3)
                    bk["pb0"].consumed(SV)
                    af_sv = SV.n
                else:
                    def s_at3(v, tms=tms, pb=pb):
                        v.wait_ge(ST.h, tms)
                        ins = v.tensor_copy(
                            AFULL[:, :].rearrange("p (c u) -> p c u", u=256)[:, :, 128:256],
                            pb[:, :].rearrange("p (c u) -> p c u", u=128))
                        SV.inc(ins)
                    blk.vector(s_at3)
                    bk["pb1"].consumed(SV)
                    af_ss = SV.n
            if lvl < 7:
                continue

            z_ms = []
            for k, (s, oh) in enumerate([(0, 0), (0, 1), (1, 0), (1, 1)]):
                bankz = zbanks[k]
                bnm = zbknames[k]

                def t_z(te, k=k, s=s, oh=oh, bankz=bankz, bnm=bnm,
                        af_sv=af_sv, af_ss=af_ss):
                    if k == 0:
                        te.wait_ge(SV.h, af_sv)
                        te.wait_ge(SV.h, af_ss)
                    bk[bnm].use(te)
                    for e in range(8):
                        te.matmul(bankz[:, :],
                                  AFULL[:, 256 * e + 128 * s:256 * e + 128 * (s + 1)],
                                  WOb[:, 1024 * e + 512 * oh:1024 * e + 512 * (oh + 1)],
                                  start=(e == 0), stop=False)
                    ins = te.matmul(bankz[:, :], ORB[0:1, :],
                                    BOBE[0:1, 512 * oh:512 * (oh + 1)],
                                    start=False, stop=True)
                    ST.inc(ins)
                blk.tensor(t_z)
                z_ms.append(ST.n)

            def v_yev(v, t=t, z_ms=tuple(z_ms)):
                for k, (s, oh) in enumerate([(0, 0), (0, 1), (1, 0), (1, 1)]):
                    v.wait_ge(ST.h, z_ms[k])
                    st_i = 2 * t + s
                    ins = v.scalar_tensor_tensor(
                        YB[:, 1024 * st_i + 512 * oh:1024 * st_i + 512 * (oh + 1)],
                        zbanks[k][:, :], 1.0,
                        XT[:, 1024 * s + 512 * oh:1024 * s + 512 * (oh + 1)],
                        ALU.mult, ALU.add)
                SV.inc(ins)
            blk.vector(v_yev)
            v_y = SV.n
            v_y_hist[t] = v_y
            for bnm in zbknames:
                bk[bnm].consumed(SV)
            if lvl < 8:
                if t + 1 < nt:
                    emit_prelude(t + 1)
                continue

            def g_sq2(g, t=t, v_y=v_y):
                g.wait_ge(SV.h, v_y)
                yb0 = YB[:, 2048 * t:2048 * t + 1024]
                yb1 = YB[:, 2048 * t + 1024:2048 * t + 2048]
                g.tensor_tensor(NRM_TM[:, 0:1024], yb0, yb0, ALU.mult)
                ins = g.tensor_tensor(NRM_TM[:, 1024:2048], yb1, yb1, ALU.mult)
                SGM.inc(ins)
            blk.gpsimd(g_sq2)
            g_sq2m = SGM.n

            def v_st2(v, t=t, v_y=v_y, g_sq2m=g_sq2m):
                v.wait_ge(SV.h, v_y)
                v.wait_ge(SGM.h, g_sq2m)
                yb0 = YB[:, 2048 * t:2048 * t + 1024]
                yb1 = YB[:, 2048 * t + 1024:2048 * t + 2048]
                v.reduce_sum(SUMT[:, 0:1], yb0, axis=AX.X)
                v.reduce_sum(SUMT[:, 1:2], yb1, axis=AX.X)
                v.reduce_sum(SQT[:, 0:1], NRM_TM[:, 0:1024], axis=AX.X)
                ins = v.reduce_sum(SQT[:, 1:2], NRM_TM[:, 1024:2048], axis=AX.X)
                v.wait_ge(SV.h, SV.inc(ins))
                v.tensor_scalar(M2C[:, 2 * t:2 * t + 2], SUMT[:, 0:2],
                                1.0 / 1024.0, None, ALU.mult)
                ins = v.tensor_tensor(T1C[:, 0:2], SUMT[:, 0:2], SUMT[:, 0:2],
                                      ALU.mult)
                v.wait_ge(SV.h, SV.inc(ins))
                ins = v.scalar_tensor_tensor(T2C[:, 0:2], T1C[:, 0:2],
                                             -1.0 / 1024.0, SQT[:, 0:2],
                                             ALU.mult, ALU.add)
                SV.inc(ins)
            blk.vector(v_st2)
            v_ln2s = SV.n

            def s_rstd2(se, t=t, v_ln2s=v_ln2s):
                se.wait_ge(SV.h, v_ln2s)
                ins = se.activation(R2C[:, 2 * t:2 * t + 2], T2C[:, 0:2], AF.Ln,
                                    bias=CVV[:, 0:1], scale=CVV[:, 1:2])
                se.wait_ge(SS.h, SS.inc(ins))
                ins = se.activation(R2C[:, 2 * t:2 * t + 2],
                                    R2C[:, 2 * t:2 * t + 2],
                                    AF.Exp, scale=CVV[:, 2:3])
                se.wait_ge(SS.h, SS.inc(ins))
                ins = se.copy(OUTS[:, 18:19], OUTS[:, 15:16])
                SS.inc(ins)
            blk.scalar(s_rstd2)
            s_rstd2m = SS.n

            def v_ap2(v, t=t, s_rstd2m=s_rstd2m, t_tr1b=t_tr1b):
                v.wait_ge(SS.h, s_rstd2m)
                v.wait_ge(ST.h, t_tr1b)
                for s in (0, 1):
                    st_i = 2 * t + s
                    ins = v.tensor_scalar(NRM_TM[:, 1024 * s:1024 * (s + 1)],
                                          YB[:, 1024 * st_i:1024 * (st_i + 1)],
                                          M2C[:, st_i:st_i + 1],
                                          R2C[:, st_i:st_i + 1],
                                          ALU.subtract, ALU.mult)
                SV.inc(ins)
            blk.vector(v_ap2)
            v_ln2 = SV.n
            v_ln2_prev = v_ln2
            if lvl < 9:
                if t + 1 < nt:
                    emit_prelude(t + 1)
                continue

            for s in (0, 1):
                pb = PB0 if s == 0 else PB1
                bname = "pb0" if s == 0 else "pb1"

                def t_tr3(te, s=s, pb=pb, bname=bname, v_ln2=v_ln2):
                    if s == 0:
                        te.wait_ge(SV.h, v_ln2)
                    bk[bname].use(te)
                    for ci in range(8):
                        ins = te.transpose(pb[:, 128 * ci:128 * (ci + 1)],
                                           NRM_TM[:, 1024 * s + 128 * ci:
                                                  1024 * s + 128 * (ci + 1)],
                                           IDT[:, :])
                    ST.inc(ins)
                blk.tensor(t_tr3)
                tms = ST.n

                if s == 0:
                    def v_tr3(v, tms=tms, pb=pb, plf=proj_last_fill_ms):
                        v.wait_ge(ST.h, tms)
                        v.wait_ge(ST.h, plf)
                        ins = v.tensor_copy(
                            NRM_FM[:, :].rearrange("p (c u) -> p c u", u=256)[:, :, 0:128],
                            pb[:, :].rearrange("p (c u) -> p c u", u=128))
                        SV.inc(ins)
                    blk.vector(v_tr3)
                    bk["pb0"].consumed(SV)
                    fm3_sv = SV.n
                else:
                    def s_tr3(v, tms=tms, pb=pb, plf=proj_last_fill_ms):
                        v.wait_ge(ST.h, tms)
                        v.wait_ge(ST.h, plf)
                        ins = v.tensor_copy(
                            NRM_FM[:, :].rearrange("p (c u) -> p c u", u=256)[:, :, 128:256],
                            pb[:, :].rearrange("p (c u) -> p c u", u=128))
                        SV.inc(ins)
                    blk.vector(s_tr3)
                    bk["pb1"].consumed(SV)
                    fm3_ss = SV.n
            t_tr3_prev = ST.n
            if lvl < 10:
                if t + 1 < nt:
                    emit_prelude(t + 1)
                continue

            rt_ms = {}
            for s in (0, 1):
                bankr = PB2 if s == 0 else PB3
                bnm = "pb2" if s == 0 else "pb3"

                def t_rt(te, s=s, bankr=bankr, bnm=bnm,
                         fm3_sv=fm3_sv, fm3_ss=fm3_ss):
                    if s == 0:
                        te.wait_ge(SV.h, fm3_sv)
                        te.wait_ge(SV.h, fm3_ss)
                    bk[bnm].use(te)
                    for ci in range(8):
                        ins = te.matmul(bankr[:, 0:8],
                                        NRM_FM[:, 256 * ci + 128 * s:
                                               256 * ci + 128 * (s + 1)],
                                        WGb[:, 8 * ci:8 * (ci + 1)],
                                        start=(ci == 0), stop=(ci == 7))
                    ST.inc(ins)
                blk.tensor(t_rt)
                rt_ms[s] = ST.n

            we_ms = []
            for k, (s2, oh) in enumerate([(0, 0), (0, 1), (1, 0), (1, 1)]):
                bankz = zbanks[k]
                bnm2 = zbknames[k]

                def t_we(te, k=k, s2=s2, oh=oh, bankz=bankz, bnm2=bnm2):
                    bk[bnm2].use(te)
                    for e in range(8):
                        te.matmul(bankz[:, :],
                                  NRM_FM[:, 256 * e + 128 * s2:
                                         256 * e + 128 * (s2 + 1)],
                                  WEb[:, 1024 * e + 512 * oh:
                                      1024 * e + 512 * (oh + 1)],
                                  start=(e == 0), stop=False)
                    ins = te.matmul(bankz[:, :], ORB[64:65, :],
                                    BOBE[64:65, 512 * oh:512 * (oh + 1)],
                                    start=False, stop=True)
                    ST.inc(ins)
                blk.tensor(t_we)
                we_ms.append(ST.n)
            nrm_last_ms = we_ms[-1]

            we_ms_t = tuple(we_ms)

            if t + 1 < nt:
                emit_prelude(t + 1)

            rte_v = {}
            rte_s = {}

            def v_mx(v, rtm=rt_ms[1]):
                v.wait_ge(ST.h, rtm)
                v.reduce_max(NEGM[:, 0:1], PB2[:, 0:8], axis=AX.X, negate=True)
                ins = v.reduce_max(NEGM[:, 1:2], PB3[:, 0:8], axis=AX.X,
                                   negate=True)
                v.wait_ge(SV.h, SV.inc(ins))
                ins = v.tensor_copy(NEGM[:, 0:2], NEGM[:, 0:2])
                SV.inc(ins)
            blk.vector(v_mx)
            v_negm = SV.n

            def s_ex(se, v_negm=v_negm):
                se.wait_ge(SV.h, v_negm)
                se.activation(OUTS[:, 0:8], PB2[:, 0:8], AF.Exp,
                              bias=NEGM[:, 0:1])
                ins = se.activation(OUTS[:, 8:16], PB3[:, 0:8], AF.Exp,
                                    bias=NEGM[:, 1:2])
                se.wait_ge(SS.h, SS.inc(ins))
                ins = se.copy(OUTS[:, 16:17], OUTS[:, 15:16])
                SS.inc(ins)
            blk.scalar(s_ex)
            rte_s[0] = rte_s[1] = SS.n

            def v_mh(v, t=t, sx=rte_s[1]):
                v.wait_ge(SS.h, sx)
                v.tensor_scalar(MHOT[:, 8 * (2 * t):8 * (2 * t + 1)],
                                PB2[:, 0:8], NEGM[:, 0:1], 0.0,
                                ALU.add, ALU.is_ge)
                v.tensor_scalar(MHOT[:, 8 * (2 * t + 1):8 * (2 * t + 2)],
                                PB3[:, 0:8], NEGM[:, 1:2], 0.0,
                                ALU.add, ALU.is_ge)
                v.reduce_sum(SEXPC[:, 0:1], OUTS[:, 0:8], axis=AX.X)
                ins = v.reduce_sum(SEXPC[:, 1:2], OUTS[:, 8:16], axis=AX.X)
                v.wait_ge(SV.h, SV.inc(ins))
                ins = v.reciprocal(GATE[:, 2 * t:2 * t + 2], SEXPC[:, 0:2])
                SV.inc(ins)
            blk.vector(v_mh)
            rte_v[0] = rte_v[1] = SV.n
            if lvl < 11:
                continue

            for s in (0, 1):
                bankr = PB2 if s == 0 else PB3
                bnm = "pb2" if s == 0 else "pb3"
                bk[bnm].consumed(SV)
                bk[bnm].pend.append((SS.h, rte_s[s]))

                def t_rk(te, s=s, t=t, bankr=bankr, bnm=bnm,
                         vr=rte_v[s], sxm=rte_s[s], v_cnt=v_cnt):
                    st_i = 2 * t + s
                    te.wait_ge(SV.h, vr)
                    te.wait_ge(SS.h, sxm)
                    if v_cnt:
                        te.wait_ge(SV.h, v_cnt)
                    else:
                        te.wait_ge(SGM.h, 1)
                    te.matmul(bankr[:, 8:16], TRI1F[:, :],
                              MHOT[:, 8 * st_i:8 * (st_i + 1)],
                              start=False, stop=False)
                    te.matmul(bankr[:, 8:16], ORF[0:1, :], CNT[0:1, :],
                              start=False, stop=False)
                    ins = te.matmul(bankr[0:1, 16:24], OCB[:, :],
                                    MHOT[:, 8 * st_i:8 * (st_i + 1)],
                                    start=False, stop=True)
                    ST.inc(ins)
                blk.tensor(t_rk)
                t_rk_ms = ST.n

                def v_rk(v, s=s, t=t, bankr=bankr, t_rk_ms=t_rk_ms):
                    st_i = 2 * t + s
                    if t == 0 and s == 0:
                        v.wait_ge(SGM.h, 1)
                    v.wait_ge(ST.h, t_rk_ms)
                    ins = v.tensor_tensor(SCR8[:, :], bankr[:, 8:16],
                                          MHOT[:, 8 * st_i:8 * (st_i + 1)],
                                          ALU.mult)
                    v.wait_ge(SV.h, SV.inc(ins))
                    v.reduce_sum(RANK[:, st_i:st_i + 1], SCR8[:, :], axis=AX.X)
                    ins = v.tensor_tensor(CNT[0:1, :], CNT[0:1, :],
                                          bankr[0:1, 16:24], ALU.add)
                    v.wait_ge(SV.h, SV.inc(ins))
                    ins = v.tensor_copy(SCR8[0:1, 0:8], CNT[0:1, :])
                    SV.inc(ins)
                blk.vector(v_rk)
                v_cnt = SV.n
                bk[bnm].consumed(SV)

            def v_z2(v, t=t, we_ms=we_ms_t):
                if t > 0:
                    v.wait_ge(SZ.h, 16 * t)
                for k in range(4):
                    v.wait_ge(ST.h, we_ms[k])
                    ins = v.tensor_scalar(Z2B[:, 512 * k:512 * (k + 1)],
                                          zbanks[k][:, :],
                                          GATE[:, 2 * t + k // 2:
                                               2 * t + k // 2 + 1],
                                          None, ALU.mult)
                SV.inc(ins)
            blk.vector(v_z2)
            v_z2_ms = SV.n
            for bnm2 in zbknames:
                bk[bnm2].consumed(SV)

            def g_z2(g, t=t, v_z2_ms=v_z2_ms):
                g.wait_ge(SV.h, v_z2_ms)
                SZ.inc(g.dma_start(out=z2d.ap()[t], in_=Z2B[:, :]), 16)
            blk.gpsimd(g_z2)

        def g_cc(g, v_cnt=v_cnt):
            if lvl < 12:
                return
            g.wait_ge(SV.h, v_cnt)
            SD.inc(g.dma_start(out=cc_in.ap()[:, :], in_=CNT[:, :]), 16)
            g.wait_ge(SD.h, SD.n)
            SC.inc(g.collective_compute(
                "AllGather", ALU.bypass,
                replica_groups=[list(range(NCORES))],
                ins=[cc_in.ap()[:, :]],
                outs=[cc_out.ap()[:, :]]))
            g.wait_ge(SC.h, SC.n)
            SD.inc(g.dma_start(out=SG[:, :], in_=cc_out.ap()[:, :]), 16)
        blk.gpsimd(g_cc)
        sd_g = SD.n

    blk_main.__exit__(None, None, None)

    for b in bk.values():
        b.pend = []

    with nc.Block() as blk:
        if lvl < 12:
            pass
        def t_off(te, sd_g=sd_g):
            if lvl < 12:
                return
            te.wait_ge(SD.h, sd_g)
            ins = te.matmul(PB2[0:1, 0:8], PREVC[:, :], SG[:, :],
                            start=True, stop=True)
            ST.inc(ins)
        blk.tensor(t_off)
        t_off_ms = ST.n

        def v_off(v, t_off_ms=t_off_ms):
            if lvl < 12:
                return
            v.wait_ge(ST.h, t_off_ms)
            ins = v.tensor_copy(SOFF[:, :], PB2[0:1, 0:8])
            v.wait_ge(SV.h, SV.inc(ins))
            ins = v.tensor_copy(SCR8[0:1, 0:8], SOFF[:, :])
            SV.inc(ins)
        blk.vector(v_off)
        v_off_ms = SV.n

        def t_ob(te, v_off_ms=v_off_ms):
            if lvl < 12:
                return
            te.wait_ge(SV.h, v_off_ms)
            ins = te.matmul(PB3[:, 0:8], ORF[0:1, :], SOFF[0:1, :],
                            start=True, stop=True)
            ST.inc(ins)
        blk.tensor(t_ob)
        t_ob_ms = ST.n

        def v_ob(v, t_ob_ms=t_ob_ms):
            if lvl < 12:
                return
            v.wait_ge(ST.h, t_ob_ms)
            ins = v.tensor_copy(OFFB[:, :], PB3[:, 0:8])
            SV.inc(ins)
        blk.vector(v_ob)

    z2_stores = SZ.n
    so0 = SO.n
    vfin_hist = {}
    with nc.Block() as blk:
        if lvl >= 13:
            def v_kga(v):
                ins = v.tensor_tensor(
                    XT[:, 0:256].rearrange("p (st e) -> p st e", e=8),
                    MHOT[:, 0:256].rearrange("p (st e) -> p st e", e=8),
                    bass.AP(OFFB, 0, [[8, 128], [0, 32], [1, 8]]),
                    ALU.mult)
                v.wait_ge(SV.h, SV.inc(ins))
                ins = v.reduce_sum(
                    M2C[:, 0:32].rearrange("p (st u) -> p st u", u=1),
                    XT[:, 0:256].rearrange("p (st e) -> p st e", e=8),
                    axis=AX.X)
                v.wait_ge(SV.h, SV.inc(ins))
                ins = v.tensor_tensor(M2C[:, 0:32], M2C[:, 0:32],
                                      RANK[:, 0:32], ALU.add)
                v.wait_ge(SV.h, SV.inc(ins))
                ins = v.tensor_scalar(M2C[:, 0:32], M2C[:, 0:32], CAPF,
                                      None, ALU.is_lt)
                v.wait_ge(SV.h, SV.inc(ins))
            blk.vector(v_kga)

        for t in (range(nt) if lvl >= 13 else []):
            zbuf = QF if t % 2 == 0 else KF

            def d_z2l(sy, t=t, zbuf=zbuf, vprev=vfin_hist.get(t - 2, 0)):
                if vprev:
                    sy.wait_ge(SV.h, vprev)
                sy.wait_ge(SZ.h, z2_stores)
                SZ.inc(sy.dma_start(out=zbuf[:, :], in_=z2d.ap()[t]), 16)
            blk.sync(d_z2l)
            z2l_ms = SZ.n

            for s in (0, 1):
                st_i = 2 * t + s
                ob_i = st_i % 3
                ob = (OUTS[:, 0:1024] if ob_i == 0 else
                      XT[:, 1024 * (ob_i - 1):1024 * ob_i])

                def v_fin(v, t=t, s=s, st_i=st_i, ob=ob, zbuf=zbuf,
                          z2l_ms=z2l_ms):
                    if st_i >= 3:
                        v.wait_ge(SO.h, so0 + 16 * (st_i - 2))
                    if s == 0:
                        v.wait_ge(SZ.h, z2l_ms)
                    ins = v.scalar_tensor_tensor(
                        ob, zbuf[:, 1024 * s:1024 * (s + 1)],
                        M2C[:, st_i:st_i + 1],
                        YB[:, 1024 * st_i:1024 * (st_i + 1)],
                        ALU.mult, ALU.add)
                    SV.inc(ins)
                blk.vector(v_fin)
                v_fin_ms = SV.n

                def d_out(sy, st_i=st_i, ob=ob, v_fin_ms=v_fin_ms):
                    sy.wait_ge(SV.h, v_fin_ms)
                    SO.inc(sy.dma_start(
                        out=y_out[128 * st_i:128 * (st_i + 1), :],
                        in_=ob), 16)
                blk.sync(d_out)
            vfin_hist[t] = v_fin_ms

    with nc.Block() as blk:
        @blk.sync
        def _(sy):
            if SO.n:
                sy.wait_ge(SO.h, SO.n)
            sy.wait_ge(SD.h, SD.n)
            sy.wait_ge(ST.h, ST.n)
            sy.wait_ge(SV.h, SV.n)
            sy.wait_ge(SS.h, SS.n)

    if dump:
        with nc.Block() as blk:
            def g_dump(g):
                r0 = 0
                for name in dump:
                    if ":" in name:
                        name, cstr = name.split(":")
                        hnd = dumpd[name]
                        cols = int(cstr)
                    else:
                        hnd = dumpd[name]
                        cols = hnd.shape[1] if hasattr(hnd, 'shape') else None
                    nb = max(1, (cols + 1023) // 1024)
                    for b in range(nb):
                        c0 = 1024 * b
                        c1 = min(cols, c0 + 1024)
                        p = hnd.shape[0]
                        SO.inc(g.dma_start(
                            out=y_out[r0:r0 + p, 0:(c1 - c0)],
                            in_=hnd[0:p, c0:c1]), 16)
                        r0 += 128
                g.wait_ge(SO.h, SO.n)
            blk.gpsimd(g_dump)

    return nc


def _consts():
    idt = np.eye(128, dtype=BFNP)
    tri04 = np.triu(np.ones((128, 128), np.float32), k=0).astype(BFNP)
    tri1f = np.triu(np.ones((128, 128), np.float32), k=1).astype(BFNP)
    ocb = np.ones((128, 1), BFNP)
    orb = np.ones((65, 128), BFNP)
    orf = np.ones((1, 128), np.float32)
    cvv = np.zeros((128, 4), np.float32)
    cvv[:, 0] = 1e-5
    cvv[:, 1] = 1.0 / 1024.0
    cvv[:, 2] = -0.5
    return idt, tri04, tri1f, ocb, orb, orf, cvv


def kernel(L0_attn_x, L0_attn_Wq, L0_attn_Wk, L0_attn_Wv, L0_attn_Wo,
           L0_attn_bo, L0_ffn_Wg, L0_ffn_We, L0_ffn_be):
    x = np.ascontiguousarray(np.asarray(L0_attn_x, np.float32)).reshape(NCORES, TOK, D)
    wq = np.ascontiguousarray(np.asarray(L0_attn_Wq, np.float32))
    wk = np.ascontiguousarray(np.asarray(L0_attn_Wk, np.float32))
    wv = np.ascontiguousarray(np.asarray(L0_attn_Wv, np.float32))
    wo = np.ascontiguousarray(np.asarray(L0_attn_Wo, np.float32))
    bo = np.ascontiguousarray(np.asarray(L0_attn_bo, np.float32))
    wg = np.ascontiguousarray(np.asarray(L0_ffn_Wg, np.float32))
    we = np.ascontiguousarray(np.asarray(L0_ffn_We, np.float32))
    be = np.ascontiguousarray(np.asarray(L0_ffn_be, np.float32))
    idt, tri04, tri1f, ocb, orb, orf, cvv = _consts()

    nc = build_program()
    in_maps = []
    for c in range(NCORES):
        prevc = (np.arange(8) < c).astype(np.float32)[:, None].copy()
        in_maps.append({
            "x": x[c], "wq": wq, "wk": wk, "wv": wv, "wo": wo, "we": we,
            "wg": wg, "bo": bo, "be": be, "idt": idt, "triu04": tri04,
            "triu1f": tri1f, "ocb": ocb, "orb": orb, "orf": orf,
            "prevc": prevc, "cvv": cvv,
        })
    res = run_bass_kernel_spmd(nc, in_maps, core_ids=list(range(NCORES)))
    out = np.stack([np.asarray(res.results[c]["y"]) for c in range(NCORES)])
    return out.reshape(16, 2048, 1024)


# version 12
# speedup vs baseline: 1.1410x; 1.0436x over previous
import numpy as np
import ml_dtypes
from contextlib import ExitStack

import concourse.bass as bass
import concourse.mybir as mybir
from concourse.bass_utils import run_bass_kernel_spmd

F32 = mybir.dt.float32
BF16 = mybir.dt.bfloat16
AF = mybir.ActivationFunctionType
ALU = mybir.AluOpType
AX = mybir.AxisListType
BFNP = ml_dtypes.bfloat16

NCORES = 8
D = 1024
TOK = 4096
TT = 256
NT = TOK // TT
NST = 2 * NT
NH = 16
DH = 64
EPS_ATT = 1e-3
CAPF = 4096.0
BATCH_TILES = 8


class Ctr:
    def __init__(self, es, nc, name):
        self.h = es.enter_context(nc.semaphore(name))
        self.n = 0

    def inc(self, ins, k=1):
        ins.then_inc(self.h, k)
        self.n += k
        return self.n


class Bank:

    def __init__(self):
        self.pend = []

    def use(self, eng):
        for h, v in self.pend:
            eng.wait_ge(h, v)
        self.pend = []

    def consumed(self, ctr):
        self.pend.append((ctr.h, ctr.n))


def build_program(nt=NT, lvl=99, dump=None):
    nc = bass.Bass("TRN2", target_bir_lowering=False)

    x_in = nc.dram_tensor("x", [TOK, D], F32, kind="ExternalInput").ap()
    wq_in = nc.dram_tensor("wq", [D, D], F32, kind="ExternalInput").ap()
    wk_in = nc.dram_tensor("wk", [D, D], F32, kind="ExternalInput").ap()
    wv_in = nc.dram_tensor("wv", [D, D], F32, kind="ExternalInput").ap()
    wo_in = nc.dram_tensor("wo", [D, D], F32, kind="ExternalInput").ap()
    we_in = nc.dram_tensor("we", [D, D], F32, kind="ExternalInput").ap()
    wg_in = nc.dram_tensor("wg", [D, 8], F32, kind="ExternalInput").ap()
    bo_in = nc.dram_tensor("bo", [D], F32, kind="ExternalInput").ap()
    be_in = nc.dram_tensor("be", [D], F32, kind="ExternalInput").ap()
    idt_in = nc.dram_tensor("idt", [128, 128], BF16, kind="ExternalInput").ap()
    tri0_in = nc.dram_tensor("triu04", [128, 128], BF16, kind="ExternalInput").ap()
    tri1_in = nc.dram_tensor("triu1f", [128, 128], BF16, kind="ExternalInput").ap()
    ocb_in = nc.dram_tensor("ocb", [128, 1], BF16, kind="ExternalInput").ap()
    orb_in = nc.dram_tensor("orb", [65, 128], BF16, kind="ExternalInput").ap()
    orf_in = nc.dram_tensor("orf", [1, 128], F32, kind="ExternalInput").ap()
    prev_in = nc.dram_tensor("prevc", [8, 1], F32, kind="ExternalInput").ap()
    cvv_in = nc.dram_tensor("cvv", [128, 4], F32, kind="ExternalInput").ap()
    y_out = nc.dram_tensor("y", [TOK, D], F32, kind="ExternalOutput").ap()

    z2d = nc.dram_tensor("z2d", [NT, 128, 2048], BF16)
    cc_in = nc.dram_tensor("cc_in", [1, 8], F32)
    cc_out = nc.dram_tensor("cc_out", [8, 8], F32, addr_space="Shared")

    es = ExitStack()

    def sb(name, shape, dtype):
        return es.enter_context(nc.sbuf_tensor(name, shape, dtype))

    def ps(name, shape, dtype):
        return es.enter_context(nc.psum_tensor(name, shape, dtype))

    XT = sb("XT", [128, 2048], F32)
    OUTS = sb("OUTS", [128, 1024], F32)
    NRM_TM = sb("NRM_TM", [128, 2048], BF16)
    NRM_FM = sb("NRM_FM", [128, 2048], BF16)
    QF = sb("QF", [128, 2048], BF16)
    KF = sb("KF", [128, 2048], BF16)
    KT = sb("KT", [128, 2048], BF16)
    VT = sb("VT", [128, 2080], BF16)
    SM = sb("SM", [128, 2048], BF16)
    AT = sb("AT", [128, 2048], BF16)
    AFULL = sb("AFULL", [128, 2048], BF16)
    Z2B = sb("Z2B", [128, 2048], BF16)
    YB = sb("YB", [128, NST * 1024], BF16)
    WQb = sb("WQb", [128, 8192], BF16)
    WKb = sb("WKb", [128, 8192], BF16)
    WVb = sb("WVb", [128, 8192], BF16)
    WOb = sb("WOb", [128, 8192], BF16)
    WEb = sb("WEb", [128, 8192], BF16)
    WGb = sb("WGb", [128, 64], BF16)
    BOBE = sb("BOBE", [65, 1024], BF16)
    IDT = sb("IDT", [128, 128], BF16)
    TRI0 = sb("TRI0", [128, 128], BF16)
    TRI1F = sb("TRI1F", [128, 128], BF16)
    OCB = sb("OCB", [128, 1], BF16)
    ORB = sb("ORB", [65, 128], BF16)
    ORF = sb("ORF", [1, 128], F32)
    PREVC = sb("PREVC", [8, 1], F32)
    CVV = sb("CVV", [128, 4], F32)
    SG = sb("SGt", [8, 8], F32)
    SOFF = sb("SOFF", [1, 8], F32)
    OFFB = sb("OFFB", [128, 8], F32)
    STE = sb("STE", [128, 8 * 65], F32)
    STO = sb("STO", [128, 8 * 65], F32)
    STEB = sb("STEB", [128, 8 * 65], BF16)
    STOB = sb("STOB", [128, 8 * 65], BF16)
    CNT = sb("CNT", [1, 8], F32)
    MHOT = sb("MHOT", [128, NST * 8], BF16)
    GATE = sb("GATE", [128, NST], F32)
    RANK = sb("RANK", [128, NST], F32)
    M2C = sb("M2C", [128, NST], F32)
    R2C = sb("R2C", [128, NST], F32)
    DINV = sb("DINV", [128, 16], F32)
    SCR8 = sb("SCR8", [128, 8], F32)
    SEXPC = sb("SEXPC", [128, 2], F32)
    NEGM = sb("NEGM", [128, 2], F32)
    SUMT = sb("SUMT", [128, 2], F32)
    SQT = sb("SQT", [128, 2], F32)
    MT = sb("MT", [128, 2], F32)
    T1C = sb("T1C", [128, 2], F32)
    T2C = sb("T2C", [128, 2], F32)
    RSTD = sb("RSTD", [128, 2], F32)
    KGC = sb("KGC", [128, 2], F32)
    OFFC = sb("OFFC", [128, 1], F32)

    dumpd = dict(XT=XT, NRM_TM=NRM_TM, NRM_FM=NRM_FM, QF=QF, KF=KF, KT=KT,
                 VT=VT, SM=SM, AT=AT, AFULL=AFULL, YB=YB, MHOT=MHOT,
                 GATE=GATE, RANK=RANK, M2C=M2C, R2C=R2C, STE=STE, STO=STO,
                 CNT=CNT, OFFB=OFFB, DINV=DINV, SGt=SG, KGC=KGC, WQb=WQb,
                 SUMT=SUMT, SQT=SQT, MT=MT, RSTD=RSTD, T2C=T2C, Z2B=Z2B, OUTS=OUTS)

    PB0 = ps("PB0", [128, 1024], BF16)
    PB1 = ps("PB1", [128, 1024], BF16)
    PB2 = ps("PB2", [128, 512], F32)
    PB3 = ps("PB3", [128, 512], F32)
    PB4 = ps("PB4", [128, 512], F32)
    PB5 = ps("PB5", [128, 512], F32)
    PB6 = ps("PB6", [128, 512], F32)
    PB7 = ps("PB7", [128, 512], F32)

    SD = Ctr(es, nc, "SD")
    SDC = Ctr(es, nc, "SDCc")
    SZ = Ctr(es, nc, "SZc")
    SO = Ctr(es, nc, "SOc")
    SC = Ctr(es, nc, "SCc")
    SGM = Ctr(es, nc, "SGMc")
    ST = Ctr(es, nc, "STc")
    SV = Ctr(es, nc, "SVc")
    SS = Ctr(es, nc, "SSc")

    bk = {name: Bank() for name in ["pb0", "pb1", "pb2", "pb3", "pb4", "pb5", "pb6", "pb7"]}
    zbanks = [PB4, PB5, PB6, PB7]
    zbknames = ["pb4", "pb5", "pb6", "pb7"]

    blk_main = nc.Block()
    blk_main.__enter__()
    blk = blk_main
    if True:
        @blk.sync
        def _(sy):
            for dst, src in [(IDT, idt_in), (TRI0, tri0_in), (TRI1F, tri1_in),
                             (OCB, ocb_in), (ORB, orb_in),
                             (ORF, orf_in), (PREVC, prev_in), (CVV, cvv_in)]:
                SDC.inc(sy.dma_start(out=dst[:, :], in_=src[:, :]), 16)

        @blk.scalar
        def _(se):
            from concourse.hw_specs import get_activation_tables
            set_id = list(get_activation_tables(nc.m.arch)).index(
                "natural_log_exp_and_others")
            se.add_instruction(mybir.InstLoadActFuncSet(
                name=nc.get_next_instruction_name(),
                act_func_set_id=set_id, ins=[], outs=[]))

        @blk.gpsimd
        def _(g):
            g.memset(STE[:, :], 0.0)
            g.memset(VT[:, :].rearrange("p (h u) -> p h u", u=65)[:, :, 64:65],
                     1.0)
            g.memset(STO[:, :], 0.0)
            g.memset(STEB[:, :], 0.0)
            g.memset(STOB[:, :], 0.0)
            ins = g.memset(CNT[:, :], 0.0)
            SGM.inc(ins)

    def emit_weight_loads():
        def g_w(g):
            for dst, src in [(WQb, wq_in), (WKb, wk_in), (WVb, wv_in),
                             (WOb, wo_in), (WEb, we_in)]:
                SD.inc(g.dma_start(
                    out=dst[:, :].rearrange("p (c o) -> p c o", c=8),
                    in_=src.rearrange("(c p) o -> p c o", p=128)), 16)
            SD.inc(g.dma_start(
                out=WGb[:, :].rearrange("p (c e) -> p c e", c=8),
                in_=wg_in.rearrange("(c p) e -> p c e", p=128)), 16)
            SD.inc(g.dma_start(out=BOBE[0:1, 0:1024],
                               in_=bo_in.rearrange("(u o) -> u o", u=1)), 16)
            SD.inc(g.dma_start(out=BOBE[64:65, 0:1024],
                               in_=be_in.rearrange("(u o) -> u o", u=1)), 16)
        blk.gpsimd(g_w)

    SETUP_SD = SD.n + 8 * 16
    SETUP_SDC = SDC.n

    v_y = 0
    v_st = 0
    v_ln2_prev = 0
    t_tr3_prev = 0
    proj_last_fill_ms = 0
    nrm_last_ms = 0
    v_cnt = 0
    first = {"t": True, "v": True, "s": True}

    v_y_hist = {}
    pre_vals = {}

    def emit_prelude(pt):
        XTW = XT
        vy_gate = v_y_hist.get(pt - 1, 0)

        def dseg(sy, pt=pt, vy_gate=vy_gate):
            if vy_gate:
                sy.wait_ge(SV.h, vy_gate)
            SDC.inc(sy.dma_start(
                out=XT[:, :].rearrange("p (s c) -> p s c", s=2),
                in_=x_in[TT * pt:TT * (pt + 1), :].rearrange(
                    "(s p) c -> p s c", p=128)), 16)
        blk.sync(dseg)
        dx = SDC.n

        def g_sq1(g, dx=dx, tprev=t_tr3_prev, XTW=XTW):
            g.wait_ge(SDC.h, dx)
            if tprev:
                g.wait_ge(ST.h, tprev)
            g.tensor_tensor(NRM_TM[:, 0:1024], XTW[:, 0:1024],
                            XTW[:, 0:1024], ALU.mult)
            ins = g.tensor_tensor(NRM_TM[:, 1024:2048], XTW[:, 1024:2048],
                                  XTW[:, 1024:2048], ALU.mult)
            SGM.inc(ins)
        blk.gpsimd(g_sq1)
        g_sq1m = SGM.n

        def v_st1(v, dx=dx, vprev=v_ln2_prev, g_sq1m=g_sq1m, XTW=XTW):
            v.wait_ge(SDC.h, dx)
            if vprev:
                v.wait_ge(SV.h, vprev)
            v.wait_ge(SGM.h, g_sq1m)
            v.reduce_sum(SUMT[:, 0:1], XTW[:, 0:1024], axis=AX.X)
            v.reduce_sum(SUMT[:, 1:2], XTW[:, 1024:2048], axis=AX.X)
            v.reduce_sum(SQT[:, 0:1], NRM_TM[:, 0:1024], axis=AX.X)
            ins = v.reduce_sum(SQT[:, 1:2], NRM_TM[:, 1024:2048], axis=AX.X)
            v.wait_ge(SV.h, SV.inc(ins))
            v.tensor_scalar(MT[:, 0:2], SUMT[:, 0:2],
                            1.0 / 1024.0, None, ALU.mult)
            ins = v.tensor_tensor(T1C[:, 0:2], SUMT[:, 0:2], SUMT[:, 0:2],
                                  ALU.mult)
            v.wait_ge(SV.h, SV.inc(ins))
            ins = v.scalar_tensor_tensor(T2C[:, 0:2], T1C[:, 0:2],
                                         -1.0 / 1024.0, SQT[:, 0:2],
                                         ALU.mult, ALU.add)
            SV.inc(ins)
        blk.vector(v_st1)
        v_ln1s = SV.n

        def s_rstd(se, v_ln1s=v_ln1s, fs=first["s"]):
            if fs:
                se.wait_ge(SDC.h, SETUP_SDC)
            se.wait_ge(SV.h, v_ln1s)
            ins = se.activation(RSTD[:, 0:2], T2C[:, 0:2], AF.Ln,
                                bias=CVV[:, 0:1], scale=CVV[:, 1:2])
            se.wait_ge(SS.h, SS.inc(ins))
            ins = se.activation(RSTD[:, 0:2], RSTD[:, 0:2],
                                AF.Exp, scale=CVV[:, 2:3])
            se.wait_ge(SS.h, SS.inc(ins))
            ins = se.copy(OUTS[:, 17:18], OUTS[:, 15:16])
            SS.inc(ins)
        blk.scalar(s_rstd)
        first["s"] = False
        s_rstd1 = SS.n

        def v_ap1(v, s_rstd1=s_rstd1, XTW=XTW):
            v.wait_ge(SS.h, s_rstd1)
            for s in (0, 1):
                ins = v.tensor_scalar(NRM_TM[:, 1024 * s:1024 * (s + 1)],
                                      XTW[:, 1024 * s:1024 * (s + 1)],
                                      MT[:, s:s + 1], RSTD[:, s:s + 1],
                                      ALU.subtract, ALU.mult)
            SV.inc(ins)
        blk.vector(v_ap1)
        pre_vals[pt] = SV.n

    if True:
        emit_prelude(0)
        emit_weight_loads()
        for t in range(nt):
            v_ln1 = pre_vals[t]
            if lvl < 2:
                if t + 1 < nt:
                    emit_prelude(t + 1)
                continue

            def t_tr1(te, v_ln1=v_ln1, ft=first["t"]):
                if ft:
                    te.wait_ge(SDC.h, SETUP_SDC)
                te.wait_ge(SV.h, v_ln1)
                for s in (0, 1):
                    pb = PB0 if s == 0 else PB1
                    bk["pb0" if s == 0 else "pb1"].use(te)
                    for ci in range(8):
                        ins = te.transpose(pb[:, 128 * ci:128 * (ci + 1)],
                                           NRM_TM[:, 1024 * s + 128 * ci:
                                                  1024 * s + 128 * (ci + 1)],
                                           IDT[:, :])
                    ST.inc(ins)
            blk.tensor(t_tr1)
            first["t"] = False
            t_tr1b = ST.n
            t_tr1a = ST.n - 1

            def v_fm0(v, t_tr1a=t_tr1a, pms=nrm_last_ms):
                v.wait_ge(ST.h, t_tr1a)
                if pms:
                    v.wait_ge(ST.h, pms)
                ins = v.tensor_copy(
                    NRM_FM[:, :].rearrange("p (c u) -> p c u", u=256)[:, :, 0:128],
                    PB0[:, :].rearrange("p (c u) -> p c u", u=128))
                SV.inc(ins)
            blk.vector(v_fm0)
            bk["pb0"].consumed(SV)
            v_fma = SV.n

            def s_fm1(v, t_tr1b=t_tr1b, pms=nrm_last_ms):
                v.wait_ge(ST.h, t_tr1b)
                if pms:
                    v.wait_ge(ST.h, pms)
                ins = v.tensor_copy(
                    NRM_FM[:, :].rearrange("p (c u) -> p c u", u=256)[:, :, 128:256],
                    PB1[:, :].rearrange("p (c u) -> p c u", u=128))
                SV.inc(ins)
            blk.vector(s_fm1)
            bk["pb1"].consumed(SV)
            s_fmb = SV.n
            if lvl < 3:
                continue

            fill_ms = []
            evac_ms = []
            for f in range(12):
                pidx = f // 4
                g = f % 4
                bank = PB2 if f % 2 == 0 else PB3
                bname = "pb2" if f % 2 == 0 else "pb3"
                wsrc = [WQb, WKb, WVb][pidx]

                def t_fill(te, f=f, g=g, bank=bank, bname=bname, wsrc=wsrc,
                           v_fma=v_fma, s_fmb=s_fmb, t=t):
                    if f == 0:
                        if t == 0:
                            te.wait_ge(SD.h, SETUP_SD)
                        te.wait_ge(SV.h, v_fma)
                        te.wait_ge(SV.h, s_fmb)
                    bk[bname].use(te)
                    for j in range(2):
                        oc = 2 * g + j
                        for ci in range(8):
                            ins = te.matmul(
                                bank[:, 256 * j:256 * (j + 1)],
                                wsrc[:, 1024 * ci + 128 * oc:
                                     1024 * ci + 128 * (oc + 1)],
                                NRM_FM[:, 256 * ci:256 * (ci + 1)],
                                start=(j == 0 and ci == 0),
                                stop=(j == 1 and ci == 7))
                    ST.inc(ins)
                blk.tensor(t_fill)
                fms = ST.n
                fill_ms.append(fms)

                dest = [QF, KF, SM][pidx]
                if pidx < 2:
                    def v_ev(v, fms=fms, bank=bank, dest=dest, g=g):
                        v.wait_ge(ST.h, fms)
                        ins = v.tensor_scalar(dest[:, 512 * g:512 * (g + 1)],
                                              bank[:, :], 0.0, EPS_ATT,
                                              ALU.max, ALU.add)
                        SV.inc(ins)
                    blk.vector(v_ev)
                    bk[bname].consumed(SV)
                    evac_ms.append(("v", SV.n))
                else:
                    def s_ev(v, fms=fms, bank=bank, dest=dest, g=g):
                        v.wait_ge(ST.h, fms)
                        ins = v.tensor_copy(dest[:, 512 * g:512 * (g + 1)],
                                            bank[:, :])
                        SV.inc(ins)
                    blk.vector(s_ev)
                    bk[bname].consumed(SV)
                    evac_ms.append(("v", SV.n))
            proj_last_fill_ms = fill_ms[-1]
            qf_sv = evac_ms[3][1]
            kf_sv = evac_ms[7][1]
            vf_ss = evac_ms[11][1]
            if lvl < 4:
                continue

            t2ms = []
            for which in range(4):
                src = KF if which < 2 else SM
                dst = KT if which < 2 else VT
                ss = which % 2
                pb = PB0 if ss == 0 else PB1
                bname = "pb0" if ss == 0 else "pb1"

                def t_t2(te, src=src, ss=ss, pb=pb, bname=bname, which=which,
                         kf_sv=kf_sv, vf_ss=vf_ss):
                    if which == 0:
                        te.wait_ge(SV.h, kf_sv)
                    if which == 2:
                        te.wait_ge(SV.h, vf_ss)
                    bk[bname].use(te)
                    for oc in range(8):
                        ins = te.transpose(pb[:, 128 * oc:128 * (oc + 1)],
                                           src[:, 256 * oc + 128 * ss:
                                               256 * oc + 128 * (ss + 1)],
                                           IDT[:, :])
                    ST.inc(ins)
                blk.tensor(t_t2)
                tms = ST.n
                t2ms.append(tms)

                if ss == 0:
                    def v_t2(v, tms=tms, pb=pb, dst=dst, which=which):
                        v.wait_ge(ST.h, tms)
                        if dst is VT:
                            out = VT[:, 0:1040].rearrange(
                                "p (h u) -> p h u", u=65)[:, :, 0:64]
                        else:
                            out = dst[:, 0:1024]
                        ins = v.tensor_copy(
                            out,
                            pb[:, :].rearrange("p (h u) -> p h u", u=64)
                            if dst is VT else pb[:, :])
                        SV.inc(ins)
                    blk.vector(v_t2)
                    bk[bname].consumed(SV)
                else:
                    def s_t2(v, tms=tms, pb=pb, dst=dst, which=which):
                        v.wait_ge(ST.h, tms)
                        if dst is VT:
                            out = VT[:, 1040:2080].rearrange(
                                "p (h u) -> p h u", u=65)[:, :, 0:64]
                        else:
                            out = dst[:, 1024:2048]
                        ins = v.tensor_copy(
                            out,
                            pb[:, :].rearrange("p (h u) -> p h u", u=64)
                            if dst is VT else pb[:, :])
                        SV.inc(ins)
                    blk.vector(s_t2)
                    bk[bname].consumed(SV)
                if which == 1:
                    kt_sv, kt_ss = SV.n, SV.n
                if which == 3:
                    vt_sv, vt_ss = SV.n, SV.n
            t2_last = t2ms[-1]
            if lvl < 5:
                continue

            if t == BATCH_TILES:
                def v_rst(v, tam=t_att_prev_batch):
                    v.wait_ge(ST.h, tam)
                    v.memset(STE[0:64, :], 0.0)
                    v.memset(STO[64:128, :], 0.0)
                    v.memset(STEB[0:64, :], 0.0)
                    ins = v.memset(STOB[64:128, :], 0.0)
                    SV.inc(ins)
                blk.vector(v_rst)
                v_st = SV.n

            t_att_ms = 0
            v_at = 0
            for ss in (0, 1):
                smv = []
                sc_groups = [(PB4, "pb4", [0, 2, 4, 6], False),
                             (PB5, "pb5", [8, 10, 12, 14], False),
                             (PB4, "pb4", [1, 3, 5, 7], True),
                             (PB5, "pb5", [9, 11, 13, 15], False)]
                for gi, (pb, bname, heads, sep) in enumerate(sc_groups):
                    def t_sc(te, gi=gi, pb=pb, bname=bname, heads=tuple(heads),
                             sep=sep, ss=ss, qf_sv=qf_sv, kf_sv=kf_sv):
                        if gi == 0 and ss == 0:
                            te.wait_ge(SV.h, qf_sv)
                            te.wait_ge(SV.h, kf_sv)
                        bk[bname].use(te)
                        if sep and t == 0 and ss == 0:
                            te.wait_ge(SGM.h, 1)
                        if sep:
                            te.matmul(pb[:, 0:1], IDT[:, :], OCB[:, :],
                                      start=False, stop=False,
                                      skip_group_check=True)
                        for i, h in enumerate(heads):
                            p0 = 64 * (h % 2)
                            c0 = 256 * (h // 2) + 128 * ss
                            ins = te.matmul(
                                pb[:, 128 * i:128 * (i + 1)],
                                KF[p0:p0 + 64, c0:c0 + 128],
                                QF[p0:p0 + 64, c0:c0 + 128],
                                start=(i == 0), stop=(i == 3))
                        ST.inc(ins)
                    blk.tensor(t_sc)
                    scms = ST.n
                    if lvl == 40:
                        bk[bname].pend = []
                        continue

                    def v_sm(v, scms=scms, pb=pb, gi=gi, ss=ss,
                             t2_last=t2_last):
                        v.wait_ge(ST.h, scms)
                        if gi == 0 and ss == 0:
                            v.wait_ge(ST.h, t2_last)
                        base = 1024 * (gi % 2)
                        half = 128 * (gi // 2)
                        ins = v.tensor_tensor(
                            SM[:, base:base + 1024].rearrange(
                                "p (g u) -> p g u", u=256)[:, :, half:half + 128],
                            pb[:, :].rearrange("p (g u) -> p g u", u=128),
                            bass.AP(TRI0, 0, [[128, 128], [0, 4], [1, 128]]),
                            ALU.mult)
                        SV.inc(ins)
                    blk.vector(v_sm)
                    bk[bname].consumed(SV)
                    smv.append(SV.n)

                if lvl in (40, 41):
                    continue
                def t_att(te, ss=ss, smv3=smv[3], v_st=v_st,
                          vt_sv=vt_sv, vt_ss=vt_ss):
                    te.wait_ge(SV.h, smv3)
                    if v_st > 0:
                        te.wait_ge(SV.h, v_st)
                    elif v_st < 0:
                        te.wait_ge(SGM.h, -v_st)
                    else:
                        te.wait_ge(SGM.h, 1)
                    if ss == 0:
                        te.wait_ge(SV.h, vt_sv)
                        te.wait_ge(SV.h, vt_ss)
                    bk["pb2"].use(te)
                    bk["pb6"].use(te)
                    bk["pb7"].use(te)
                    for h in range(NH):
                        if h < 7:
                            ob, oc0 = PB6, 65 * h
                        elif h < 14:
                            ob, oc0 = PB7, 65 * (h - 7)
                        else:
                            ob, oc0 = PB2, 65 * (h - 14)
                        g = h // 2
                        c0 = 256 * g + 128 * ss
                        qs = QF[:, c0:c0 + 128]
                        stb = STEB if h % 2 == 0 else STOB
                        sm = SM[:, 128 * h:128 * (h + 1)]
                        te.matmul(ob[:, oc0:oc0 + 65], sm,
                                  VT[:, 1040 * ss + 65 * h:1040 * ss + 65 * (h + 1)],
                                  start=(oc0 == 0), stop=False)
                        ins = te.matmul(ob[:, oc0:oc0 + 65], qs,
                                        stb[:, 65 * g:65 * g + 65],
                                        start=False,
                                        stop=(h in (6, 13, 15)))
                    ST.inc(ins)
                blk.tensor(t_att)
                t_att_ms = ST.n

                def v_dv(v, t_att_ms=t_att_ms, ss=ss):
                    v.wait_ge(ST.h, t_att_ms)
                    v.reciprocal(DINV[:, 0:7],
                                 PB6[:, 0:455].rearrange("p (h u) -> p h u",
                                                         u=65)[:, :, 64:65])
                    v.reciprocal(DINV[:, 7:14],
                                 PB7[:, 0:455].rearrange("p (h u) -> p h u",
                                                         u=65)[:, :, 64:65])
                    ins = v.reciprocal(DINV[:, 14:16],
                                       PB2[:, 0:130].rearrange(
                                           "p (h u) -> p h u",
                                           u=65)[:, 0:2, 64:65])
                    v.wait_ge(SV.h, SV.inc(ins))
                    for pb, h0, nh in ((PB6, 0, 7), (PB7, 7, 7), (PB2, 14, 2)):
                        ins = v.tensor_tensor(
                            AT[:, 1024 * ss + 64 * h0:
                               1024 * ss + 64 * (h0 + nh)].rearrange(
                                   "p (h e) -> p h e", e=64),
                            pb[:, 0:65 * nh].rearrange(
                                "p (h e) -> p h e", e=65)[:, :, 0:64],
                            bass.AP(DINV, h0, [[16, 128], [1, nh], [0, 64]]),
                            ALU.mult)
                    SV.inc(ins)
                blk.vector(v_dv)
                v_dinv = SV.n - 1
                v_at = SV.n
                bk["pb6"].consumed(SV)
                bk["pb7"].consumed(SV)
                bk["pb2"].consumed(SV)

                if lvl == 42:
                    continue
                def t_stu(te, ss=ss, v_dinv=v_dinv, v_at=v_at,
                          kt_sv=kt_sv, kt_ss=kt_ss):
                    te.wait_ge(SV.h, v_dinv)
                    te.wait_ge(SV.h, v_at)
                    if ss == 0:
                        te.wait_ge(SV.h, kt_sv)
                        te.wait_ge(SV.h, kt_ss)
                    bk["pb3"].use(te)
                    bk["pb6"].use(te)
                    for g in range(8):
                        ktp = KT[:, 1024 * ss + 128 * g:1024 * ss + 128 * (g + 1)]
                        ve = VT[:, 1040 * ss + 130 * g:1040 * ss + 130 * g + 64]
                        vo = VT[:, 1040 * ss + 130 * g + 65:
                                1040 * ss + 130 * g + 129]
                        te.matmul(PB3[:, 64 * g:64 * (g + 1)], ktp, ve,
                                  start=(g == 0), stop=False)
                        te.matmul(PB6[:, 64 * g:64 * (g + 1)], ktp, vo,
                                  start=(g == 0), stop=False)
                        ins = te.matmul(PB2[:, 256 + g:257 + g], ktp, OCB[:, :],
                                        start=False, stop=(g == 7))
                    ST.inc(ins)
                blk.tensor(t_stu)
                t_stu_ms = ST.n

                def v_sta(v, t_stu_ms=t_stu_ms, t=t, ss=ss):
                    if t == 0 and ss == 0:
                        v.wait_ge(SGM.h, 1)
                    v.wait_ge(ST.h, t_stu_ms)
                    ste = STE[:, :].rearrange("p (g u) -> p g u", u=65)
                    sto = STO[:, :].rearrange("p (g u) -> p g u", u=65)
                    v.tensor_tensor(ste[0:64, :, 0:64], ste[0:64, :, 0:64],
                                    PB3[0:64, :].rearrange("p (g u) -> p g u", u=64),
                                    ALU.add)
                    v.tensor_tensor(sto[64:128, :, 0:64], sto[64:128, :, 0:64],
                                    PB6[64:128, :].rearrange("p (g u) -> p g u", u=64),
                                    ALU.add)
                    v.tensor_tensor(ste[0:64, :, 64:65], ste[0:64, :, 64:65],
                                    PB2[0:64, 256:264].rearrange(
                                        "p (g u) -> p g u", u=1),
                                    ALU.add)
                    ins = v.tensor_tensor(sto[64:128, :, 64:65],
                                          sto[64:128, :, 64:65],
                                          PB2[64:128, 256:264].rearrange(
                                              "p (g u) -> p g u", u=1),
                                          ALU.add)
                    ins2 = ins
                    SV.inc(ins2)
                blk.vector(v_sta)
                v_sta_ms = SV.n

                def g_stc(g, v_sta_ms=v_sta_ms):
                    g.wait_ge(SV.h, v_sta_ms)
                    g.tensor_copy(STEB[0:64, :], STE[0:64, :])
                    ins = g.tensor_copy(STOB[64:128, :], STO[64:128, :])
                    g.wait_ge(SGM.h, SGM.inc(ins))
                    ins = g.tensor_copy(SCR8[0:1, 0:2], STE[0:1, 0:2])
                    SGM.inc(ins)
                blk.gpsimd(g_stc)
                v_st = -SGM.n
                bk["pb2"].consumed(SV)
                bk["pb3"].consumed(SV)
                bk["pb6"].consumed(SV)
            t_att_prev_batch = t_att_ms
            if lvl < 6:
                continue

            at3ms = []
            for ss in (0, 1):
                pb = PB0 if ss == 0 else PB1
                bname = "pb0" if ss == 0 else "pb1"

                def t_at3(te, ss=ss, pb=pb, bname=bname, v_at=v_at):
                    if ss == 0 and v_at:
                        te.wait_ge(SV.h, v_at)
                    bk[bname].use(te)
                    for ec in range(8):
                        ins = te.transpose(pb[:, 128 * ec:128 * (ec + 1)],
                                           AT[:, 1024 * ss + 128 * ec:
                                              1024 * ss + 128 * (ec + 1)],
                                           IDT[:, :])
                    ST.inc(ins)
                blk.tensor(t_at3)
                tms = ST.n
                at3ms.append(tms)

                if ss == 0:
                    def v_at3(v, tms=tms, pb=pb):
                        v.wait_ge(ST.h, tms)
                        ins = v.tensor_copy(
                            AFULL[:, :].rearrange("p (c u) -> p c u", u=256)[:, :, 0:128],
                            pb[:, :].rearrange("p (c u) -> p c u", u=128))
                        SV.inc(ins)
                    blk.vector(v_at3)
                    bk["pb0"].consumed(SV)
                    af_sv = SV.n
                else:
                    def s_at3(v, tms=tms, pb=pb):
                        v.wait_ge(ST.h, tms)
                        ins = v.tensor_copy(
                            AFULL[:, :].rearrange("p (c u) -> p c u", u=256)[:, :, 128:256],
                            pb[:, :].rearrange("p (c u) -> p c u", u=128))
                        SV.inc(ins)
                    blk.vector(s_at3)
                    bk["pb1"].consumed(SV)
                    af_ss = SV.n
            if lvl < 7:
                continue

            z_ms = []
            for k, (s, oh) in enumerate([(0, 0), (0, 1), (1, 0), (1, 1)]):
                bankz = zbanks[k]
                bnm = zbknames[k]

                def t_z(te, k=k, s=s, oh=oh, bankz=bankz, bnm=bnm,
                        af_sv=af_sv, af_ss=af_ss):
                    if k == 0:
                        te.wait_ge(SV.h, af_sv)
                        te.wait_ge(SV.h, af_ss)
                    bk[bnm].use(te)
                    for e in range(8):
                        te.matmul(bankz[:, :],
                                  AFULL[:, 256 * e + 128 * s:256 * e + 128 * (s + 1)],
                                  WOb[:, 1024 * e + 512 * oh:1024 * e + 512 * (oh + 1)],
                                  start=(e == 0), stop=False)
                    ins = te.matmul(bankz[:, :], ORB[0:1, :],
                                    BOBE[0:1, 512 * oh:512 * (oh + 1)],
                                    start=False, stop=True)
                    ST.inc(ins)
                blk.tensor(t_z)
                z_ms.append(ST.n)

            def v_yev(v, t=t, z_ms=tuple(z_ms)):
                for k, (s, oh) in enumerate([(0, 0), (0, 1), (1, 0), (1, 1)]):
                    v.wait_ge(ST.h, z_ms[k])
                    st_i = 2 * t + s
                    ins = v.scalar_tensor_tensor(
                        YB[:, 1024 * st_i + 512 * oh:1024 * st_i + 512 * (oh + 1)],
                        zbanks[k][:, :], 1.0,
                        XT[:, 1024 * s + 512 * oh:1024 * s + 512 * (oh + 1)],
                        ALU.mult, ALU.add)
                SV.inc(ins)
            blk.vector(v_yev)
            v_y = SV.n
            v_y_hist[t] = v_y
            for bnm in zbknames:
                bk[bnm].consumed(SV)
            if lvl < 8:
                if t + 1 < nt:
                    emit_prelude(t + 1)
                continue

            def g_sq2(g, t=t, v_y=v_y):
                g.wait_ge(SV.h, v_y)
                yb0 = YB[:, 2048 * t:2048 * t + 1024]
                yb1 = YB[:, 2048 * t + 1024:2048 * t + 2048]
                g.tensor_tensor(NRM_TM[:, 0:1024], yb0, yb0, ALU.mult)
                ins = g.tensor_tensor(NRM_TM[:, 1024:2048], yb1, yb1, ALU.mult)
                SGM.inc(ins)
            blk.gpsimd(g_sq2)
            g_sq2m = SGM.n

            def v_st2(v, t=t, v_y=v_y, g_sq2m=g_sq2m):
                v.wait_ge(SV.h, v_y)
                v.wait_ge(SGM.h, g_sq2m)
                yb0 = YB[:, 2048 * t:2048 * t + 1024]
                yb1 = YB[:, 2048 * t + 1024:2048 * t + 2048]
                v.reduce_sum(SUMT[:, 0:1], yb0, axis=AX.X)
                v.reduce_sum(SUMT[:, 1:2], yb1, axis=AX.X)
                v.reduce_sum(SQT[:, 0:1], NRM_TM[:, 0:1024], axis=AX.X)
                ins = v.reduce_sum(SQT[:, 1:2], NRM_TM[:, 1024:2048], axis=AX.X)
                v.wait_ge(SV.h, SV.inc(ins))
                v.tensor_scalar(M2C[:, 2 * t:2 * t + 2], SUMT[:, 0:2],
                                1.0 / 1024.0, None, ALU.mult)
                ins = v.tensor_tensor(T1C[:, 0:2], SUMT[:, 0:2], SUMT[:, 0:2],
                                      ALU.mult)
                v.wait_ge(SV.h, SV.inc(ins))
                ins = v.scalar_tensor_tensor(T2C[:, 0:2], T1C[:, 0:2],
                                             -1.0 / 1024.0, SQT[:, 0:2],
                                             ALU.mult, ALU.add)
                SV.inc(ins)
            blk.vector(v_st2)
            v_ln2s = SV.n

            def s_rstd2(se, t=t, v_ln2s=v_ln2s):
                se.wait_ge(SV.h, v_ln2s)
                ins = se.activation(R2C[:, 2 * t:2 * t + 2], T2C[:, 0:2], AF.Ln,
                                    bias=CVV[:, 0:1], scale=CVV[:, 1:2])
                se.wait_ge(SS.h, SS.inc(ins))
                ins = se.activation(R2C[:, 2 * t:2 * t + 2],
                                    R2C[:, 2 * t:2 * t + 2],
                                    AF.Exp, scale=CVV[:, 2:3])
                se.wait_ge(SS.h, SS.inc(ins))
                ins = se.copy(OUTS[:, 18:19], OUTS[:, 15:16])
                SS.inc(ins)
            blk.scalar(s_rstd2)
            s_rstd2m = SS.n

            def v_ap2(v, t=t, s_rstd2m=s_rstd2m, t_tr1b=t_tr1b):
                v.wait_ge(SS.h, s_rstd2m)
                v.wait_ge(ST.h, t_tr1b)
                for s in (0, 1):
                    st_i = 2 * t + s
                    ins = v.tensor_scalar(NRM_TM[:, 1024 * s:1024 * (s + 1)],
                                          YB[:, 1024 * st_i:1024 * (st_i + 1)],
                                          M2C[:, st_i:st_i + 1],
                                          R2C[:, st_i:st_i + 1],
                                          ALU.subtract, ALU.mult)
                SV.inc(ins)
            blk.vector(v_ap2)
            v_ln2 = SV.n
            v_ln2_prev = v_ln2
            if lvl < 9:
                if t + 1 < nt:
                    emit_prelude(t + 1)
                continue

            for s in (0, 1):
                pb = PB0 if s == 0 else PB1
                bname = "pb0" if s == 0 else "pb1"

                def t_tr3(te, s=s, pb=pb, bname=bname, v_ln2=v_ln2):
                    if s == 0:
                        te.wait_ge(SV.h, v_ln2)
                    bk[bname].use(te)
                    for ci in range(8):
                        ins = te.transpose(pb[:, 128 * ci:128 * (ci + 1)],
                                           NRM_TM[:, 1024 * s + 128 * ci:
                                                  1024 * s + 128 * (ci + 1)],
                                           IDT[:, :])
                    ST.inc(ins)
                blk.tensor(t_tr3)
                tms = ST.n

                if s == 0:
                    def v_tr3(v, tms=tms, pb=pb, plf=proj_last_fill_ms):
                        v.wait_ge(ST.h, tms)
                        v.wait_ge(ST.h, plf)
                        ins = v.tensor_copy(
                            NRM_FM[:, :].rearrange("p (c u) -> p c u", u=256)[:, :, 0:128],
                            pb[:, :].rearrange("p (c u) -> p c u", u=128))
                        SV.inc(ins)
                    blk.vector(v_tr3)
                    bk["pb0"].consumed(SV)
                    fm3_sv = SV.n
                else:
                    def s_tr3(v, tms=tms, pb=pb, plf=proj_last_fill_ms):
                        v.wait_ge(ST.h, tms)
                        v.wait_ge(ST.h, plf)
                        ins = v.tensor_copy(
                            NRM_FM[:, :].rearrange("p (c u) -> p c u", u=256)[:, :, 128:256],
                            pb[:, :].rearrange("p (c u) -> p c u", u=128))
                        SV.inc(ins)
                    blk.vector(s_tr3)
                    bk["pb1"].consumed(SV)
                    fm3_ss = SV.n
            t_tr3_prev = ST.n
            if lvl < 10:
                if t + 1 < nt:
                    emit_prelude(t + 1)
                continue

            rt_ms = {}
            for s in (0, 1):
                bankr = PB2 if s == 0 else PB3
                bnm = "pb2" if s == 0 else "pb3"

                def t_rt(te, s=s, bankr=bankr, bnm=bnm,
                         fm3_sv=fm3_sv, fm3_ss=fm3_ss):
                    if s == 0:
                        te.wait_ge(SV.h, fm3_sv)
                        te.wait_ge(SV.h, fm3_ss)
                    bk[bnm].use(te)
                    for ci in range(8):
                        ins = te.matmul(bankr[:, 0:8],
                                        NRM_FM[:, 256 * ci + 128 * s:
                                               256 * ci + 128 * (s + 1)],
                                        WGb[:, 8 * ci:8 * (ci + 1)],
                                        start=(ci == 0), stop=(ci == 7))
                    ST.inc(ins)
                blk.tensor(t_rt)
                rt_ms[s] = ST.n

            we_ms = []
            for k, (s2, oh) in enumerate([(0, 0), (0, 1), (1, 0), (1, 1)]):
                bankz = zbanks[k]
                bnm2 = zbknames[k]

                def t_we(te, k=k, s2=s2, oh=oh, bankz=bankz, bnm2=bnm2):
                    bk[bnm2].use(te)
                    for e in range(8):
                        te.matmul(bankz[:, :],
                                  NRM_FM[:, 256 * e + 128 * s2:
                                         256 * e + 128 * (s2 + 1)],
                                  WEb[:, 1024 * e + 512 * oh:
                                      1024 * e + 512 * (oh + 1)],
                                  start=(e == 0), stop=False)
                    ins = te.matmul(bankz[:, :], ORB[64:65, :],
                                    BOBE[64:65, 512 * oh:512 * (oh + 1)],
                                    start=False, stop=True)
                    ST.inc(ins)
                blk.tensor(t_we)
                we_ms.append(ST.n)
            nrm_last_ms = we_ms[-1]

            we_ms_t = tuple(we_ms)

            if t + 1 < nt:
                emit_prelude(t + 1)

            rte_v = {}
            rte_s = {}

            def v_mx(v, rtm=rt_ms[1]):
                v.wait_ge(ST.h, rtm)
                v.reduce_max(NEGM[:, 0:1], PB2[:, 0:8], axis=AX.X, negate=True)
                ins = v.reduce_max(NEGM[:, 1:2], PB3[:, 0:8], axis=AX.X,
                                   negate=True)
                v.wait_ge(SV.h, SV.inc(ins))
                ins = v.tensor_copy(NEGM[:, 0:2], NEGM[:, 0:2])
                SV.inc(ins)
            blk.vector(v_mx)
            v_negm = SV.n

            def s_ex(se, v_negm=v_negm):
                se.wait_ge(SV.h, v_negm)
                se.activation(OUTS[:, 0:8], PB2[:, 0:8], AF.Exp,
                              bias=NEGM[:, 0:1])
                ins = se.activation(OUTS[:, 8:16], PB3[:, 0:8], AF.Exp,
                                    bias=NEGM[:, 1:2])
                se.wait_ge(SS.h, SS.inc(ins))
                ins = se.copy(OUTS[:, 16:17], OUTS[:, 15:16])
                SS.inc(ins)
            blk.scalar(s_ex)
            rte_s[0] = rte_s[1] = SS.n

            def v_mh(v, t=t, sx=rte_s[1]):
                v.wait_ge(SS.h, sx)
                v.tensor_scalar(MHOT[:, 8 * (2 * t):8 * (2 * t + 1)],
                                PB2[:, 0:8], NEGM[:, 0:1], 0.0,
                                ALU.add, ALU.is_ge)
                v.tensor_scalar(MHOT[:, 8 * (2 * t + 1):8 * (2 * t + 2)],
                                PB3[:, 0:8], NEGM[:, 1:2], 0.0,
                                ALU.add, ALU.is_ge)
                v.reduce_sum(SEXPC[:, 0:1], OUTS[:, 0:8], axis=AX.X)
                ins = v.reduce_sum(SEXPC[:, 1:2], OUTS[:, 8:16], axis=AX.X)
                v.wait_ge(SV.h, SV.inc(ins))
                ins = v.reciprocal(GATE[:, 2 * t:2 * t + 2], SEXPC[:, 0:2])
                SV.inc(ins)
            blk.vector(v_mh)
            rte_v[0] = rte_v[1] = SV.n
            if lvl < 11:
                continue

            for s in (0, 1):
                bankr = PB2 if s == 0 else PB3
                bnm = "pb2" if s == 0 else "pb3"
                bk[bnm].consumed(SV)
                bk[bnm].pend.append((SS.h, rte_s[s]))

                def t_rk(te, s=s, t=t, bankr=bankr, bnm=bnm,
                         vr=rte_v[s], sxm=rte_s[s], v_cnt=v_cnt):
                    st_i = 2 * t + s
                    te.wait_ge(SV.h, vr)
                    te.wait_ge(SS.h, sxm)
                    if v_cnt:
                        te.wait_ge(SV.h, v_cnt)
                    else:
                        te.wait_ge(SGM.h, 1)
                    te.matmul(bankr[:, 8:16], TRI1F[:, :],
                              MHOT[:, 8 * st_i:8 * (st_i + 1)],
                              start=False, stop=False)
                    te.matmul(bankr[:, 8:16], ORF[0:1, :], CNT[0:1, :],
                              start=False, stop=False)
                    ins = te.matmul(bankr[0:1, 16:24], OCB[:, :],
                                    MHOT[:, 8 * st_i:8 * (st_i + 1)],
                                    start=False, stop=True)
                    ST.inc(ins)
                blk.tensor(t_rk)
                t_rk_ms = ST.n

                def v_rk(v, s=s, t=t, bankr=bankr, t_rk_ms=t_rk_ms):
                    st_i = 2 * t + s
                    if t == 0 and s == 0:
                        v.wait_ge(SGM.h, 1)
                    v.wait_ge(ST.h, t_rk_ms)
                    ins = v.tensor_tensor(SCR8[:, :], bankr[:, 8:16],
                                          MHOT[:, 8 * st_i:8 * (st_i + 1)],
                                          ALU.mult)
                    v.wait_ge(SV.h, SV.inc(ins))
                    v.reduce_sum(RANK[:, st_i:st_i + 1], SCR8[:, :], axis=AX.X)
                    ins = v.tensor_tensor(CNT[0:1, :], CNT[0:1, :],
                                          bankr[0:1, 16:24], ALU.add)
                    v.wait_ge(SV.h, SV.inc(ins))
                    ins = v.tensor_copy(SCR8[0:1, 0:8], CNT[0:1, :])
                    SV.inc(ins)
                blk.vector(v_rk)
                v_cnt = SV.n
                bk[bnm].consumed(SV)

            def v_z2(v, t=t, we_ms=we_ms_t):
                if t > 0:
                    v.wait_ge(SZ.h, 16 * t)
                for k in range(4):
                    v.wait_ge(ST.h, we_ms[k])
                    ins = v.tensor_scalar(Z2B[:, 512 * k:512 * (k + 1)],
                                          zbanks[k][:, :],
                                          GATE[:, 2 * t + k // 2:
                                               2 * t + k // 2 + 1],
                                          None, ALU.mult)
                SV.inc(ins)
            blk.vector(v_z2)
            v_z2_ms = SV.n
            for bnm2 in zbknames:
                bk[bnm2].consumed(SV)

            def g_z2(g, t=t, v_z2_ms=v_z2_ms):
                g.wait_ge(SV.h, v_z2_ms)
                SZ.inc(g.dma_start(out=z2d.ap()[t], in_=Z2B[:, :]), 16)
            blk.gpsimd(g_z2)

        def g_cc(g, v_cnt=v_cnt):
            if lvl < 12:
                return
            g.wait_ge(SV.h, v_cnt)
            SD.inc(g.dma_start(out=cc_in.ap()[:, :], in_=CNT[:, :]), 16)
            g.wait_ge(SD.h, SD.n)
            SC.inc(g.collective_compute(
                "AllGather", ALU.bypass,
                replica_groups=[list(range(NCORES))],
                ins=[cc_in.ap()[:, :]],
                outs=[cc_out.ap()[:, :]]))
            g.wait_ge(SC.h, SC.n)
            SD.inc(g.dma_start(out=SG[:, :], in_=cc_out.ap()[:, :]), 16)
        blk.gpsimd(g_cc)
        sd_g = SD.n

    blk_main.__exit__(None, None, None)

    for b in bk.values():
        b.pend = []

    z2_stores = SZ.n
    with nc.Block() as blk:
        def g_pre(g):
            if lvl < 13:
                return
            g.wait_ge(SZ.h, z2_stores)
            SZ.inc(g.dma_start(out=QF[:, :], in_=z2d.ap()[0]), 16)
            SZ.inc(g.dma_start(out=KF[:, :], in_=z2d.ap()[1]), 16)
        blk.gpsimd(g_pre)
        if lvl < 12:
            pass
        def t_off(te, sd_g=sd_g):
            if lvl < 12:
                return
            te.wait_ge(SD.h, sd_g)
            ins = te.matmul(PB2[0:1, 0:8], PREVC[:, :], SG[:, :],
                            start=True, stop=True)
            ST.inc(ins)
        blk.tensor(t_off)
        t_off_ms = ST.n

        def v_off(v, t_off_ms=t_off_ms):
            if lvl < 12:
                return
            v.wait_ge(ST.h, t_off_ms)
            ins = v.tensor_copy(SOFF[:, :], PB2[0:1, 0:8])
            v.wait_ge(SV.h, SV.inc(ins))
            ins = v.tensor_copy(SCR8[0:1, 0:8], SOFF[:, :])
            SV.inc(ins)
        blk.vector(v_off)
        v_off_ms = SV.n

        def t_ob(te, v_off_ms=v_off_ms):
            if lvl < 12:
                return
            te.wait_ge(SV.h, v_off_ms)
            ins = te.matmul(PB3[:, 0:8], ORF[0:1, :], SOFF[0:1, :],
                            start=True, stop=True)
            ST.inc(ins)
        blk.tensor(t_ob)
        t_ob_ms = ST.n

        def v_ob(v, t_ob_ms=t_ob_ms):
            if lvl < 12:
                return
            v.wait_ge(ST.h, t_ob_ms)
            ins = v.tensor_copy(OFFB[:, :], PB3[:, 0:8])
            SV.inc(ins)
        blk.vector(v_ob)

    so0 = SO.n
    z2l_pre = {0: z2_stores + 16, 1: z2_stores + 32}
    vfin_hist = {}
    with nc.Block() as blk:
        if lvl >= 13:
            def v_kga(v):
                ins = v.tensor_tensor(
                    XT[:, 0:256].rearrange("p (st e) -> p st e", e=8),
                    MHOT[:, 0:256].rearrange("p (st e) -> p st e", e=8),
                    bass.AP(OFFB, 0, [[8, 128], [0, 32], [1, 8]]),
                    ALU.mult)
                v.wait_ge(SV.h, SV.inc(ins))
                ins = v.reduce_sum(
                    M2C[:, 0:32].rearrange("p (st u) -> p st u", u=1),
                    XT[:, 0:256].rearrange("p (st e) -> p st e", e=8),
                    axis=AX.X)
                v.wait_ge(SV.h, SV.inc(ins))
                ins = v.tensor_tensor(M2C[:, 0:32], M2C[:, 0:32],
                                      RANK[:, 0:32], ALU.add)
                v.wait_ge(SV.h, SV.inc(ins))
                ins = v.tensor_scalar(M2C[:, 0:32], M2C[:, 0:32], CAPF,
                                      None, ALU.is_lt)
                v.wait_ge(SV.h, SV.inc(ins))
            blk.vector(v_kga)

        for t in (range(nt) if lvl >= 13 else []):
            zbuf = QF if t % 2 == 0 else KF

            if t < 2:
                z2l_ms = z2l_pre[t]
            else:
                def d_z2l(g, t=t, zbuf=zbuf, vprev=vfin_hist[t - 2]):
                    g.wait_ge(SV.h, vprev)
                    SZ.inc(g.dma_start(out=zbuf[:, :], in_=z2d.ap()[t]), 16)
                blk.gpsimd(d_z2l)
                z2l_ms = SZ.n

            for s in (0, 1):
                st_i = 2 * t + s
                ob_i = st_i % 3
                ob = (OUTS[:, 0:1024] if ob_i == 0 else
                      XT[:, 1024 * (ob_i - 1):1024 * ob_i])

                def v_fin(v, t=t, s=s, st_i=st_i, ob=ob, zbuf=zbuf,
                          z2l_ms=z2l_ms):
                    if st_i >= 3:
                        v.wait_ge(SO.h, so0 + 16 * (st_i - 2))
                    if s == 0:
                        v.wait_ge(SZ.h, z2l_ms)
                    ins = v.scalar_tensor_tensor(
                        ob, zbuf[:, 1024 * s:1024 * (s + 1)],
                        M2C[:, st_i:st_i + 1],
                        YB[:, 1024 * st_i:1024 * (st_i + 1)],
                        ALU.mult, ALU.add)
                    SV.inc(ins)
                blk.vector(v_fin)
                v_fin_ms = SV.n

                def d_out(sy, st_i=st_i, ob=ob, v_fin_ms=v_fin_ms):
                    sy.wait_ge(SV.h, v_fin_ms)
                    SO.inc(sy.dma_start(
                        out=y_out[128 * st_i:128 * (st_i + 1), :],
                        in_=ob), 16)
                blk.sync(d_out)
            vfin_hist[t] = v_fin_ms

    with nc.Block() as blk:
        @blk.sync
        def _(sy):
            if SO.n:
                sy.wait_ge(SO.h, SO.n)
            sy.wait_ge(SD.h, SD.n)
            sy.wait_ge(ST.h, ST.n)
            sy.wait_ge(SV.h, SV.n)
            sy.wait_ge(SS.h, SS.n)

    if dump:
        with nc.Block() as blk:
            def g_dump(g):
                r0 = 0
                for name in dump:
                    if ":" in name:
                        name, cstr = name.split(":")
                        hnd = dumpd[name]
                        cols = int(cstr)
                    else:
                        hnd = dumpd[name]
                        cols = hnd.shape[1] if hasattr(hnd, 'shape') else None
                    nb = max(1, (cols + 1023) // 1024)
                    for b in range(nb):
                        c0 = 1024 * b
                        c1 = min(cols, c0 + 1024)
                        p = hnd.shape[0]
                        SO.inc(g.dma_start(
                            out=y_out[r0:r0 + p, 0:(c1 - c0)],
                            in_=hnd[0:p, c0:c1]), 16)
                        r0 += 128
                g.wait_ge(SO.h, SO.n)
            blk.gpsimd(g_dump)

    return nc


def _consts():
    idt = np.eye(128, dtype=BFNP)
    tri04 = np.triu(np.ones((128, 128), np.float32), k=0).astype(BFNP)
    tri1f = np.triu(np.ones((128, 128), np.float32), k=1).astype(BFNP)
    ocb = np.ones((128, 1), BFNP)
    orb = np.ones((65, 128), BFNP)
    orf = np.ones((1, 128), np.float32)
    cvv = np.zeros((128, 4), np.float32)
    cvv[:, 0] = 1e-5
    cvv[:, 1] = 1.0 / 1024.0
    cvv[:, 2] = -0.5
    return idt, tri04, tri1f, ocb, orb, orf, cvv


def kernel(L0_attn_x, L0_attn_Wq, L0_attn_Wk, L0_attn_Wv, L0_attn_Wo,
           L0_attn_bo, L0_ffn_Wg, L0_ffn_We, L0_ffn_be):
    x = np.ascontiguousarray(np.asarray(L0_attn_x, np.float32)).reshape(NCORES, TOK, D)
    wq = np.ascontiguousarray(np.asarray(L0_attn_Wq, np.float32))
    wk = np.ascontiguousarray(np.asarray(L0_attn_Wk, np.float32))
    wv = np.ascontiguousarray(np.asarray(L0_attn_Wv, np.float32))
    wo = np.ascontiguousarray(np.asarray(L0_attn_Wo, np.float32))
    bo = np.ascontiguousarray(np.asarray(L0_attn_bo, np.float32))
    wg = np.ascontiguousarray(np.asarray(L0_ffn_Wg, np.float32))
    we = np.ascontiguousarray(np.asarray(L0_ffn_We, np.float32))
    be = np.ascontiguousarray(np.asarray(L0_ffn_be, np.float32))
    idt, tri04, tri1f, ocb, orb, orf, cvv = _consts()

    nc = build_program()
    in_maps = []
    for c in range(NCORES):
        prevc = (np.arange(8) < c).astype(np.float32)[:, None].copy()
        in_maps.append({
            "x": x[c], "wq": wq, "wk": wk, "wv": wv, "wo": wo, "we": we,
            "wg": wg, "bo": bo, "be": be, "idt": idt, "triu04": tri04,
            "triu1f": tri1f, "ocb": ocb, "orb": orb, "orf": orf,
            "prevc": prevc, "cvv": cvv,
        })
    res = run_bass_kernel_spmd(nc, in_maps, core_ids=list(range(NCORES)))
    out = np.stack([np.asarray(res.results[c]["y"]) for c in range(NCORES)])
    return out.reshape(16, 2048, 1024)
